# version 12
# baseline (speedup 1.0000x reference)
import numpy as np
import ml_dtypes
import concourse.bass as bass
import concourse.mybir as mybir
from concourse.bass_utils import run_bass_kernel_spmd

F32 = mybir.dt.float32
BF16 = mybir.dt.bfloat16
AF = mybir.ActivationFunctionType
ALU = mybir.AluOpType
AX = mybir.AxisListType.X
bfdt = ml_dtypes.bfloat16

B_, L_, D_ = 4, 2048, 4096
NTOK = 1024
NTT = 8
NDT = 32
PROJ = 10304
NFT = 81
NXBC = 48
NCH = 8
H_, P_, N_, G_ = 64, 64, 128, 8
EPS = 1e-5
GROUPS = [[0, 1], [2, 3], [4, 5], [6, 7]]


class Buf:
    __slots__ = ("name", "last_w", "readers")

    def __init__(self, name):
        self.name = name
        self.last_w = None
        self.readers = []


class Emit:
    ENGINES = ("tensor", "vector", "scalar", "gpsimd", "sync")

    def __init__(self, nc):
        self.nc = nc
        self.ops = {e: [] for e in self.ENGINES}
        self.cnt = {}
        self.syncd = {e: {} for e in self.ENGINES}
        self.sems = {}
        keys = ["t", "v", "s", "p"]
        keys += [f"qd{i}" for i in range(8)] + [f"pd{i}" for i in range(8)]
        for k in keys:
            self.sems[k] = nc.alloc_semaphore("sem_" + k)
            self.cnt[k] = 0
        self.dma_i = {"sync": 0, "gpsimd": 0}
        self.eng_sem = {"tensor": "t", "vector": "v", "scalar": "s", "gpsimd": "p"}
        self.pending = {e: [] for e in self.ENGINES}

    def _waits(self, eng, bufs_r, bufs_w):
        need = {}
        for b in bufs_r:
            if b.last_w is not None:
                k, v = b.last_w
                if need.get(k, 0) < v:
                    need[k] = v
        for b in bufs_w:
            if b.last_w is not None:
                k, v = b.last_w
                if need.get(k, 0) < v:
                    need[k] = v
            for k, v in b.readers:
                if need.get(k, 0) < v:
                    need[k] = v
        out = []
        sy = self.syncd[eng]
        for k, v in need.items():
            if sy.get(k, 0) < v:
                out.append((k, v))
                sy[k] = v
        return out

    def _rec(self, eng, fn, sk, n_inc, reads, writes, war):
        waits = self._waits(eng, reads, list(writes) + list(war))
        self.cnt[sk] += n_inc
        newv = self.cnt[sk]
        self.ops[eng].append((waits, fn, sk, n_inc))
        done = (sk, newv)
        if n_inc:
            for b in self.pending[eng]:
                b.readers.append(done)
            self.pending[eng] = []
            for b in writes:
                b.last_w = done
                b.readers = []
            for b in reads:
                b.readers.append(done)
        else:
            self.pending[eng].extend(reads)
        return done

    def op(self, eng, fn, reads=(), writes=(), war=(), n_inc=1):
        return self._rec(eng, fn, self.eng_sem[eng], n_inc, reads, writes, war)

    def mm(self, fn, reads=(), writes=(), war=(), last=True, tick=False):
        return self._rec("tensor", fn, "t", 1 if (last or tick) else 0,
                         reads, writes, war)

    def dma(self, queue, out_ap, in_ap, reads=(), writes=(), war=()):
        i = self.dma_i[queue]
        self.dma_i[queue] = i + 1
        sk = ("qd" if queue == "sync" else "pd") + str(i % 8)

        def fn(e, _o=out_ap, _i=in_ap):
            return e.dma_start(out=_o, in_=_i)

        return self._rec(queue, fn, sk, 16, reads, writes, war)

    def coll(self, ins, outs, reads=(), writes=()):
        def fn(e):
            return e.collective_compute("AllGather", ALU.bypass,
                                        replica_groups=GROUPS, ins=ins, outs=outs)

        return self._rec("gpsimd", fn, "p", 1, reads, writes, ())

    def wait_all(self, eng, bufs):
        waits = self._waits(eng, bufs, ())
        if waits:
            self.ops[eng].append((waits, None, None, 0))

    def finalize(self):
        sems = self.sems
        with self.nc.Block() as block:
            for eng in self.ENGINES:
                lst = self.ops[eng]

                def run(e, _lst=lst):
                    for waits, fn, sk, n_inc in _lst:
                        for k, v in waits:
                            e.wait_ge(sems[k], v)
                        if fn is not None:
                            ins = fn(e)
                            if n_inc:
                                ins.then_inc(sems[sk], n_inc)
                getattr(block, eng)(run)


def build(debug=False):
    nc = bass.Bass("TRN2", target_bir_lowering=False, debug=False)

    def inp(name, shape):
        return nc.dram_tensor(name, list(shape), F32, kind="ExternalInput")

    hid = inp("hid", (NTOK, D_))
    flagc = inp("flagc", (128, 1))
    w_inT = inp("w_inT", (D_, PROJ))
    w_outT = inp("w_outT", (D_, D_))
    a_Td = inp("a_T", (D_, 1024))
    b_flat = inp("b_flat", (1024, D_))
    wr_Td = inp("wr_T", (D_, 64))
    convw_d = inp("convw", (128, NXBC * 4))
    convb_d = inp("convb", (128, NXBC))
    dtb_d = inp("dtb", (64, 1))
    alog_d = inp("alog", (64, 1))
    dvec_d = inp("dvec", (128, 32))
    normw_d = inp("normw", (128, 32))

    out_d = nc.dram_tensor("out", [NTOK, D_], F32, kind="ExternalOutput")

    def scratch(name, shape, dt):
        kind = "ExternalOutput" if debug else "Internal"
        return nc.dram_tensor(name, list(shape), dt, kind=kind)

    xbc_sp = scratch("xbc_sp", (NXBC, 128, NTOK), BF16)
    gate_sp = scratch("gate_sp", (32, 128, NTOK), BF16)
    x2_sp = scratch("x2_sp", (NTT, 128, D_), F32)
    ln2_sp = scratch("ln2_sp", (NDT, 128, NTOK), BF16)
    xt_sp = nc.dram_tensor("xt_sp", [NCH, 128, 4096], BF16)
    bt_sp = nc.dram_tensor("bt_sp", [NCH, 128, 1024], BF16)
    cc_h_in = nc.dram_tensor("cc_h_in", [128, NXBC * 3], F32)
    cc_h_out = nc.dram_tensor("cc_h_out", [256, NXBC * 3], F32)
    cc_s_in = nc.dram_tensor("cc_s_in", [128, H_ * P_], F32)
    cc_s_out = nc.dram_tensor("cc_s_out", [256, H_ * P_], F32)
    dbg = {}
    if debug:
        for nm, shp in [("d_dtp", (64, NTOK)), ("d_gcum", (64, NTOK)),
                        ("d_statefin", (128, H_ * P_)), ("d_s0", (128, H_ * P_)),
                        ("d_y", (NCH, 128, D_)), ("d_yn", (NCH, 128, D_)),
                        ("d_scores", (NTT, 128, 64)), ("d_gates", (NTT, 128, 64)),
                        ("d_hg", (128, 8 * NTOK)),
                        ("d_g0", (128, NTOK)), ("d_op", (128, 512)),
                        ("d_cw", (128, 64)), ("d_at", (128, 64)),
                        ("d_dtt", (128, 64)), ("d_bt", (128, 1024)),
                        ("d_xt0", (128, 128)), ("d_st1", (128, H_ * P_)),
                        ("d_sm", (128, 1024)), ("d_mt0", (128, 128)),
                        ("d_ych_pre", (128, D_))]:
            dbg[nm] = nc.dram_tensor(nm, list(shp), F32, kind="ExternalOutput")

    ident_bf_d = nc.inline_tensor(np.eye(128, dtype=bfdt), "c_ident_bf")
    ident_f32_d = nc.inline_tensor(np.eye(128, dtype=np.float32), "c_ident_f32")
    si = np.arange(128)[:, None]
    ti = np.arange(128)[None, :]
    maskadd_d = nc.inline_tensor(np.where(si <= ti, 0.0, -1e30).astype(np.float32),
                                 "c_maskadd")
    tri01_d = nc.inline_tensor((si <= ti).astype(bfdt), "c_tri01")
    onesr_d = nc.inline_tensor(np.ones((1, 128), dtype=bfdt), "c_onesr")
    gsel_np = np.zeros((128, 32 * 8), dtype=bfdt)
    for i in range(32):
        gsel_np[:, i * 8 + i // 4] = 1.0
    gsel_d = nc.inline_tensor(gsel_np, "c_gsel")
    E_np = np.zeros((64, 1024), dtype=bfdt)
    for kr in range(1024):
        E_np[kr // 16, kr] = 1.0
    E_d = nc.inline_tensor(E_np, "c_E")

    SB = nc.alloc_sbuf_tensor
    AR1 = SB("AR1", [128, 32768], BF16)
    SCR = SB("SCR", [128, 16896], BF16)
    AR3 = SB("AR3", [128, 3072], F32)
    state = SB("state", [128, H_ * P_], F32)
    ych = SB("ych", [128, D_], F32)
    hbuf = SB("hbuf", [128, D_], F32)
    wbuf = SB("wbuf", [128, 8192], BF16)
    dtp = SB("dtp", [64, NTOK], F32)
    gcum = SB("gcum", [64, NTOK], F32)
    ecfl = SB("ecfl", [1, 1024], BF16)
    bnout = SB("bnout", [128, 6 * 8], F32)
    bng = SB("bng", [128, 8 * 8 * 6], F32)
    aggr = SB("aggr", [128, 2], F32)
    cols = SB("cols", [128, 16], F32)
    colsg = SB("colsg", [8, 256], F32)
    rstd_bfr = SB("rstd_bfr", [8, 128], BF16)
    rfl = SB("rfl", [1, 1024], BF16)
    m8a = SB("m8a", [128, 8], F32)
    m8b = SB("m8b", [128, 8], F32)
    mrtmp = SB("mrtmp", [128, 64], F32)
    mrtmp_dump = SB("mrtmp_dump", [128, 512], F32)
    gbuf = SB("gbuf", [128, 64], F32)
    scores_b = SB("scores_b", [128, 64], F32)
    ecwrk = SB("ecwrk", [64, 128], F32)
    zc = SB("zc", [64, 128], F32)
    lntmp = SB("lntmp", [64, 128], F32)
    cwrow = SB("cwrow", [64, 128], F32)
    ecum_bf = SB("ecum_bf", [64, 128], BF16)
    dtp_tok = SB("dtp_tok", [128, 64], F32)
    cw_tok = SB("cw_tok", [128, 64], F32)
    atot_cols = SB("atot_cols", [128, 64], F32)
    arow = SB("arow", [1, 256], F32)
    egl = SB("egl", [64, 4], F32)
    epsc = SB("epsc", [128, 1], F32)
    onec = SB("onec", [128, 1], F32)
    y2b = SB("y2b", [128, 256], BF16)
    halo_send = SB("halo_send", [128, NXBC * 3], F32)
    halo_in = SB("halo_in", [128, NXBC * 3], F32)
    presilu = SB("presilu", [128, NXBC * 3], F32)
    fix3 = SB("fix3", [128, NXBC * 3], BF16)
    convw_s = SB("convw_s", [128, NXBC * 4], F32)
    convb_s = SB("convb_s", [128, NXBC], F32)
    dtb_s = SB("dtb_s", [64, 1], F32)
    alog_s = SB("alog_s", [64, 1], F32)
    nA = SB("nA", [64, 1], F32)
    dvec_s = SB("dvec_s", [128, 32], F32)
    normw_s = SB("normw_s", [128, 32], F32)
    flag_s = SB("flag_s", [128, 1], F32)
    ident_bf_s = SB("ident_bf_s", [128, 128], BF16)
    ident_f32_s = SB("ident_f32_s", [128, 128], F32)
    maskadd_s = SB("maskadd_s", [128, 128], F32)
    tri01_s = SB("tri01_s", [128, 128], BF16)
    onesr_s = SB("onesr_s", [1, 128], BF16)
    gsel_s = SB("gsel_s", [128, 32 * 8], BF16)
    E_s = SB("E_s", [64, 1024], BF16)

    BV3 = AR3.ap().bitcast(BF16)
    SCRF = SCR.ap().bitcast(F32)
    dtvals = ych[0:64, 0:1024]

    PA = nc.alloc_psum_tensor("pa", [128, 1024], F32)
    PB = nc.alloc_psum_tensor("pb", [128, 1024], F32)
    PY0 = nc.alloc_psum_tensor("py0", [128, 512], F32)
    PY1 = nc.alloc_psum_tensor("py1", [128, 512], F32)
    PSX = nc.alloc_psum_tensor("psx", [128, 512], F32)
    PST = nc.alloc_psum_tensor("pst", [128, 1024], BF16)
    PSF = [PA[:, 0:512], PA[:, 512:1024], PB[:, 0:512], PB[:, 512:1024],
           PY0[:, 0:512], PY1[:, 0:512]]

    em = Emit(nc)
    bufs = {}

    def B(name):
        if name not in bufs:
            bufs[name] = Buf(name)
        return bufs[name]

    bPSF = [B(f"psf{i}") for i in range(6)]
    bPST = B("pst")
    bPSX = B("psx")

    def vo(fn, r=(), w=(), war=()):
        return em.op("vector", fn, r, w, war)

    def so(fn, r=(), w=(), war=()):
        return em.op("scalar", fn, r, w, war)

    em.dma("sync", convw_s[:, :], convw_d.ap(), writes=[B("convw")])
    em.dma("sync", convb_s[:, :], convb_d.ap(), writes=[B("convb")])
    em.dma("sync", dtb_s[:, :], dtb_d.ap(), writes=[B("dtb")])
    em.dma("sync", alog_s[:, :], alog_d.ap(), writes=[B("alog")])
    em.dma("sync", dvec_s[:, :], dvec_d.ap(), writes=[B("dvec")])
    em.dma("sync", normw_s[:, :], normw_d.ap(), writes=[B("normw")])
    em.dma("sync", flag_s[:, :], flagc.ap(), writes=[B("flag")])
    em.dma("sync", ident_bf_s[:, :], ident_bf_d.ap(), writes=[B("identb")])
    em.dma("sync", ident_f32_s[:, :], ident_f32_d.ap(), writes=[B("identf")])
    em.dma("sync", maskadd_s[:, :], maskadd_d.ap(), writes=[B("maskadd")])
    em.dma("sync", tri01_s[:, :], tri01_d.ap(), writes=[B("tri01")])
    em.dma("sync", onesr_s[:, :], onesr_d.ap(), writes=[B("onesr")])
    em.dma("sync", gsel_s[:, :], gsel_d.ap(), writes=[B("gsel")])
    em.dma("sync", E_s[:, :], E_d.ap(), writes=[B("E")])
    so(lambda e: e.activation(out=nA[:, :], in_=alog_s[:, :], func=AF.Exp),
       [B("alog")], [B("nA")])
    vo(lambda e: e.tensor_scalar_mul(nA[:, :], nA[:, :], -1.0), [B("nA")], [B("nA")])
    vo(lambda e: e.memset(state[:, :], 0.0), [], [B("state")])
    vo(lambda e: e.memset(epsc[:, :], EPS), [], [B("epsc")])
    vo(lambda e: e.memset(onec[:, :], 1.0), [], [B("onec")])

    bln1T = B("ln1T")
    blnbf = B("lnbf")
    lnbf = SCR[:, 0:4096]
    for tt in range(NTT):
        em.dma("sync", hbuf[:, :], hid.ap()[tt * 128:(tt + 1) * 128, :],
               writes=[B("hbuf")])
        for i in range(8):
            vo(lambda e, i=i: e.bn_stats(bnout[:, i * 6:(i + 1) * 6],
                                         hbuf[:, i * 512:(i + 1) * 512]),
               [B("hbuf")], [B("bnout")])
        vo(lambda e: e.bn_aggr(aggr[:, :], bnout[:, :]), [B("bnout")], [B("aggr")])
        so(lambda e: e.activation(out=cols[:, 0:1], in_=aggr[:, 1:2], func=AF.Ln,
                                  bias=epsc[:, 0:1]), [B("aggr"), B("epsc")], [B("c0")])
        so(lambda e: e.activation(out=cols[:, 1:2], in_=cols[:, 0:1], func=AF.Exp,
                                  scale=-0.5), [B("c0")], [B("c1")])
        vo(lambda e: e.scalar_tensor_tensor(cols[:, 2:3], aggr[:, 0:1], -1.0,
                                            cols[:, 1:2], ALU.mult, ALU.mult),
           [B("aggr"), B("c1")], [B("c2")])
        vo(lambda e: e.tensor_scalar(lnbf, hbuf[:, :], cols[:, 1:2], cols[:, 2:3],
                                     ALU.mult, ALU.add),
           [B("hbuf"), B("c1"), B("c2")], [blnbf])
        for k in range(NDT):
            sl = k % 8
            em.mm(lambda e, k=k, sl=sl: e.transpose(
                PST[:, sl * 128:(sl + 1) * 128], lnbf[:, k * 128:(k + 1) * 128],
                ident_bf_s[:, :]),
                [blnbf, B("identb")], [bPST], war=[bPST], last=True)
            vo(lambda e, k=k, tt=tt, sl=sl: e.tensor_copy(
                AR1[:, k * NTOK + tt * 128: k * NTOK + tt * 128 + 128],
                PST[:, sl * 128:(sl + 1) * 128]),
               [bPST], [bln1T])

    pc_v = [AR3[:, 0:1024], AR3[:, 0:1024]]
    bpc = [B("pc0"), B("pc0")]
    cv_v = [AR3[:, 1024:2048], AR3[:, 2048:3072]]
    bcv = [B("cv0"), B("cv1")]
    xstage_r = [SCR[:, 4096:5120], SCR[:, 5120:6144]]
    tstage_r = [SCR[:, 0:1024], SCR[:, 1024:2048]]
    gstage = SCR[:, 5120:6144]
    bxs_r = [B("xstage0"), B("xstage1")]
    bts_r = [B("tstage0"), B("tstage1")]
    bgs = B("gstage")

    def wslot(s):
        return wbuf[:, s * 2048:(s + 1) * 2048]

    bws = [B(f"ws{i}") for i in range(4)]

    def tr_job(fx):
        xst = xstage_r[fx % 2]
        bxst = bxs_r[fx % 2]
        tst = tstage_r[fx % 2]
        btst = bts_r[fx % 2]
        for ch in range(8):
            sl = ch % 8
            em.mm(lambda e, ch=ch, sl=sl, xst=xst: e.transpose(
                PST[:, sl * 128:(sl + 1) * 128],
                xst[:, ch * 128:(ch + 1) * 128], ident_bf_s[:, :]),
                [bxst, B("identb")], [bPST], war=[bPST])
            vo(lambda e, ch=ch, sl=sl, tst=tst: e.tensor_copy(
                tst[:, ch * 128:(ch + 1) * 128],
                PST[:, sl * 128:(sl + 1) * 128]),
               [bPST], [btst], war=[blnbf])
        if fx < 32:
            em.dma("sync",
                   xt_sp.ap().rearrange("c p f -> p c f")
                   [:, :, fx * 128:(fx + 1) * 128],
                   tst.rearrange("p (c f) -> p c f", c=8),
                   reads=[btst], writes=[B("xt_hbm")])
        else:
            em.dma("sync",
                   bt_sp.ap().rearrange("c p f -> p c f")
                   [:, :, (fx - 32) * 128:(fx - 31) * 128],
                   tst.rearrange("p (c f) -> p c f", c=8),
                   reads=[btst], writes=[B("bt_hbm")])

    pend_tr = []

    for f in range(NFT):
        fw = 64 if f == NFT - 1 else 128

        for half in range(2):
            s = (2 * f + half) % 4
            em.dma("gpsimd",
                   wslot(s).rearrange("p (k f) -> p k f", k=16)[:, :, 0:fw],
                   w_inT.ap().rearrange("(k p) f -> p k f", p=128)
                   [:, half * 16:(half + 1) * 16, f * 128:f * 128 + fw],
                   writes=[bws[s]])
        pse = (0, 1) if f % 2 == 0 else (2, 3)
        for k in range(NDT):
            s = (2 * f + k // 16) % 4
            for j in range(2):
                pb, bpb = PSF[pse[j]], bPSF[pse[j]]
                em.mm(lambda e, k=k, j=j, pb=pb, s=s, fw=fw: e.matmul(
                    pb[0:fw, :],
                    wslot(s)[:, (k % 16) * 128:(k % 16) * 128 + fw],
                    AR1[:, k * NTOK + j * 512: k * NTOK + (j + 1) * 512],
                    start=(k == 0), stop=(k == NDT - 1)),
                    [bws[s], bln1T],
                    [bpb] if k == NDT - 1 else [],
                    war=[bpb] if k == 0 else [],
                    last=(k == NDT - 1))
        if f == NFT - 1:
            for j in range(2):
                vo(lambda e, j=j, pse=pse: e.tensor_copy(
                    dtvals[:, j * 512:(j + 1) * 512], PSF[pse[j]][0:64, :]),
                   [bPSF[pse[j]]], [B("dtvals")])
            continue
        pcb, bpcb = pc_v[f % 2], bpc[f % 2]
        if f < 32:
            for j in range(2):
                so(lambda e, j=j, pcb=pcb, pse=pse: e.activation(
                    out=pcb[:, j * 512:(j + 1) * 512], in_=PSF[pse[j]],
                    func=AF.Silu), [bPSF[pse[j]]], [bpcb])
            gst = xstage_r[f % 2]
            bgst = bxs_r[f % 2]
            vo(lambda e, f=f, pcb=pcb, gst=gst: e.tensor_scalar_mul(
                gst, pcb, normw_s[:, f:f + 1]),
               [bpcb, B("normw")], [bgst])
            if debug and f == 0:
                em.dma("sync", dbg["d_g0"].ap(), pcb, reads=[bpcb])
            em.dma("sync", gate_sp.ap()[f, :, :], gst, reads=[bgst], writes=[B("gate_hbm")])
        else:
            fx = f - 32
            for j in range(2):
                vo(lambda e, j=j, pcb=pcb, pse=pse: e.tensor_copy(
                    pcb[:, j * 512:(j + 1) * 512], PSF[pse[j]]),
                   [bPSF[pse[j]]], [bpcb])
            vo(lambda e, fx=fx, pcb=pcb: e.tensor_copy(
                halo_send[:, fx * 3:fx * 3 + 3], pcb[:, NTOK - 3:NTOK]),
               [bpcb], [B("halo_send")])
            cv0, cv1 = cv_v[0], cv_v[1]
            b0, b1 = bcv[0], bcv[1]
            w, bc = convw_s, convb_s
            vo(lambda e, fx=fx, pcb=pcb, cv0=cv0: e.tensor_scalar(
                cv0, pcb, w[:, fx * 4 + 3:fx * 4 + 4], bc[:, fx:fx + 1],
                ALU.mult, ALU.add),
               [bpcb, B("convw"), B("convb")], [b0])
            vo(lambda e, fx=fx, pcb=pcb, cv0=cv0, cv1=cv1: e.scalar_tensor_tensor(
                cv1[:, 1:NTOK], pcb[:, 0:NTOK - 1], w[:, fx * 4 + 2:fx * 4 + 3],
                cv0[:, 1:NTOK], ALU.mult, ALU.add),
               [bpcb, b0, B("convw")], [b1])
            vo(lambda e, cv0=cv0, cv1=cv1: e.tensor_copy(cv1[:, 0:1], cv0[:, 0:1]),
               [b0], [b1])
            vo(lambda e, fx=fx, pcb=pcb, cv0=cv0, cv1=cv1: e.scalar_tensor_tensor(
                cv0[:, 2:NTOK], pcb[:, 0:NTOK - 2], w[:, fx * 4 + 1:fx * 4 + 2],
                cv1[:, 2:NTOK], ALU.mult, ALU.add),
               [bpcb, b1, B("convw")], [b0])
            vo(lambda e, cv0=cv0, cv1=cv1: e.tensor_copy(cv0[:, 0:2], cv1[:, 0:2]),
               [b1], [b0])
            vo(lambda e, fx=fx, pcb=pcb, cv0=cv0, cv1=cv1: e.scalar_tensor_tensor(
                cv1[:, 3:NTOK], pcb[:, 0:NTOK - 3], w[:, fx * 4 + 0:fx * 4 + 1],
                cv0[:, 3:NTOK], ALU.mult, ALU.add),
               [bpcb, b0, B("convw")], [b1])
            vo(lambda e, cv0=cv0, cv1=cv1: e.tensor_copy(cv1[:, 0:3], cv0[:, 0:3]),
               [b0], [b1])
            vo(lambda e, fx=fx, cv1=cv1: e.tensor_copy(
                presilu[:, fx * 3:fx * 3 + 3], cv1[:, 0:3]),
               [b1], [B("presilu")])
            xst = xstage_r[fx % 2]
            bxst = bxs_r[fx % 2]
            tst = tstage_r[fx % 2]
            btst = bts_r[fx % 2]
            so(lambda e, cv1=cv1, xst=xst: e.activation(out=xst, in_=cv1,
                                                        func=AF.Silu),
               [b1], [bxst])
            em.dma("sync", xbc_sp.ap()[fx, :, :], xst, reads=[bxst],
                   writes=[B("xbc_hbm")])


    em.dma("sync", cc_h_in.ap(), halo_send[:, :], reads=[B("halo_send")],
           writes=[B("cch_in")])
    em.coll([cc_h_in.ap().opt()], [cc_h_out.ap().opt()],
            reads=[B("cch_in")], writes=[B("cch_out")])
    em.dma("sync", halo_in[:, :], cc_h_out.ap()[0:128, :], reads=[B("cch_out")],
           writes=[B("halo_in")])
    vo(lambda e: e.tensor_scalar_mul(halo_in[:, :], halo_in[:, :], flag_s[:, 0:1]),
       [B("halo_in"), B("flag")], [B("halo_in")])
    hv = halo_in.ap().rearrange("p (f j) -> p f j", f=NXBC)
    pv = presilu.ap().rearrange("p (f j) -> p f j", f=NXBC)
    wv = convw_s.ap().rearrange("p (f k) -> p f k", f=NXBC)
    tmp3 = mrtmp.ap().rearrange("p (f j) -> p f j", j=1)[:, 0:48, :]
    for (t, k, j) in [(0, 0, 0), (0, 1, 1), (0, 2, 2), (1, 0, 1), (1, 1, 2),
                      (2, 0, 2)]:
        vo(lambda e, k=k, j=j: e.tensor_tensor(
            tmp3, hv[:, :, j:j + 1], wv[:, :, k:k + 1], ALU.mult),
           [B("halo_in"), B("convw")], [B("mr")])
        vo(lambda e, t=t: e.tensor_tensor(
            pv[:, :, t:t + 1], pv[:, :, t:t + 1], tmp3, ALU.add),
           [B("mr"), B("presilu")], [B("presilu")])
    so(lambda e: e.activation(out=fix3[:, :], in_=presilu[:, :], func=AF.Silu),
       [B("presilu")], [B("fix3")])

    dt1 = AR3[:, 0:1024]
    dt2 = AR3[:, 1024:2048]
    bd1, bd2 = B("dt1"), B("dt2")
    vo(lambda e: e.tensor_scalar_add(dt1[0:64, :], dtvals[:, :], dtb_s[:, 0:1]),
       [B("dtvals"), B("dtb")], [bd1], war=[bpc[0], bcv[0]])
    so(lambda e: e.activation(out=dt2[0:64, :], in_=dt1[0:64, :], func=AF.Abs),
       [bd1], [bd2], war=[bcv[0], bcv[1]])
    so(lambda e: e.activation(out=dt2[0:64, :], in_=dt2[0:64, :], func=AF.Exp,
                              scale=-1.0), [bd2], [bd2])
    so(lambda e: e.activation(out=dt2[0:64, :], in_=dt2[0:64, :], func=AF.Ln,
                              bias=onec[0:64, 0:1]), [bd2, B("onec")], [bd2])
    vo(lambda e: e.tensor_scalar_max(dtp[:, :], dt1[0:64, :], 0.0), [bd1], [B("dtp")])
    vo(lambda e: e.tensor_add(dtp[:, :], dtp[:, :], dt2[0:64, :]),
       [B("dtp"), bd2], [B("dtp")])
    vo(lambda e: e.tensor_scalar_mul(dt1[0:64, :], dtp[:, :], nA[:, 0:1]),
       [B("dtp"), B("nA")], [bd1])
    vo(lambda e: e.tensor_tensor_scan(gcum[:, :], dt1[0:64, :], dt1[0:64, :], 0.0,
                                      ALU.add, ALU.bypass),
       [bd1], [B("gcum")])
    if debug:
        em.dma("sync", dbg["d_dtp"].ap(), dtp[:, :], reads=[B("dtp")])
        em.dma("sync", dbg["d_gcum"].ap(), gcum[:, :], reads=[B("gcum")])

    xch = SCR[:, 0:6144]
    xt_v = SCR[:, 6144:10240]
    bt_v = SCR[:, 10240:11264]
    sm_v = SCR[:, 11264:12288]
    state_bf = hbuf.ap().bitcast(BF16)[:, 0:4096]
    gate_ch = SCR[:, 12288:16384]
    ux_v = SCR[:, 16384:16896]
    bxch, bxt, bbt, bsm = B("xch"), B("xt"), B("bt"), B("sm")
    bsbf, bgch = B("sbf"), B("gch")
    tmask8 = AR3[:, 0:1024]
    exp8 = BV3[:, 2048:3072]
    mt8 = BV3[:, 3072:4096]
    cxt8 = BV3[:, 4096:5120]
    btm, bte, bmt, bcx = B("tmask"), B("texp"), B("mt"), B("cxt")

    def chunk_decay(c, need_ecum):
        c0, c1 = c * 128, (c + 1) * 128
        vo(lambda e, c0=c0, c1=c1: e.tensor_scalar(
            ecwrk[:, :], gcum[:, c0:c1], -1.0, gcum[:, c1 - 1:c1],
            ALU.mult, ALU.add), [B("gcum")], [B("ecw")])
        so(lambda e: e.activation(out=ecwrk[:, :], in_=ecwrk[:, :], func=AF.Exp),
           [B("ecw")], [B("ecw")])
        vo(lambda e, c0=c0, c1=c1: e.tensor_mul(
            cwrow[:, :], dtp[:, c0:c1], ecwrk[:, :]),
           [B("dtp"), B("ecw")], [B("cwr")])
        em.mm(lambda e: e.transpose(PSX[:, 0:64], cwrow[0:64, :], ident_f32_s[0:64, 0:64]),
              [B("cwr"), B("identf")], [bPSX], war=[bPSX])
        vo(lambda e: e.tensor_copy(cw_tok[:, 0:64], PSX[:, 0:64]),
           [bPSX], [B("cwt")])
        if c == 0:
            vo(lambda e, c1=c1: e.tensor_copy(egl[:, 0:1], gcum[:, c1 - 1:c1]),
               [B("gcum")], [B("egl")])
        else:
            vo(lambda e, c0=c0, c1=c1: e.tensor_scalar(
                egl[:, 0:1], gcum[:, c0 - 1:c0], -1.0, gcum[:, c1 - 1:c1],
                ALU.mult, ALU.add), [B("gcum")], [B("egl")])
        em.dma("sync", arow[0:1, 0:64], egl[:, 0:1], reads=[B("egl")],
               writes=[B("arow")])
        so(lambda e: e.activation(out=arow[0:1, 64:128], in_=arow[0:1, 0:64],
                                  func=AF.Exp), [B("arow")], [B("arow")])
        vo(lambda e: e.tensor_copy(arow.ap().bitcast(BF16)[0:1, 384:448],
                                   arow[0:1, 64:128]), [B("arow")], [B("arowb")])
        em.mm(lambda e: e.matmul(PSX[:, 256:320], onesr_s[0:1, :],
                                 arow.ap().bitcast(BF16)[0:1, 384:448],
                                 start=True, stop=True),
              [B("arowb"), B("onesr")], [bPSX], war=[bPSX])
        vo(lambda e: e.tensor_copy(atot_cols[:, :], PSX[:, 256:320]),
           [bPSX], [B("atc")])
        if need_ecum:
            if c == 0:
                so(lambda e, c0=c0, c1=c1: e.activation(
                    out=ecum_bf[:, :], in_=gcum[:, c0:c1], func=AF.Exp),
                    [B("gcum")], [B("ecb_r")])
            else:
                vo(lambda e, c0=c0: e.tensor_scalar_mul(
                    egl[:, 1:2], gcum[:, c0 - 1:c0], -1.0), [B("gcum")], [B("egl2")])
                so(lambda e, c0=c0, c1=c1: e.activation(
                    out=ecum_bf[:, :], in_=gcum[:, c0:c1], func=AF.Exp,
                    bias=egl[:, 1:2]), [B("gcum"), B("egl2")], [B("ecb_r")])
            so(lambda e, c0=c0, c1=c1: e.activation(
                out=lntmp[:, :], in_=dtp[:, c0:c1], func=AF.Ln),
               [B("dtp")], [B("lntmp")])
            vo(lambda e, c0=c0, c1=c1: e.tensor_sub(
                zc[:, :], gcum[:, c0:c1], lntmp[:, :]),
               [B("gcum"), B("lntmp")], [B("zc")])

    def load_chunk(c, n_ft):
        c0 = c * 128
        em.dma("sync", xch.rearrange("p (f t) -> p f t", f=NXBC)[:, 0:n_ft, :],
               xbc_sp.ap().rearrange("f p t -> p f t")[:, 0:n_ft, c0:c0 + 128],
               reads=[B("xbc_hbm")], writes=[bxch])
        if c == 0:
            vo(lambda e, n_ft=n_ft: e.tensor_copy(
                xch.rearrange("p (f t) -> p f t", f=NXBC)[:, 0:n_ft, 0:3],
                fix3.ap().rearrange("p (f j) -> p f j", f=NXBC)[:, 0:n_ft, :]),
               [B("fix3")], [bxch])
        for i in range(40):
            sl = i % 8
            em.mm(lambda e, i=i, sl=sl: e.transpose(
                PST[:, sl * 128:(sl + 1) * 128], xch[:, i * 128:(i + 1) * 128],
                ident_bf_s[:, :]),
                [bxch, B("identb")], [bPST], war=[bPST])
            if i < 32:
                vo(lambda e, i=i, sl=sl: e.tensor_copy(
                    xt_v[:, i * 128:(i + 1) * 128], PST[:, sl * 128:(sl + 1) * 128]),
                   [bPST], [bxt], war=[B("dtvals")])
            else:
                vo(lambda e, i=i, sl=sl: e.tensor_copy(
                    bt_v[:, (i - 32) * 128:(i - 31) * 128],
                    PST[:, sl * 128:(sl + 1) * 128]),
                   [bPST], [bbt])

    bux = [B(f"ux{i}") for i in range(8)]

    def state_update():
        for pr in range(32):
            g = pr // 4
            for hh in range(2):
                h = pr * 2 + hh
                us = h % 8
                vo(lambda e, h=h, pr=pr, hh=hh, us=us: e.tensor_scalar_mul(
                    ux_v[:, us * 64:(us + 1) * 64],
                    xt_v[:, pr * 128 + hh * 64: pr * 128 + hh * 64 + 64],
                    cw_tok[:, h:h + 1]),
                   [bxt, B("cwt")], [bux[us]])
            bank = 4 + pr % 2
            pb, bpb = PSF[bank], bPSF[bank]
            u0 = (pr * 2) % 8
            em.mm(lambda e, pr=pr, g=g, pb=pb, u0=u0: e.matmul(
                pb[:, 0:128], bt_v[:, g * 128:(g + 1) * 128],
                ux_v[:, u0 * 64: u0 * 64 + 128], start=True, stop=True),
                [bbt, bux[u0], bux[u0 + 1]], [bpb], war=[bpb])
            for hh in range(2):
                h = pr * 2 + hh
                vo(lambda e, h=h, hh=hh, pb=pb: e.scalar_tensor_tensor(
                    state[:, h * 64:(h + 1) * 64], state[:, h * 64:(h + 1) * 64],
                    atot_cols[:, h:h + 1], pb[:, hh * 64:hh * 64 + 64],
                    ALU.mult, ALU.add),
                   [B("state"), B("atc"), bpb], [B("state")])

    uxw = wbuf[:, 0:512]
    cwbf = wbuf[:, 512:576]
    buxw, bcwbf = B("uxw"), B("cwbf")
    xch_d = [xch[:, 0:5120], AR1[:, 0:5120]]
    xt_d = [xt_v, AR1[:, 5120:9216]]
    bt_d = [bt_v, AR1[:, 9216:10240]]
    bxch_d = [bxch, B("xchB")]
    bxt_d = [bxt, B("xtB")]
    bbt_d = [bbt, B("btB")]
    for c in range(NCH):
        c0, c1 = c * 128, (c + 1) * 128
        par = c % 2
        xchp, xtp, btp = xch_d[par], xt_d[par], bt_d[par]
        bxchp, bxtp, bbtp = bxch_d[par], bxt_d[par], bbt_d[par]
        em.dma("gpsimd", xchp.rearrange("p (f t) -> p f t", f=40),
               xbc_sp.ap().rearrange("f p t -> p f t")[:, 0:40, c0:c0 + 128],
               reads=[B("xbc_hbm")], writes=[bxchp],
               war=([bln1T] if c == 1 else []))
        if c == 0:
            vo(lambda e: e.tensor_copy(
                xchp.rearrange("p (f t) -> p f t", f=40)[:, :, 0:3],
                fix3.ap().rearrange("p (f j) -> p f j", f=NXBC)[:, 0:40, :]),
               [B("fix3")], [bxchp])
        for i in range(40):
            sl = i % 8
            em.mm(lambda e, i=i, sl=sl, xchp=xchp: e.transpose(
                PST[:, sl * 128:(sl + 1) * 128], xchp[:, i * 128:(i + 1) * 128],
                ident_bf_s[:, :]),
                [bxchp, B("identb")], [bPST], war=[bPST])
            if i < 32:
                vo(lambda e, i=i, sl=sl, xtp=xtp: e.tensor_copy(
                    xtp[:, i * 128:(i + 1) * 128], PST[:, sl * 128:(sl + 1) * 128]),
                   [bPST], [bxtp], war=([B("dtvals"), bln1T] if c <= 1 else []))
            else:
                vo(lambda e, i=i, sl=sl, btp=btp: e.tensor_copy(
                    btp[:, (i - 32) * 128:(i - 31) * 128],
                    PST[:, sl * 128:(sl + 1) * 128]),
                   [bPST], [bbtp], war=([bln1T] if c == 1 else []))
        em.dma("gpsimd", xt_sp.ap()[c, :, :], xtp, reads=[bxtp],
               writes=[B("xt_hbm")])
        em.dma("gpsimd", bt_sp.ap()[c, :, :], btp, reads=[bbtp],
               writes=[B("bt_hbm")])
        vo(lambda e, c0=c0, c1=c1: e.tensor_scalar(
            ecwrk[:, :], gcum[:, c0:c1], -1.0, gcum[:, NTOK - 1:NTOK],
            ALU.mult, ALU.add), [B("gcum")], [B("ecw")])
        so(lambda e: e.activation(out=ecwrk[:, :], in_=ecwrk[:, :], func=AF.Exp),
           [B("ecw")], [B("ecw")])
        vo(lambda e, c0=c0, c1=c1: e.tensor_mul(
            cwrow[:, :], dtp[:, c0:c1], ecwrk[:, :]),
           [B("dtp"), B("ecw")], [B("cwr")])
        em.mm(lambda e: e.transpose(PSX[:, 0:64], cwrow[0:64, :],
                                    ident_f32_s[0:64, 0:64]),
              [B("cwr"), B("identf")], [bPSX], war=[bPSX])
        vo(lambda e: e.tensor_copy(cwbf, PSX[:, 0:64]), [bPSX], [bcwbf],
           war=[bws[0]])
        for q in range(8):
            em.op("gpsimd", lambda e, q=q, xtp=xtp: e.tensor_tensor(
                uxw.rearrange("p (a t) -> p a t", a=8),
                xtp[:, q * 512:(q + 1) * 512].rearrange("p (a t) -> p a t", a=8),
                cwbf.rearrange("p (a t) -> p a t", t=1)[:, q * 8:(q + 1) * 8, :]
                .to_broadcast([128, 8, 64]), ALU.mult),
                [bxtp, bcwbf], [buxw], war=[bws[0]])
            bank = PY0 if q % 2 == 0 else PY1
            bbank = bPSF[4] if q % 2 == 0 else bPSF[5]
            for pp2 in range(4):
                pr = q * 4 + pp2
                g = pr // 4
                em.mm(lambda e, pr=pr, g=g, pp2=pp2, bank=bank, btp=btp: e.matmul(
                    bank[:, pp2 * 128:(pp2 + 1) * 128],
                    btp[:, g * 128:(g + 1) * 128],
                    uxw[:, pp2 * 128:(pp2 + 1) * 128], start=True, stop=True),
                    [bbtp, buxw], [bbank] if pp2 == 3 else [],
                    war=[bbank] if pp2 == 0 else [], last=(pp2 == 3), tick=True)
            vo(lambda e, q=q, bank=bank: e.tensor_tensor(
                state[:, q * 512:(q + 1) * 512], state[:, q * 512:(q + 1) * 512],
                bank[:, :], ALU.add),
               [B("state"), bbank], [B("state")])
    em.dma("sync", cc_s_in.ap(), state[:, :], reads=[B("state")],
           writes=[B("ccs_in")])
    em.coll([cc_s_in.ap().opt()], [cc_s_out.ap().opt()],
            reads=[B("ccs_in")], writes=[B("ccs_out")])
    if debug:
        em.dma("sync", dbg["d_statefin"].ap(), state[:, :], reads=[B("state")])
    em.dma("sync", state[:, :], cc_s_out.ap()[0:128, :],
           reads=[B("ccs_out")], writes=[B("state")])
    vo(lambda e: e.tensor_scalar_mul(state[:, :], state[:, :], flag_s[:, 0:1]),
       [B("state"), B("flag")], [B("state")])
    if debug:
        em.dma("sync", dbg["d_s0"].ap(), state[:, :], reads=[B("state")])

    bynT = B("ynT")
    bych = B("ych")
    PSTF = PST.ap().bitcast(F32)
    LH8 = hbuf[0:2, 2048:3072]
    RH8 = hbuf[0:2, 3072:4096]
    bLH8, bRH8 = B("LH8"), B("RH8")
    mtb = [wbuf[:, 1024:2048], wbuf[:, 2048:3072]]
    cxb = [wbuf[:, 3072:4096], wbuf[:, 4096:5120]]
    xd4 = wbuf.ap().bitcast(F32)[:, 2560:2816]
    bmtb = [B("mtb0"), B("mtb1")]
    bcxb = [B("cxb0"), B("cxb1")]
    bxd4 = B("xd4")
    vo(lambda e: e.memset(hbuf[0:2, 2048:3072], 1.0), [], [bLH8],
       war=[B("hbuf")])
    vo(lambda e: e.memset(hbuf[0:2, 3072:4096], -1.0), [], [bRH8],
       war=[B("hbuf")])
    for c in range(NCH):
        em.dma("gpsimd", xch.rearrange("p (f t) -> p f t", f=NXBC),
               xbc_sp.ap().rearrange("f p t -> p f t")[:, :, c * 128:(c + 1) * 128],
               reads=[B("xbc_hbm")], writes=[bxch])
        if c == 0:
            vo(lambda e: e.tensor_copy(
                xch.rearrange("p (f t) -> p f t", f=NXBC)[:, :, 0:3],
                fix3.ap().rearrange("p (f j) -> p f j", f=NXBC)),
               [B("fix3")], [bxch])
        em.dma("gpsimd", xt_v, xt_sp.ap()[c, :, :], reads=[B("xt_hbm")],
               writes=[bxt])
        em.dma("gpsimd", bt_v, bt_sp.ap()[c, :, :], reads=[B("bt_hbm")],
               writes=[bbt])
        chunk_decay(c, need_ecum=True)
        vo(lambda e: e.tensor_copy(cwbf, cw_tok[:, 0:64]), [B("cwt")], [bcwbf],
           war=[bws[0]])
        em.dma("gpsimd", gate_ch.rearrange("p (i t) -> p i t", i=32),
               gate_sp.ap().rearrange("i p t -> p i t")[:, :, c * 128:(c + 1) * 128],
               reads=[B("gate_hbm")], writes=[bgch])
        vo(lambda e: e.tensor_copy(state_bf, state[:, :]), [B("state")], [bsbf],
           war=[B("hbuf")])
        for r2 in range(2):
            for gg in range(4):
                g = r2 * 4 + gg
                em.mm(lambda e, g=g, gg=gg: e.matmul(
                    PSX[:, gg * 128:(gg + 1) * 128],
                    xch[:, (32 + g) * 128:(33 + g) * 128],
                    xch[:, (40 + g) * 128:(41 + g) * 128], start=True, stop=True),
                    [bxch], [bPSX], war=[bPSX] if gg == 0 else [],
                    last=(gg == 3), tick=False)
            vo(lambda e, r2=r2: e.tensor_tensor(
                sm_v.rearrange("p (a t) -> p a t", a=8)[:, r2 * 4:(r2 + 1) * 4, :],
                PSX.ap().rearrange("p (a t) -> p a t", a=4),
                tri01_s.ap().rearrange("p (a t) -> p a t", a=1)
                .to_broadcast([128, 4, 128]), ALU.mult),
               [bPSX, B("tri01")], [bsm])
        c0 = c * 128

        def fS1(g, c0=c0):
            pp = PA if g % 2 == 0 else PB
            bpp = [bPSF[0], bPSF[1]] if g % 2 == 0 else [bPSF[2], bPSF[3]]
            em.dma("sync", LH8[0:1, :], zc[g * 8: g * 8 + 8, :],
                   reads=[B("zc")], writes=[bLH8])
            em.dma("sync", RH8[1:2, :], gcum[g * 8: g * 8 + 8, c0:c0 + 128],
                   reads=[B("gcum")], writes=[bRH8])
            for hs in range(8):
                em.mm(lambda e, hs=hs, pp=pp: e.matmul(
                    pp[:, hs * 128:(hs + 1) * 128],
                    LH8[:, hs * 128:(hs + 1) * 128],
                    RH8[:, hs * 128:(hs + 1) * 128],
                    start=True, stop=True),
                    [bLH8, bRH8],
                    bpp if hs == 7 else [],
                    war=bpp if hs == 0 else [],
                    last=(hs == 7), tick=(hs == 3))

        def fS2(g):
            pp = PA if g % 2 == 0 else PB
            bpp = [bPSF[0], bPSF[1]] if g % 2 == 0 else [bPSF[2], bPSF[3]]
            par = g % 2
            vo(lambda e, pp=pp: e.scalar_tensor_tensor(
                tmask8.rearrange("p (a t) -> p a t", a=8),
                pp.ap().rearrange("p (a t) -> p a t", a=8), 1.0,
                maskadd_s.ap().rearrange("p (a t) -> p a t", a=1)
                .to_broadcast([128, 8, 128]), ALU.mult, ALU.add),
               bpp + [B("maskadd")], [btm], war=[bd1])
            so(lambda e: e.activation(out=exp8, in_=tmask8, func=AF.Exp),
               [btm], [bte], war=[bd2])
            em.dma("sync", ecfl[0:1, :], ecum_bf[g * 8:g * 8 + 8, :],
                   reads=[B("ecb_r")], writes=[B("ecfl")])
            vo(lambda e, g=g, par=par: e.tensor_tensor(
                mtb[par].rearrange("p (a t) -> p a t", a=8),
                exp8.rearrange("p (a t) -> p a t", a=8),
                sm_v[:, g * 128:(g + 1) * 128]
                .rearrange("p (a t) -> p a t", a=1).to_broadcast([128, 8, 128]),
                ALU.mult),
               [bte, bsm], [bmtb[par]], war=[bws[0], bws[1]])
            em.mm(lambda e: e.matmul(PSX[:, 0:512], onesr_s[0:1, :],
                                     ecfl[0:1, 0:512], start=True, stop=True),
                  [B("ecfl"), B("onesr")], [bPSX], war=[bPSX])
            em.mm(lambda e: e.matmul(PSTF[:, 0:512], onesr_s[0:1, :],
                                     ecfl[0:1, 512:1024], start=True, stop=True),
                  [B("ecfl"), B("onesr")], [bPST], war=[bPST])
            so(lambda e: e.activation(out=exp8[:, 0:512], in_=PSX[:, 0:512],
                                      func=AF.Copy),
               [bPSX], [bte])
            so(lambda e: e.activation(out=exp8[:, 512:1024], in_=PSTF[:, 0:512],
                                      func=AF.Copy),
               [bPST], [bte])
            vo(lambda e, g=g, par=par: e.tensor_tensor(
                cxb[par].rearrange("p (a t) -> p a t", a=8),
                exp8.rearrange("p (a t) -> p a t", a=8),
                xch[:, (40 + g) * 128:(41 + g) * 128]
                .rearrange("p (a t) -> p a t", a=1).to_broadcast([128, 8, 128]),
                ALU.mult),
               [bte, bxch], [bcxb[par]], war=[bws[1], bws[2], bws[3]])

        def fS3(g):
            par = g % 2
            for hh8 in range(8):
                h = g * 8 + hh8
                pr, hh = h // 2, h % 2
                q4 = (pr // 2) % 2
                sl2 = pr % 2
                pyp = PY0 if q4 == 0 else PY1
                bpy = bPSF[4] if q4 == 0 else bPSF[5]
                em.mm(lambda e, pr=pr, hh=hh, hh8=hh8, pyp=pyp, sl2=sl2, par=par: e.matmul(
                    pyp[hh * 64:hh * 64 + 64, sl2 * 128:(sl2 + 1) * 128],
                    xt_v[:, pr * 128 + hh * 64: pr * 128 + hh * 64 + 64],
                    mtb[par][:, hh8 * 128:(hh8 + 1) * 128],
                    start=True, stop=False, tile_position=(0, hh * 64)),
                    [bxt, bmtb[par]], [],
                    war=([bpy] if (hh == 0 and sl2 == 0) else []), last=False)
                em.mm(lambda e, h=h, hh=hh, hh8=hh8, pyp=pyp, sl2=sl2, par=par: e.matmul(
                    pyp[hh * 64:hh * 64 + 64, sl2 * 128:(sl2 + 1) * 128],
                    state_bf[:, h * 64:(h + 1) * 64],
                    cxb[par][:, hh8 * 128:(hh8 + 1) * 128],
                    start=False, stop=True, tile_position=(0, hh * 64)),
                    [bsbf, bcxb[par]],
                    [bpy] if (hh == 1 and sl2 == 1) else [],
                    last=(hh == 1 and sl2 == 1), tick=(hh == 1))
                if hh == 1 and sl2 == 1:
                    pr0 = pr - 1
                    em.op("gpsimd", lambda e, pr0=pr0: e.tensor_tensor(
                        xd4.rearrange("p (a t) -> p a t", a=2),
                        xch[:, pr0 * 128:(pr0 + 2) * 128]
                        .rearrange("p (a t) -> p a t", a=2),
                        dvec_s.ap().rearrange("p (a t) -> p a t", t=1)
                        [:, pr0:pr0 + 2, :].to_broadcast([128, 2, 128]), ALU.mult),
                        [bxch, B("dvec")], [bxd4], war=[bws[2]])
                    vo(lambda e, pr0=pr0, pyp=pyp: e.tensor_tensor(
                        ych[:, pr0 * 128:(pr0 + 2) * 128], xd4,
                        pyp[:, 0:256], ALU.add),
                       [bxd4, bpy], [bych], war=[B("dtvals")])

        for i in range(10):
            if i < 8:
                fS1(i)
            if 1 <= i < 9:
                fS2(i - 1)
            if i >= 2:
                fS3(i - 2)
        vo(lambda e: e.tensor_tensor(
            state.ap().rearrange("p (a t) -> p a t", a=64),
            state.ap().rearrange("p (a t) -> p a t", a=64),
            atot_cols.ap().rearrange("p (a t) -> p a t", t=1)
            .to_broadcast([128, 64, 64]), ALU.mult),
           [B("state"), B("atc")], [B("state")])
        for q in range(8):
            em.op("gpsimd", lambda e, q=q: e.tensor_tensor(
                uxw.rearrange("p (a t) -> p a t", a=8),
                xt_v[:, q * 512:(q + 1) * 512].rearrange("p (a t) -> p a t", a=8),
                cwbf.rearrange("p (a t) -> p a t", t=1)[:, q * 8:(q + 1) * 8, :]
                .to_broadcast([128, 8, 64]), ALU.mult),
                [bxt, bcwbf], [buxw], war=[bws[0]])
            bank = PY0 if q % 2 == 0 else PY1
            bbank = bPSF[4] if q % 2 == 0 else bPSF[5]
            for pp2 in range(4):
                pr = q * 4 + pp2
                g = pr // 4
                em.mm(lambda e, pr=pr, g=g, pp2=pp2, bank=bank: e.matmul(
                    bank[:, pp2 * 128:(pp2 + 1) * 128],
                    bt_v[:, g * 128:(g + 1) * 128],
                    uxw[:, pp2 * 128:(pp2 + 1) * 128], start=True, stop=True),
                    [bbt, buxw], [bbank] if pp2 == 3 else [],
                    war=[bbank] if pp2 == 0 else [], last=(pp2 == 3), tick=True)
            vo(lambda e, q=q, bank=bank: e.tensor_tensor(
                state[:, q * 512:(q + 1) * 512], state[:, q * 512:(q + 1) * 512],
                bank[:, :], ALU.add),
               [B("state"), bbank], [B("state")])
        if debug:
            em.dma("sync", dbg["d_y"].ap()[c, :, :], ych[:, :], reads=[bych])
        for i in range(32):
            ysl = i % 2
            em.op("gpsimd", lambda e, i=i, ysl=ysl: e.tensor_tensor(
                y2b[:, ysl * 128:(ysl + 1) * 128], ych[:, i * 128:(i + 1) * 128],
                ych[:, i * 128:(i + 1) * 128], ALU.mult),
                [bych], [B(f"y2{ysl}")])
            em.mm(lambda e, i=i, ysl=ysl: e.matmul(
                PSX[0:8, 0:128], gsel_s[:, i * 8:(i + 1) * 8],
                y2b[:, ysl * 128:(ysl + 1) * 128],
                start=(i == 0), stop=(i == 31)),
                [B(f"y2{ysl}"), B("gsel")],
                [bPSX] if i == 31 else [],
                war=[bPSX] if i == 0 else [], last=(i == 31), tick=True)
        vo(lambda e: e.tensor_copy(colsg[0:8, 0:128], PSX[0:8, 0:128]),
           [bPSX], [B("varg")])
        so(lambda e: e.activation(out=colsg[0:8, 128:256], in_=colsg[0:8, 0:128],
                                  func=AF.Ln, bias=epsc[0:8, 0:1], scale=1.0 / 512.0),
           [B("varg"), B("epsc")], [B("varg")])
        so(lambda e: e.activation(out=colsg[0:8, 0:128], in_=colsg[0:8, 128:256],
                                  func=AF.Exp, scale=-0.5),
           [B("varg")], [B("varg")])
        vo(lambda e: e.tensor_copy(rstd_bfr[0:8, :], colsg[0:8, 0:128]),
           [B("varg")], [B("rstdb")])
        em.dma("sync", rfl[0:1, :], rstd_bfr[0:8, :], reads=[B("rstdb")],
               writes=[B("rfl")])
        for g in range(8):
            em.mm(lambda e, g=g: e.matmul(
                PSX[:, 0:128], onesr_s[0:1, :], rfl[0:1, g * 128:(g + 1) * 128],
                start=True, stop=True),
                [B("rfl"), B("onesr")], [bPSX], war=[bPSX])
            vo(lambda e, g=g: e.tensor_tensor(
                xd4.rearrange("p (a t) -> p a t", a=2)
                if False else uxw.rearrange("p (a t) -> p a t", a=4)[:, 0:4, :],
                ych[:, g * 512:(g + 1) * 512].rearrange("p (a t) -> p a t", a=4),
                PSX[:, 0:128].rearrange("p (a t) -> p a t", a=1)
                .to_broadcast([128, 4, 128]), ALU.mult),
               [bych, bPSX], [buxw], war=[B("cwt2")])
            vo(lambda e, g=g, c=c: e.tensor_tensor(
                AR1.ap().rearrange("p (i t) -> p i t", i=NDT)
                [:, g * 4:(g + 1) * 4, c * 128:(c + 1) * 128],
                uxw.rearrange("p (a t) -> p a t", a=4),
                gate_ch.rearrange("p (a t) -> p a t", a=32)[:, g * 4:(g + 1) * 4, :],
                ALU.mult),
               [buxw, bgch], [bynT],
               war=([B("xchB"), B("xtB"), B("btB")] if c == 0 else []))
        if debug:
            em.dma("gpsimd",
                   dbg["d_yn"].ap()[c, :, :].rearrange("p (i t) -> p i t", i=NDT),
                   AR1.ap().rearrange("p (i t) -> p i t", i=NDT)
                   [:, :, c * 128:(c + 1) * 128],
                   reads=[bynT])

    bx2 = [B(f"x2_{tt}") for tt in range(NTT)]
    bwo = [B(f"wo{i}") for i in range(4)]

    def gpsum(tt):
        if tt < 6:
            return PSF[tt], bPSF[tt]
        if tt == 6:
            return PST.ap().bitcast(F32)[:, 0:512], bPST
        return PSX[:, 0:512], bPSX

    for dblk in range(8):
        for i in range(NDT):
            s = (dblk * NDT + i) % 4
            em.dma("gpsimd", wbuf[:, s * 512:(s + 1) * 512],
                   w_outT.ap()[i * 128:(i + 1) * 128, dblk * 512:(dblk + 1) * 512],
                   writes=[bwo[s]],
                   war=((bws + [B("uxw"), B("cwbf"), B("mtb0")]) if (dblk == 0 and i < 4) else []))
            for tt in range(NTT):
                pap, bpb = gpsum(tt)
                em.mm(lambda e, i=i, tt=tt, s=s, pap=pap: e.matmul(
                    pap, AR1[:, i * NTOK + tt * 128: i * NTOK + tt * 128 + 128],
                    wbuf[:, s * 512:(s + 1) * 512],
                    start=(i == 0), stop=(i == NDT - 1)),
                    [bynT, bwo[s]],
                    [bpb] if i == NDT - 1 else [],
                    war=[bpb] if i == 0 else [], last=(i == NDT - 1),
                    tick=(tt == NTT - 1))
        em.dma("sync", hbuf.ap().rearrange("p (a d) -> p a d", a=8),
               hid.ap().rearrange("(a p) d -> p a d", p=128)
               [:, :, dblk * 512:(dblk + 1) * 512],
               writes=[B("hbuf")], war=[bsbf, bLH8, bRH8])
        for tt in range(NTT):
            pap, bpb = gpsum(tt)
            if debug and dblk == 0 and tt == 0:
                vo(lambda e, pap=pap: e.tensor_copy(mrtmp_dump[:, :], pap), [bpb],
                   [B("opd")])
                em.dma("sync", dbg["d_op"].ap(), mrtmp_dump[:, :], reads=[B("opd")])
            vo(lambda e, tt=tt, pap=pap: e.tensor_tensor(
                ych[:, tt * 512:(tt + 1) * 512], pap,
                hbuf[:, tt * 512:(tt + 1) * 512], ALU.add),
               [bpb, B("hbuf")], [bych])
            vo(lambda e, tt=tt, dblk=dblk: e.bn_stats(
                bng[:, (tt * 8 + dblk) * 6:(tt * 8 + dblk) * 6 + 6],
                ych[:, tt * 512:(tt + 1) * 512]),
               [bych], [B("bng")])
            em.dma("sync", x2_sp.ap()[tt, :, dblk * 512:(dblk + 1) * 512],
                   ych[:, tt * 512:(tt + 1) * 512], reads=[bych], writes=[bx2[tt]])

    for half in range(2):
        em.dma("gpsimd",
               AR1.ap().rearrange("p (k t) -> p k t", k=8)
               [:, half * 4:(half + 1) * 4, :],
               b_flat.ap().rearrange("(k p) d -> p k d", p=128)
               [:, half * 4:(half + 1) * 4, :],
               writes=[B("bflat")], war=([bynT, bln1T] if half == 0 else []))
    bbf = B("bflat")
    wr_all = SCR[:, 4096:6144]
    bwr = B("wr_all")
    em.dma("gpsimd", wr_all.rearrange("p (k f) -> p k f", k=NDT),
           wr_Td.ap().rearrange("(k p) f -> p k f", p=128),
           writes=[bwr], war=[bxs_r[0], bxs_r[1], bgs, bxch])
    gt_all = SCR[:, 6144:7168]
    gexp_all = SCR[:, 8192:16384]
    bgt = B("gt_all")
    ln2st = BV3[:, 0:4096]
    bln2st = B("ln2st")
    for tt in range(NTT):
        vo(lambda e, tt=tt: e.bn_aggr(aggr[:, :], bng[:, tt * 48:(tt + 1) * 48]),
           [B("bng")], [B("aggr")])
        so(lambda e: e.activation(out=cols[:, 0:1], in_=aggr[:, 1:2], func=AF.Ln,
                                  bias=epsc[:, 0:1]), [B("aggr"), B("epsc")], [B("c0")])
        so(lambda e: e.activation(out=cols[:, 1:2], in_=cols[:, 0:1], func=AF.Exp,
                                  scale=-0.5), [B("c0")], [B("c1")])
        vo(lambda e: e.scalar_tensor_tensor(cols[:, 2:3], aggr[:, 0:1], -1.0,
                                            cols[:, 1:2], ALU.mult, ALU.mult),
           [B("aggr"), B("c1")], [B("c2")])
        em.dma("sync", hbuf[:, :], x2_sp.ap()[tt, :, :],
               reads=[bx2[tt]], writes=[B("hbuf")])
        vo(lambda e: e.tensor_scalar(lnbf, hbuf[:, :], cols[:, 1:2], cols[:, 2:3],
                                     ALU.mult, ALU.add),
           [B("hbuf"), B("c1"), B("c2")], [blnbf], war=[bxch, bts_r[0], bts_r[1]])
        for k in range(NDT):
            sl = k % 8
            em.mm(lambda e, k=k, sl=sl: e.transpose(
                PST[:, sl * 128:(sl + 1) * 128], lnbf[:, k * 128:(k + 1) * 128],
                ident_bf_s[:, :]),
                [blnbf, B("identb")], [bPST], war=[bPST])
            vo(lambda e, k=k, sl=sl: e.tensor_copy(
                ln2st[:, k * 128:(k + 1) * 128], PST[:, sl * 128:(sl + 1) * 128]),
               [bPST], [bln2st],
               war=([btm, bte, bmt, bcx, bd1, bd2, bpc[0]] if tt == 0 and k == 0
                    else []))
        em.dma("sync", ln2_sp.ap().rearrange("k p t -> p k t")[:, :, tt * 128:(tt + 1) * 128],
               ln2st.rearrange("p (k t) -> p k t", k=NDT),
               reads=[bln2st], writes=[B("ln2_hbm")])
        for k in range(NDT):
            em.mm(lambda e, k=k: e.matmul(
                PSX[:, 0:64], ln2st[:, k * 128:(k + 1) * 128],
                wr_all[:, k * 64:(k + 1) * 64], start=(k == 0), stop=(k == NDT - 1)),
                [bln2st, bwr],
                [bPSX] if k == NDT - 1 else [],
                war=[bPSX] if k == 0 else [], last=(k == NDT - 1))
        vo(lambda e: e.tensor_copy(scores_b[:, :], PSX[:, 0:64]),
           [bPSX], [B("scb")])
        if debug:
            em.dma("sync", dbg["d_scores"].ap()[tt, :, :], scores_b[:, :],
                   reads=[B("scb")])
        vo(lambda e: e.max(out=m8a[:, :], in_=scores_b[:, :]), [B("scb")], [B("m8a")])
        vo(lambda e: e.match_replace(out=mrtmp[:, :], in_to_replace=m8a[:, :],
                                     in_values=scores_b[:, :], imm_value=-1e30),
           [B("scb"), B("m8a")], [B("mr")])
        vo(lambda e: e.max(out=m8b[:, :], in_=mrtmp[:, :]), [B("mr")], [B("m8b")])
        vo(lambda e: e.tensor_scalar_mul(cols[:, 3:4], m8a[:, 0:1], -1.0),
           [B("m8a")], [B("c3")])
        so(lambda e: e.activation(out=gbuf[:, :], in_=scores_b[:, :], func=AF.Exp,
                                  bias=cols[:, 3:4]), [B("scb"), B("c3")], [B("gb")])
        vo(lambda e: e.tensor_scalar(mrtmp[:, :], scores_b[:, :], m8b[:, 7:8], None,
                                     ALU.is_ge), [B("scb"), B("m8b")], [B("mr")])
        vo(lambda e: e.tensor_mul(gbuf[:, :], gbuf[:, :], mrtmp[:, :]),
           [B("gb"), B("mr")], [B("gb")])
        vo(lambda e: e.reduce_sum(cols[:, 4:5], gbuf[:, :], axis=AX),
           [B("gb")], [B("c4")])
        vo(lambda e: e.reciprocal(cols[:, 5:6], cols[:, 4:5]), [B("c4")], [B("c5")])
        vo(lambda e: e.tensor_scalar_mul(gbuf[:, :], gbuf[:, :], cols[:, 5:6]),
           [B("gb"), B("c5")], [B("gb")])
        if debug:
            em.dma("sync", dbg["d_gates"].ap()[tt, :, :], gbuf[:, :],
                   reads=[B("gb")])
        em.mm(lambda e: e.transpose(PSX[0:64, 128:256], gbuf[:, 0:64],
                                    ident_f32_s[:, :]),
              [B("gb"), B("identf")], [bPSX], war=[bPSX])
        vo(lambda e, tt=tt: e.tensor_copy(gt_all[0:64, tt * 128:(tt + 1) * 128],
                                          PSX[0:64, 128:256]),
           [bPSX], [bgt], war=([bxt] if tt == 0 else []))
        for kr in range(8):
            em.mm(lambda e, kr=kr, tt=tt: e.matmul(
                PSX[:, 0:128], E_s[0:64, kr * 128:(kr + 1) * 128],
                gt_all[0:64, tt * 128:(tt + 1) * 128], start=True, stop=True),
                [B("E"), bgt], [bPSX], war=[bPSX])
            vo(lambda e, kr=kr, tt=tt: e.tensor_copy(
                gexp_all[:, kr * NTOK + tt * 128: kr * NTOK + tt * 128 + 128],
                PSX[:, 0:128]),
               [bPSX], [B("gexp")],
               war=([bsbf, bgch] if (tt == 0 and kr == 0) else []))

    hg_v = SCR[:, 0:8192]
    bhg = B("hg")
    bah = [B("ah0"), B("ah1")]
    aring = [wbuf[:, i * 1024:(i + 1) * 1024] for i in range(4)]
    baring = [B(f"ar{i}") for i in range(4)]
    ln2r = [AR3.ap().bitcast(BF16)[:, i * 512:(i + 1) * 512] for i in range(4)]
    bln2r = [B(f"ln2r{i}") for i in range(4)]
    HB = [(PA[:, 0:512], bPSF[0]), (PA[:, 512:1024], bPSF[1]),
          (PB[:, 0:512], bPSF[2]), (PB[:, 512:1024], bPSF[3]),
          (PY0[:, 0:512], bPSF[4]), (PY1[:, 0:512], bPSF[5]),
          (PSX[:, 0:512], bPSX), (PSTF[:, 0:512], bPST)]
    for half in range(2):
        for k in range(NDT):
            rs = (half * NDT + k) % 4
            em.dma("sync", ln2r[rs],
                   ln2_sp.ap()[k, :, half * 512:(half + 1) * 512],
                   reads=[B("ln2_hbm")], writes=[bln2r[rs]],
                   war=([bd1, bd2, btm, bte, bmt, bcx, bln2st] if
                        (half == 0 and k < 4) else []))
            em.dma("gpsimd", aring[rs],
                   a_Td.ap()[k * 128:(k + 1) * 128, :],
                   writes=[baring[rs]],
                   war=((bwo + [B("uxw"), B("cwbf"), B("mtb0"), B("mtb1"),
                                B("cxb0"), B("cxb1")]) if (half == 0 and k < 4)
                        else []))
            for kr in range(8):
                pap, bpb = HB[kr]
                em.mm(lambda e, k=k, kr=kr, rs=rs, pap=pap: e.matmul(
                    pap, aring[rs][:, kr * 128:(kr + 1) * 128],
                    ln2r[rs], start=(k == 0), stop=(k == NDT - 1)),
                    [baring[rs], bln2r[rs]],
                    [bpb] if k == NDT - 1 else [],
                    war=[bpb] if k == 0 else [],
                    last=(k == NDT - 1), tick=(kr == 7))
        for kr in range(8):
            pap, bpb = HB[kr]
            vo(lambda e, kr=kr, half=half, pap=pap: e.tensor_tensor(
                hg_v[:, kr * NTOK + half * 512: kr * NTOK + (half + 1) * 512],
                pap, gexp_all[:, kr * NTOK + half * 512: kr * NTOK + (half + 1) * 512],
                ALU.mult),
               [bpb, B("gexp")], [bhg],
               war=([B("uxw"), B("cwbf")] if (half == 0 and kr == 0) else []))
    if debug:
        em.dma("gpsimd", dbg["d_hg"].ap(), hg_v, reads=[bhg])
    for tt in range(NTT):
        for kr in range(8):
            for ds in range(8):
                pap, bpb = gpsum(ds)
                em.mm(lambda e, kr=kr, tt=tt, ds=ds, pap=pap: e.matmul(
                    pap, hg_v[:, kr * NTOK + tt * 128: kr * NTOK + tt * 128 + 128],
                    AR1[:, kr * D_ + ds * 512: kr * D_ + (ds + 1) * 512],
                    start=(kr == 0), stop=(kr == 7)),
                    [bhg, bbf],
                    [bpb] if kr == 7 else [],
                    war=[bpb] if kr == 0 else [], last=(kr == 7))
        em.dma("sync", hbuf[:, :], x2_sp.ap()[tt, :, :],
               reads=[bx2[tt]], writes=[B("hbuf")])
        for ds in range(8):
            pap, bpb = gpsum(ds)
            vo(lambda e, ds=ds, pap=pap: e.scalar_tensor_tensor(
                ych[:, ds * 512:(ds + 1) * 512], pap, 2.0,
                hbuf[:, ds * 512:(ds + 1) * 512], ALU.mult, ALU.add),
               [bpb, B("hbuf")], [bych])
        em.dma("sync", out_d.ap()[tt * 128:(tt + 1) * 128, :], ych[:, :],
               reads=[bych], writes=[B("outd")])

    em.wait_all("sync", [B("outd")])
    em.ops["sync"].append(
        ([(k, em.cnt[k]) for k in em.cnt if k[:2] in ("qd", "pd") and em.cnt[k]],
         None, None, 0))
    em.finalize()
    return nc, dbg


_CACHE = {}


def _get_nc(debug=False):
    key = bool(debug)
    if key not in _CACHE:
        _CACHE[key] = build(debug)
    return _CACHE[key]


def kernel(**inputs):
    hidf = np.ascontiguousarray(np.asarray(inputs["L0_attn_hidden_states"],
                                           dtype=np.float32))
    w_in = np.asarray(inputs["L0_attn_in_proj_w"], dtype=np.float32)
    convw = np.asarray(inputs["L0_attn_conv_weight"], dtype=np.float32)
    convb = np.asarray(inputs["L0_attn_conv_bias"], dtype=np.float32)
    dtb = np.asarray(inputs["L0_attn_dt_bias"], dtype=np.float32)
    alog = np.asarray(inputs["L0_attn_A_log"], dtype=np.float32)
    Dv = np.asarray(inputs["L0_attn_D"], dtype=np.float32)
    normw = np.asarray(inputs["L0_attn_norm_weight"], dtype=np.float32)
    w_out = np.asarray(inputs["L0_attn_out_proj_w"], dtype=np.float32)
    A = np.asarray(inputs["L0_ffn_A"], dtype=np.float32)
    Bf = np.asarray(inputs["L0_ffn_B"], dtype=np.float32)
    Wr = np.asarray(inputs["L0_ffn_Wr"], dtype=np.float32)

    shared = {
        "w_inT": np.ascontiguousarray(w_in.T),
        "w_outT": np.ascontiguousarray(w_out.T),
        "a_T": np.ascontiguousarray(A.reshape(1024, D_).T),
        "b_flat": np.ascontiguousarray(Bf.transpose(0, 2, 1).reshape(1024, D_)),
        "wr_T": np.ascontiguousarray(Wr.T),
        "convw": np.ascontiguousarray(
            convw.reshape(NXBC, 128, 4).transpose(1, 0, 2).reshape(128, NXBC * 4)),
        "convb": np.ascontiguousarray(convb.reshape(NXBC, 128).T),
        "dtb": dtb.reshape(64, 1).copy(),
        "alog": alog.reshape(64, 1).copy(),
        "dvec": np.ascontiguousarray(np.repeat(Dv, P_).reshape(32, 128).T),
        "normw": np.ascontiguousarray(normw.reshape(32, 128).T),
    }
    nc, _ = _get_nc(False)
    in_maps = []
    for c in range(8):
        b, th = c // 2, c % 2
        m = dict(shared)
        m["hid"] = np.ascontiguousarray(hidf[b, th * NTOK:(th + 1) * NTOK, :])
        m["flagc"] = np.full((128, 1), float(th), dtype=np.float32)
        in_maps.append(m)
    r = run_bass_kernel_spmd(nc, in_maps, core_ids=list(range(8)))
    out = np.empty((B_, L_, D_), dtype=np.float32)
    for c in range(8):
        b, th = c // 2, c % 2
        out[b, th * NTOK:(th + 1) * NTOK, :] = np.asarray(r.results[c]["out"])
    return out


# version 13 (final)
# speedup vs baseline: 1.0041x; 1.0041x over previous
import numpy as np
import ml_dtypes
import concourse.bass as bass
import concourse.mybir as mybir
from concourse.bass_utils import run_bass_kernel_spmd

F32 = mybir.dt.float32
BF16 = mybir.dt.bfloat16
AF = mybir.ActivationFunctionType
ALU = mybir.AluOpType
AX = mybir.AxisListType.X
bfdt = ml_dtypes.bfloat16

B_, L_, D_ = 4, 2048, 4096
NTOK = 1024
NTT = 8
NDT = 32
PROJ = 10304
NFT = 81
NXBC = 48
NCH = 8
H_, P_, N_, G_ = 64, 64, 128, 8
EPS = 1e-5
GROUPS = [[0, 1], [2, 3], [4, 5], [6, 7]]


class Buf:
    __slots__ = ("name", "last_w", "readers")

    def __init__(self, name):
        self.name = name
        self.last_w = None
        self.readers = []


class Emit:
    ENGINES = ("tensor", "vector", "scalar", "gpsimd", "sync")

    def __init__(self, nc):
        self.nc = nc
        self.ops = {e: [] for e in self.ENGINES}
        self.cnt = {}
        self.syncd = {e: {} for e in self.ENGINES}
        self.sems = {}
        keys = ["t", "v", "s", "p"]
        keys += [f"qd{i}" for i in range(8)] + [f"pd{i}" for i in range(8)]
        for k in keys:
            self.sems[k] = nc.alloc_semaphore("sem_" + k)
            self.cnt[k] = 0
        self.dma_i = {"sync": 0, "gpsimd": 0}
        self.eng_sem = {"tensor": "t", "vector": "v", "scalar": "s", "gpsimd": "p"}
        self.pending = {e: [] for e in self.ENGINES}

    def _waits(self, eng, bufs_r, bufs_w):
        need = {}
        for b in bufs_r:
            if b.last_w is not None:
                k, v = b.last_w
                if need.get(k, 0) < v:
                    need[k] = v
        for b in bufs_w:
            if b.last_w is not None:
                k, v = b.last_w
                if need.get(k, 0) < v:
                    need[k] = v
            for k, v in b.readers:
                if need.get(k, 0) < v:
                    need[k] = v
        out = []
        sy = self.syncd[eng]
        for k, v in need.items():
            if sy.get(k, 0) < v:
                out.append((k, v))
                sy[k] = v
        return out

    def _rec(self, eng, fn, sk, n_inc, reads, writes, war):
        waits = self._waits(eng, reads, list(writes) + list(war))
        self.cnt[sk] += n_inc
        newv = self.cnt[sk]
        self.ops[eng].append((waits, fn, sk, n_inc))
        done = (sk, newv)
        if n_inc:
            for b in self.pending[eng]:
                b.readers.append(done)
            self.pending[eng] = []
            for b in writes:
                b.last_w = done
                b.readers = []
            for b in reads:
                b.readers.append(done)
        else:
            self.pending[eng].extend(reads)
        return done

    def op(self, eng, fn, reads=(), writes=(), war=(), n_inc=1):
        return self._rec(eng, fn, self.eng_sem[eng], n_inc, reads, writes, war)

    def mm(self, fn, reads=(), writes=(), war=(), last=True, tick=False):
        return self._rec("tensor", fn, "t", 1 if (last or tick) else 0,
                         reads, writes, war)

    def dma(self, queue, out_ap, in_ap, reads=(), writes=(), war=()):
        i = self.dma_i[queue]
        self.dma_i[queue] = i + 1
        sk = ("qd" if queue == "sync" else "pd") + str(i % 8)

        def fn(e, _o=out_ap, _i=in_ap):
            return e.dma_start(out=_o, in_=_i)

        return self._rec(queue, fn, sk, 16, reads, writes, war)

    def coll(self, ins, outs, reads=(), writes=()):
        def fn(e):
            return e.collective_compute("AllGather", ALU.bypass,
                                        replica_groups=GROUPS, ins=ins, outs=outs)

        return self._rec("gpsimd", fn, "p", 1, reads, writes, ())

    def wait_all(self, eng, bufs):
        waits = self._waits(eng, bufs, ())
        if waits:
            self.ops[eng].append((waits, None, None, 0))

    def finalize(self):
        sems = self.sems
        with self.nc.Block() as block:
            for eng in self.ENGINES:
                lst = self.ops[eng]

                def run(e, _lst=lst):
                    for waits, fn, sk, n_inc in _lst:
                        for k, v in waits:
                            e.wait_ge(sems[k], v)
                        if fn is not None:
                            ins = fn(e)
                            if n_inc:
                                ins.then_inc(sems[sk], n_inc)
                getattr(block, eng)(run)


def build(debug=False):
    nc = bass.Bass("TRN2", target_bir_lowering=False, debug=False)

    def inp(name, shape):
        return nc.dram_tensor(name, list(shape), F32, kind="ExternalInput")

    hid = inp("hid", (NTOK, D_))
    flagc = inp("flagc", (128, 1))
    w_inT = inp("w_inT", (D_, PROJ))
    w_outT = inp("w_outT", (D_, D_))
    a_Td = inp("a_T", (D_, 1024))
    b_flat = inp("b_flat", (1024, D_))
    wr_Td = inp("wr_T", (D_, 64))
    convw_d = inp("convw", (128, NXBC * 4))
    convb_d = inp("convb", (128, NXBC))
    dtb_d = inp("dtb", (64, 1))
    alog_d = inp("alog", (64, 1))
    dvec_d = inp("dvec", (128, 32))
    normw_d = inp("normw", (128, 32))

    out_d = nc.dram_tensor("out", [NTOK, D_], F32, kind="ExternalOutput")

    def scratch(name, shape, dt):
        kind = "ExternalOutput" if debug else "Internal"
        return nc.dram_tensor(name, list(shape), dt, kind=kind)

    xbc_sp = scratch("xbc_sp", (NXBC, 128, NTOK), BF16)
    gate_sp = scratch("gate_sp", (32, 128, NTOK), BF16)
    x2_sp = scratch("x2_sp", (NTT, 128, D_), F32)
    ln2_sp = scratch("ln2_sp", (NDT, 128, NTOK), BF16)
    xt_sp = nc.dram_tensor("xt_sp", [NCH, 128, 4096], BF16)
    bt_sp = nc.dram_tensor("bt_sp", [NCH, 128, 1024], BF16)
    cc_h_in = nc.dram_tensor("cc_h_in", [128, NXBC * 3], F32)
    cc_h_out = nc.dram_tensor("cc_h_out", [256, NXBC * 3], F32)
    cc_s_in = nc.dram_tensor("cc_s_in", [128, H_ * P_], F32)
    cc_s_out = nc.dram_tensor("cc_s_out", [256, H_ * P_], F32)
    dbg = {}
    if debug:
        for nm, shp in [("d_dtp", (64, NTOK)), ("d_gcum", (64, NTOK)),
                        ("d_statefin", (128, H_ * P_)), ("d_s0", (128, H_ * P_)),
                        ("d_y", (NCH, 128, D_)), ("d_yn", (NCH, 128, D_)),
                        ("d_scores", (NTT, 128, 64)), ("d_gates", (NTT, 128, 64)),
                        ("d_hg", (128, 8 * NTOK)),
                        ("d_g0", (128, NTOK)), ("d_op", (128, 512)),
                        ("d_cw", (128, 64)), ("d_at", (128, 64)),
                        ("d_dtt", (128, 64)), ("d_bt", (128, 1024)),
                        ("d_xt0", (128, 128)), ("d_st1", (128, H_ * P_)),
                        ("d_sm", (128, 1024)), ("d_mt0", (128, 128)),
                        ("d_ych_pre", (128, D_))]:
            dbg[nm] = nc.dram_tensor(nm, list(shp), F32, kind="ExternalOutput")

    ident_bf_d = nc.inline_tensor(np.eye(128, dtype=bfdt), "c_ident_bf")
    ident_f32_d = nc.inline_tensor(np.eye(128, dtype=np.float32), "c_ident_f32")
    si = np.arange(128)[:, None]
    ti = np.arange(128)[None, :]
    maskadd_d = nc.inline_tensor(np.where(si <= ti, 0.0, -1e30).astype(np.float32),
                                 "c_maskadd")
    tri01_d = nc.inline_tensor((si <= ti).astype(bfdt), "c_tri01")
    onesr_d = nc.inline_tensor(np.ones((1, 128), dtype=bfdt), "c_onesr")
    gsel_np = np.zeros((128, 32 * 8), dtype=bfdt)
    for i in range(32):
        gsel_np[:, i * 8 + i // 4] = 1.0
    gsel_d = nc.inline_tensor(gsel_np, "c_gsel")
    E_np = np.zeros((64, 1024), dtype=bfdt)
    for kr in range(1024):
        E_np[kr // 16, kr] = 1.0
    E_d = nc.inline_tensor(E_np, "c_E")

    SB = nc.alloc_sbuf_tensor
    AR1 = SB("AR1", [128, 32768], BF16)
    SCR = SB("SCR", [128, 16896], BF16)
    AR3 = SB("AR3", [128, 3072], F32)
    state = SB("state", [128, H_ * P_], F32)
    ych = SB("ych", [128, D_], F32)
    hbuf = SB("hbuf", [128, D_], F32)
    wbuf = SB("wbuf", [128, 8192], BF16)
    dtp = SB("dtp", [64, NTOK], F32)
    gcum = SB("gcum", [64, NTOK], F32)
    ecfl = SB("ecfl", [1, 1024], BF16)
    bnout = SB("bnout", [128, 6 * 8], F32)
    bng = SB("bng", [128, 8 * 8 * 6], F32)
    aggr = SB("aggr", [128, 2], F32)
    cols = SB("cols", [128, 16], F32)
    colsg = SB("colsg", [8, 256], F32)
    rstd_bfr = SB("rstd_bfr", [8, 128], BF16)
    rfl = SB("rfl", [1, 1024], BF16)
    m8a = SB("m8a", [128, 8], F32)
    m8b = SB("m8b", [128, 8], F32)
    mrtmp = SB("mrtmp", [128, 64], F32)
    mrtmp_dump = SB("mrtmp_dump", [128, 512], F32)
    gbuf = SB("gbuf", [128, 64], F32)
    scores_b = SB("scores_b", [128, 64], F32)
    ecwrk = SB("ecwrk", [64, 128], F32)
    zc = SB("zc", [64, 128], F32)
    lntmp = SB("lntmp", [64, 128], F32)
    cwrow = SB("cwrow", [64, 128], F32)
    ecum_bf = SB("ecum_bf", [64, 128], BF16)
    dtp_tok = SB("dtp_tok", [128, 64], F32)
    cw_tok = SB("cw_tok", [128, 64], F32)
    atot_cols = SB("atot_cols", [128, 64], F32)
    arow = SB("arow", [1, 256], F32)
    egl = SB("egl", [64, 4], F32)
    epsc = SB("epsc", [128, 1], F32)
    onec = SB("onec", [128, 1], F32)
    y2b = SB("y2b", [128, 256], BF16)
    halo_send = SB("halo_send", [128, NXBC * 3], F32)
    halo_in = SB("halo_in", [128, NXBC * 3], F32)
    presilu = SB("presilu", [128, NXBC * 3], F32)
    fix3 = SB("fix3", [128, NXBC * 3], BF16)
    convw_s = SB("convw_s", [128, NXBC * 4], F32)
    convb_s = SB("convb_s", [128, NXBC], F32)
    dtb_s = SB("dtb_s", [64, 1], F32)
    alog_s = SB("alog_s", [64, 1], F32)
    nA = SB("nA", [64, 1], F32)
    dvec_s = SB("dvec_s", [128, 32], F32)
    normw_s = SB("normw_s", [128, 32], F32)
    flag_s = SB("flag_s", [128, 1], F32)
    ident_bf_s = SB("ident_bf_s", [128, 128], BF16)
    ident_f32_s = SB("ident_f32_s", [128, 128], F32)
    maskadd_s = SB("maskadd_s", [128, 128], F32)
    tri01_s = SB("tri01_s", [128, 128], BF16)
    onesr_s = SB("onesr_s", [1, 128], BF16)
    gsel_s = SB("gsel_s", [128, 32 * 8], BF16)
    E_s = SB("E_s", [64, 1024], BF16)

    BV3 = AR3.ap().bitcast(BF16)
    SCRF = SCR.ap().bitcast(F32)
    dtvals = ych[0:64, 0:1024]

    PA = nc.alloc_psum_tensor("pa", [128, 1024], F32)
    PB = nc.alloc_psum_tensor("pb", [128, 1024], F32)
    PY0 = nc.alloc_psum_tensor("py0", [128, 512], F32)
    PY1 = nc.alloc_psum_tensor("py1", [128, 512], F32)
    PSX = nc.alloc_psum_tensor("psx", [128, 512], F32)
    PST = nc.alloc_psum_tensor("pst", [128, 1024], BF16)
    PSF = [PA[:, 0:512], PA[:, 512:1024], PB[:, 0:512], PB[:, 512:1024],
           PY0[:, 0:512], PY1[:, 0:512]]

    em = Emit(nc)
    bufs = {}

    def B(name):
        if name not in bufs:
            bufs[name] = Buf(name)
        return bufs[name]

    bPSF = [B(f"psf{i}") for i in range(6)]
    bPST0 = B("pst")
    bPST1 = B("pst1")
    bPSTh = lambda sl: bPST0 if sl < 4 else bPST1
    bPSTfull = [bPST0, bPST1]
    bPSX = B("psx")

    def vo(fn, r=(), w=(), war=()):
        return em.op("vector", fn, r, w, war)

    def so(fn, r=(), w=(), war=()):
        return em.op("scalar", fn, r, w, war)

    em.dma("sync", convw_s[:, :], convw_d.ap(), writes=[B("convw")])
    em.dma("sync", convb_s[:, :], convb_d.ap(), writes=[B("convb")])
    em.dma("sync", dtb_s[:, :], dtb_d.ap(), writes=[B("dtb")])
    em.dma("sync", alog_s[:, :], alog_d.ap(), writes=[B("alog")])
    em.dma("sync", dvec_s[:, :], dvec_d.ap(), writes=[B("dvec")])
    em.dma("sync", normw_s[:, :], normw_d.ap(), writes=[B("normw")])
    em.dma("sync", flag_s[:, :], flagc.ap(), writes=[B("flag")])
    em.dma("sync", ident_bf_s[:, :], ident_bf_d.ap(), writes=[B("identb")])
    em.dma("sync", ident_f32_s[:, :], ident_f32_d.ap(), writes=[B("identf")])
    em.dma("sync", maskadd_s[:, :], maskadd_d.ap(), writes=[B("maskadd")])
    em.dma("sync", tri01_s[:, :], tri01_d.ap(), writes=[B("tri01")])
    em.dma("sync", onesr_s[:, :], onesr_d.ap(), writes=[B("onesr")])
    em.dma("sync", gsel_s[:, :], gsel_d.ap(), writes=[B("gsel")])
    em.dma("sync", E_s[:, :], E_d.ap(), writes=[B("E")])
    so(lambda e: e.activation(out=nA[:, :], in_=alog_s[:, :], func=AF.Exp),
       [B("alog")], [B("nA")])
    vo(lambda e: e.tensor_scalar_mul(nA[:, :], nA[:, :], -1.0), [B("nA")], [B("nA")])
    vo(lambda e: e.memset(state[:, :], 0.0), [], [B("state")])
    vo(lambda e: e.memset(epsc[:, :], EPS), [], [B("epsc")])
    vo(lambda e: e.memset(onec[:, :], 1.0), [], [B("onec")])

    bln1T = B("ln1T")
    blnbf = B("lnbf")
    lnbf = SCR[:, 0:4096]
    for tt in range(NTT):
        em.dma("sync", hbuf[:, :], hid.ap()[tt * 128:(tt + 1) * 128, :],
               writes=[B("hbuf")])
        for i in range(8):
            vo(lambda e, i=i: e.bn_stats(bnout[:, i * 6:(i + 1) * 6],
                                         hbuf[:, i * 512:(i + 1) * 512]),
               [B("hbuf")], [B("bnout")])
        vo(lambda e: e.bn_aggr(aggr[:, :], bnout[:, :]), [B("bnout")], [B("aggr")])
        so(lambda e: e.activation(out=cols[:, 0:1], in_=aggr[:, 1:2], func=AF.Ln,
                                  bias=epsc[:, 0:1]), [B("aggr"), B("epsc")], [B("c0")])
        so(lambda e: e.activation(out=cols[:, 1:2], in_=cols[:, 0:1], func=AF.Exp,
                                  scale=-0.5), [B("c0")], [B("c1")])
        vo(lambda e: e.scalar_tensor_tensor(cols[:, 2:3], aggr[:, 0:1], -1.0,
                                            cols[:, 1:2], ALU.mult, ALU.mult),
           [B("aggr"), B("c1")], [B("c2")])
        vo(lambda e: e.tensor_scalar(lnbf, hbuf[:, :], cols[:, 1:2], cols[:, 2:3],
                                     ALU.mult, ALU.add),
           [B("hbuf"), B("c1"), B("c2")], [blnbf])
        for k in range(NDT):
            sl = k % 8
            em.mm(lambda e, k=k, sl=sl: e.transpose(
                PST[:, sl * 128:(sl + 1) * 128], lnbf[:, k * 128:(k + 1) * 128],
                ident_bf_s[:, :]),
                [blnbf, B("identb")], [bPSTh(sl)], war=[bPSTh(sl)], last=True)
            vo(lambda e, k=k, tt=tt, sl=sl: e.tensor_copy(
                AR1[:, k * NTOK + tt * 128: k * NTOK + tt * 128 + 128],
                PST[:, sl * 128:(sl + 1) * 128]),
               [bPSTh(sl)], [bln1T])

    pc_v = [AR3[:, 0:1024], AR3[:, 0:1024]]
    bpc = [B("pc0"), B("pc0")]
    cv_v = [AR3[:, 1024:2048], AR3[:, 2048:3072]]
    bcv = [B("cv0"), B("cv1")]
    xstage_r = [SCR[:, 4096:5120], SCR[:, 5120:6144]]
    tstage_r = [SCR[:, 0:1024], SCR[:, 1024:2048]]
    gstage = SCR[:, 5120:6144]
    bxs_r = [B("xstage0"), B("xstage1")]
    bts_r = [B("tstage0"), B("tstage1")]
    bgs = B("gstage")

    def wslot(s):
        return wbuf[:, s * 2048:(s + 1) * 2048]

    bws = [B(f"ws{i}") for i in range(4)]

    def tr_job(fx):
        xst = xstage_r[fx % 2]
        bxst = bxs_r[fx % 2]
        tst = tstage_r[fx % 2]
        btst = bts_r[fx % 2]
        for ch in range(8):
            sl = ch % 8
            em.mm(lambda e, ch=ch, sl=sl, xst=xst: e.transpose(
                PST[:, sl * 128:(sl + 1) * 128],
                xst[:, ch * 128:(ch + 1) * 128], ident_bf_s[:, :]),
                [bxst, B("identb")], [bPSTh(sl)], war=[bPSTh(sl)])
            vo(lambda e, ch=ch, sl=sl, tst=tst: e.tensor_copy(
                tst[:, ch * 128:(ch + 1) * 128],
                PST[:, sl * 128:(sl + 1) * 128]),
               [bPSTh(sl)], [btst], war=[blnbf])
        if fx < 32:
            em.dma("sync",
                   xt_sp.ap().rearrange("c p f -> p c f")
                   [:, :, fx * 128:(fx + 1) * 128],
                   tst.rearrange("p (c f) -> p c f", c=8),
                   reads=[btst], writes=[B("xt_hbm")])
        else:
            em.dma("sync",
                   bt_sp.ap().rearrange("c p f -> p c f")
                   [:, :, (fx - 32) * 128:(fx - 31) * 128],
                   tst.rearrange("p (c f) -> p c f", c=8),
                   reads=[btst], writes=[B("bt_hbm")])

    pend_tr = []

    for f in range(NFT):
        fw = 64 if f == NFT - 1 else 128

        for half in range(2):
            s = (2 * f + half) % 4
            em.dma("gpsimd",
                   wslot(s).rearrange("p (k f) -> p k f", k=16)[:, :, 0:fw],
                   w_inT.ap().rearrange("(k p) f -> p k f", p=128)
                   [:, half * 16:(half + 1) * 16, f * 128:f * 128 + fw],
                   writes=[bws[s]])
        pse = (0, 1) if f % 2 == 0 else (2, 3)
        for k in range(NDT):
            s = (2 * f + k // 16) % 4
            for j in range(2):
                pb, bpb = PSF[pse[j]], bPSF[pse[j]]
                em.mm(lambda e, k=k, j=j, pb=pb, s=s, fw=fw: e.matmul(
                    pb[0:fw, :],
                    wslot(s)[:, (k % 16) * 128:(k % 16) * 128 + fw],
                    AR1[:, k * NTOK + j * 512: k * NTOK + (j + 1) * 512],
                    start=(k == 0), stop=(k == NDT - 1)),
                    [bws[s], bln1T],
                    [bpb] if k == NDT - 1 else [],
                    war=[bpb] if k == 0 else [],
                    last=(k == NDT - 1))
        if f == NFT - 1:
            for j in range(2):
                vo(lambda e, j=j, pse=pse: e.tensor_copy(
                    dtvals[:, j * 512:(j + 1) * 512], PSF[pse[j]][0:64, :]),
                   [bPSF[pse[j]]], [B("dtvals")])
            continue
        pcb, bpcb = pc_v[f % 2], bpc[f % 2]
        if f < 32:
            for j in range(2):
                so(lambda e, j=j, pcb=pcb, pse=pse: e.activation(
                    out=pcb[:, j * 512:(j + 1) * 512], in_=PSF[pse[j]],
                    func=AF.Silu), [bPSF[pse[j]]], [bpcb])
            gst = xstage_r[f % 2]
            bgst = bxs_r[f % 2]
            vo(lambda e, f=f, pcb=pcb, gst=gst: e.tensor_scalar_mul(
                gst, pcb, normw_s[:, f:f + 1]),
               [bpcb, B("normw")], [bgst])
            if debug and f == 0:
                em.dma("sync", dbg["d_g0"].ap(), pcb, reads=[bpcb])
            em.dma("sync", gate_sp.ap()[f, :, :], gst, reads=[bgst], writes=[B("gate_hbm")])
        else:
            fx = f - 32
            for j in range(2):
                vo(lambda e, j=j, pcb=pcb, pse=pse: e.tensor_copy(
                    pcb[:, j * 512:(j + 1) * 512], PSF[pse[j]]),
                   [bPSF[pse[j]]], [bpcb])
            vo(lambda e, fx=fx, pcb=pcb: e.tensor_copy(
                halo_send[:, fx * 3:fx * 3 + 3], pcb[:, NTOK - 3:NTOK]),
               [bpcb], [B("halo_send")])
            cv0, cv1 = cv_v[0], cv_v[1]
            b0, b1 = bcv[0], bcv[1]
            w, bc = convw_s, convb_s
            vo(lambda e, fx=fx, pcb=pcb, cv0=cv0: e.tensor_scalar(
                cv0, pcb, w[:, fx * 4 + 3:fx * 4 + 4], bc[:, fx:fx + 1],
                ALU.mult, ALU.add),
               [bpcb, B("convw"), B("convb")], [b0])
            vo(lambda e, fx=fx, pcb=pcb, cv0=cv0, cv1=cv1: e.scalar_tensor_tensor(
                cv1[:, 1:NTOK], pcb[:, 0:NTOK - 1], w[:, fx * 4 + 2:fx * 4 + 3],
                cv0[:, 1:NTOK], ALU.mult, ALU.add),
               [bpcb, b0, B("convw")], [b1])
            vo(lambda e, cv0=cv0, cv1=cv1: e.tensor_copy(cv1[:, 0:1], cv0[:, 0:1]),
               [b0], [b1])
            vo(lambda e, fx=fx, pcb=pcb, cv0=cv0, cv1=cv1: e.scalar_tensor_tensor(
                cv0[:, 2:NTOK], pcb[:, 0:NTOK - 2], w[:, fx * 4 + 1:fx * 4 + 2],
                cv1[:, 2:NTOK], ALU.mult, ALU.add),
               [bpcb, b1, B("convw")], [b0])
            vo(lambda e, cv0=cv0, cv1=cv1: e.tensor_copy(cv0[:, 0:2], cv1[:, 0:2]),
               [b1], [b0])
            vo(lambda e, fx=fx, pcb=pcb, cv0=cv0, cv1=cv1: e.scalar_tensor_tensor(
                cv1[:, 3:NTOK], pcb[:, 0:NTOK - 3], w[:, fx * 4 + 0:fx * 4 + 1],
                cv0[:, 3:NTOK], ALU.mult, ALU.add),
               [bpcb, b0, B("convw")], [b1])
            vo(lambda e, cv0=cv0, cv1=cv1: e.tensor_copy(cv1[:, 0:3], cv0[:, 0:3]),
               [b0], [b1])
            vo(lambda e, fx=fx, cv1=cv1: e.tensor_copy(
                presilu[:, fx * 3:fx * 3 + 3], cv1[:, 0:3]),
               [b1], [B("presilu")])
            xst = xstage_r[fx % 2]
            bxst = bxs_r[fx % 2]
            tst = tstage_r[fx % 2]
            btst = bts_r[fx % 2]
            so(lambda e, cv1=cv1, xst=xst: e.activation(out=xst, in_=cv1,
                                                        func=AF.Silu),
               [b1], [bxst])
            em.dma("sync", xbc_sp.ap()[fx, :, :], xst, reads=[bxst],
                   writes=[B("xbc_hbm")])


    em.dma("sync", cc_h_in.ap(), halo_send[:, :], reads=[B("halo_send")],
           writes=[B("cch_in")])
    em.coll([cc_h_in.ap().opt()], [cc_h_out.ap().opt()],
            reads=[B("cch_in")], writes=[B("cch_out")])
    em.dma("sync", halo_in[:, :], cc_h_out.ap()[0:128, :], reads=[B("cch_out")],
           writes=[B("halo_in")])
    vo(lambda e: e.tensor_scalar_mul(halo_in[:, :], halo_in[:, :], flag_s[:, 0:1]),
       [B("halo_in"), B("flag")], [B("halo_in")])
    hv = halo_in.ap().rearrange("p (f j) -> p f j", f=NXBC)
    pv = presilu.ap().rearrange("p (f j) -> p f j", f=NXBC)
    wv = convw_s.ap().rearrange("p (f k) -> p f k", f=NXBC)
    tmp3 = mrtmp.ap().rearrange("p (f j) -> p f j", j=1)[:, 0:48, :]
    for (t, k, j) in [(0, 0, 0), (0, 1, 1), (0, 2, 2), (1, 0, 1), (1, 1, 2),
                      (2, 0, 2)]:
        vo(lambda e, k=k, j=j: e.tensor_tensor(
            tmp3, hv[:, :, j:j + 1], wv[:, :, k:k + 1], ALU.mult),
           [B("halo_in"), B("convw")], [B("mr")])
        vo(lambda e, t=t: e.tensor_tensor(
            pv[:, :, t:t + 1], pv[:, :, t:t + 1], tmp3, ALU.add),
           [B("mr"), B("presilu")], [B("presilu")])
    so(lambda e: e.activation(out=fix3[:, :], in_=presilu[:, :], func=AF.Silu),
       [B("presilu")], [B("fix3")])

    dt1 = AR3[:, 0:1024]
    dt2 = AR3[:, 1024:2048]
    bd1, bd2 = B("dt1"), B("dt2")
    vo(lambda e: e.tensor_scalar_add(dt1[0:64, :], dtvals[:, :], dtb_s[:, 0:1]),
       [B("dtvals"), B("dtb")], [bd1], war=[bpc[0], bcv[0]])
    so(lambda e: e.activation(out=dt2[0:64, :], in_=dt1[0:64, :], func=AF.Abs),
       [bd1], [bd2], war=[bcv[0], bcv[1]])
    so(lambda e: e.activation(out=dt2[0:64, :], in_=dt2[0:64, :], func=AF.Exp,
                              scale=-1.0), [bd2], [bd2])
    so(lambda e: e.activation(out=dt2[0:64, :], in_=dt2[0:64, :], func=AF.Ln,
                              bias=onec[0:64, 0:1]), [bd2, B("onec")], [bd2])
    vo(lambda e: e.tensor_scalar_max(dtp[:, :], dt1[0:64, :], 0.0), [bd1], [B("dtp")])
    vo(lambda e: e.tensor_add(dtp[:, :], dtp[:, :], dt2[0:64, :]),
       [B("dtp"), bd2], [B("dtp")])
    vo(lambda e: e.tensor_scalar_mul(dt1[0:64, :], dtp[:, :], nA[:, 0:1]),
       [B("dtp"), B("nA")], [bd1])
    vo(lambda e: e.tensor_tensor_scan(gcum[:, :], dt1[0:64, :], dt1[0:64, :], 0.0,
                                      ALU.add, ALU.bypass),
       [bd1], [B("gcum")])
    if debug:
        em.dma("sync", dbg["d_dtp"].ap(), dtp[:, :], reads=[B("dtp")])
        em.dma("sync", dbg["d_gcum"].ap(), gcum[:, :], reads=[B("gcum")])

    xch = SCR[:, 0:6144]
    xt_v = SCR[:, 6144:10240]
    bt_v = SCR[:, 10240:11264]
    sm_v = SCR[:, 11264:12288]
    state_bf = hbuf.ap().bitcast(BF16)[:, 0:4096]
    gate_ch = SCR[:, 12288:16384]
    ux_v = SCR[:, 16384:16896]
    bxch, bxt, bbt, bsm = B("xch"), B("xt"), B("bt"), B("sm")
    bsbf, bgch = B("sbf"), B("gch")
    tmask8 = AR3[:, 0:1024]
    exp8 = BV3[:, 2048:3072]
    mt8 = BV3[:, 3072:4096]
    cxt8 = BV3[:, 4096:5120]
    btm, bte, bmt, bcx = B("tmask"), B("texp"), B("mt"), B("cxt")

    def chunk_decay(c, need_ecum):
        c0, c1 = c * 128, (c + 1) * 128
        vo(lambda e, c0=c0, c1=c1: e.tensor_scalar(
            ecwrk[:, :], gcum[:, c0:c1], -1.0, gcum[:, c1 - 1:c1],
            ALU.mult, ALU.add), [B("gcum")], [B("ecw")])
        so(lambda e: e.activation(out=ecwrk[:, :], in_=ecwrk[:, :], func=AF.Exp),
           [B("ecw")], [B("ecw")])
        vo(lambda e, c0=c0, c1=c1: e.tensor_mul(
            cwrow[:, :], dtp[:, c0:c1], ecwrk[:, :]),
           [B("dtp"), B("ecw")], [B("cwr")])
        em.mm(lambda e: e.transpose(PSX[:, 0:64], cwrow[0:64, :], ident_f32_s[0:64, 0:64]),
              [B("cwr"), B("identf")], [bPSX], war=[bPSX])
        vo(lambda e: e.tensor_copy(cw_tok[:, 0:64], PSX[:, 0:64]),
           [bPSX], [B("cwt")])
        if c == 0:
            vo(lambda e, c1=c1: e.tensor_copy(egl[:, 0:1], gcum[:, c1 - 1:c1]),
               [B("gcum")], [B("egl")])
        else:
            vo(lambda e, c0=c0, c1=c1: e.tensor_scalar(
                egl[:, 0:1], gcum[:, c0 - 1:c0], -1.0, gcum[:, c1 - 1:c1],
                ALU.mult, ALU.add), [B("gcum")], [B("egl")])
        em.dma("sync", arow[0:1, 0:64], egl[:, 0:1], reads=[B("egl")],
               writes=[B("arow")])
        so(lambda e: e.activation(out=arow[0:1, 64:128], in_=arow[0:1, 0:64],
                                  func=AF.Exp), [B("arow")], [B("arow")])
        vo(lambda e: e.tensor_copy(arow.ap().bitcast(BF16)[0:1, 384:448],
                                   arow[0:1, 64:128]), [B("arow")], [B("arowb")])
        em.mm(lambda e: e.matmul(PSX[:, 256:320], onesr_s[0:1, :],
                                 arow.ap().bitcast(BF16)[0:1, 384:448],
                                 start=True, stop=True),
              [B("arowb"), B("onesr")], [bPSX], war=[bPSX])
        vo(lambda e: e.tensor_copy(atot_cols[:, :], PSX[:, 256:320]),
           [bPSX], [B("atc")])
        if need_ecum:
            if c == 0:
                so(lambda e, c0=c0, c1=c1: e.activation(
                    out=ecum_bf[:, :], in_=gcum[:, c0:c1], func=AF.Exp),
                    [B("gcum")], [B("ecb_r")])
            else:
                vo(lambda e, c0=c0: e.tensor_scalar_mul(
                    egl[:, 1:2], gcum[:, c0 - 1:c0], -1.0), [B("gcum")], [B("egl2")])
                so(lambda e, c0=c0, c1=c1: e.activation(
                    out=ecum_bf[:, :], in_=gcum[:, c0:c1], func=AF.Exp,
                    bias=egl[:, 1:2]), [B("gcum"), B("egl2")], [B("ecb_r")])
            so(lambda e, c0=c0, c1=c1: e.activation(
                out=lntmp[:, :], in_=dtp[:, c0:c1], func=AF.Ln),
               [B("dtp")], [B("lntmp")])
            vo(lambda e, c0=c0, c1=c1: e.tensor_sub(
                zc[:, :], gcum[:, c0:c1], lntmp[:, :]),
               [B("gcum"), B("lntmp")], [B("zc")])

    def load_chunk(c, n_ft):
        c0 = c * 128
        em.dma("sync", xch.rearrange("p (f t) -> p f t", f=NXBC)[:, 0:n_ft, :],
               xbc_sp.ap().rearrange("f p t -> p f t")[:, 0:n_ft, c0:c0 + 128],
               reads=[B("xbc_hbm")], writes=[bxch])
        if c == 0:
            vo(lambda e, n_ft=n_ft: e.tensor_copy(
                xch.rearrange("p (f t) -> p f t", f=NXBC)[:, 0:n_ft, 0:3],
                fix3.ap().rearrange("p (f j) -> p f j", f=NXBC)[:, 0:n_ft, :]),
               [B("fix3")], [bxch])
        for i in range(40):
            sl = i % 8
            em.mm(lambda e, i=i, sl=sl: e.transpose(
                PST[:, sl * 128:(sl + 1) * 128], xch[:, i * 128:(i + 1) * 128],
                ident_bf_s[:, :]),
                [bxch, B("identb")], [bPSTh(sl)], war=[bPSTh(sl)])
            if i < 32:
                vo(lambda e, i=i, sl=sl: e.tensor_copy(
                    xt_v[:, i * 128:(i + 1) * 128], PST[:, sl * 128:(sl + 1) * 128]),
                   [bPSTh(sl)], [bxt], war=[B("dtvals")])
            else:
                vo(lambda e, i=i, sl=sl: e.tensor_copy(
                    bt_v[:, (i - 32) * 128:(i - 31) * 128],
                    PST[:, sl * 128:(sl + 1) * 128]),
                   [bPSTh(sl)], [bbt])

    bux = [B(f"ux{i}") for i in range(8)]

    def state_update():
        for pr in range(32):
            g = pr // 4
            for hh in range(2):
                h = pr * 2 + hh
                us = h % 8
                vo(lambda e, h=h, pr=pr, hh=hh, us=us: e.tensor_scalar_mul(
                    ux_v[:, us * 64:(us + 1) * 64],
                    xt_v[:, pr * 128 + hh * 64: pr * 128 + hh * 64 + 64],
                    cw_tok[:, h:h + 1]),
                   [bxt, B("cwt")], [bux[us]])
            bank = 4 + pr % 2
            pb, bpb = PSF[bank], bPSF[bank]
            u0 = (pr * 2) % 8
            em.mm(lambda e, pr=pr, g=g, pb=pb, u0=u0: e.matmul(
                pb[:, 0:128], bt_v[:, g * 128:(g + 1) * 128],
                ux_v[:, u0 * 64: u0 * 64 + 128], start=True, stop=True),
                [bbt, bux[u0], bux[u0 + 1]], [bpb], war=[bpb])
            for hh in range(2):
                h = pr * 2 + hh
                vo(lambda e, h=h, hh=hh, pb=pb: e.scalar_tensor_tensor(
                    state[:, h * 64:(h + 1) * 64], state[:, h * 64:(h + 1) * 64],
                    atot_cols[:, h:h + 1], pb[:, hh * 64:hh * 64 + 64],
                    ALU.mult, ALU.add),
                   [B("state"), B("atc"), bpb], [B("state")])

    uxw = wbuf[:, 0:512]
    cwbf = wbuf[:, 512:576]
    buxw, bcwbf = B("uxw"), B("cwbf")
    for c in range(NCH):
        c0, c1 = c * 128, (c + 1) * 128
        load_chunk(c, 40)
        em.dma("gpsimd", xt_sp.ap()[c, :, :], xt_v, reads=[bxt],
               writes=[B("xt_hbm")])
        em.dma("gpsimd", bt_sp.ap()[c, :, :], bt_v, reads=[bbt],
               writes=[B("bt_hbm")])
        vo(lambda e, c0=c0, c1=c1: e.tensor_scalar(
            ecwrk[:, :], gcum[:, c0:c1], -1.0, gcum[:, NTOK - 1:NTOK],
            ALU.mult, ALU.add), [B("gcum")], [B("ecw")])
        so(lambda e: e.activation(out=ecwrk[:, :], in_=ecwrk[:, :], func=AF.Exp),
           [B("ecw")], [B("ecw")])
        vo(lambda e, c0=c0, c1=c1: e.tensor_mul(
            cwrow[:, :], dtp[:, c0:c1], ecwrk[:, :]),
           [B("dtp"), B("ecw")], [B("cwr")])
        em.mm(lambda e: e.transpose(PSX[:, 0:64], cwrow[0:64, :],
                                    ident_f32_s[0:64, 0:64]),
              [B("cwr"), B("identf")], [bPSX], war=[bPSX])
        vo(lambda e: e.tensor_copy(cwbf, PSX[:, 0:64]), [bPSX], [bcwbf],
           war=[bws[0]])
        for q in range(8):
            em.op("gpsimd", lambda e, q=q: e.tensor_tensor(
                uxw.rearrange("p (a t) -> p a t", a=8),
                xt_v[:, q * 512:(q + 1) * 512].rearrange("p (a t) -> p a t", a=8),
                cwbf.rearrange("p (a t) -> p a t", t=1)[:, q * 8:(q + 1) * 8, :]
                .to_broadcast([128, 8, 64]), ALU.mult),
                [bxt, bcwbf], [buxw], war=[bws[0]])
            bank = PY0 if q % 2 == 0 else PY1
            bbank = bPSF[4] if q % 2 == 0 else bPSF[5]
            for pp2 in range(4):
                pr = q * 4 + pp2
                g = pr // 4
                em.mm(lambda e, pr=pr, g=g, pp2=pp2, bank=bank: e.matmul(
                    bank[:, pp2 * 128:(pp2 + 1) * 128],
                    bt_v[:, g * 128:(g + 1) * 128],
                    uxw[:, pp2 * 128:(pp2 + 1) * 128], start=True, stop=True),
                    [bbt, buxw], [bbank] if pp2 == 3 else [],
                    war=[bbank] if pp2 == 0 else [], last=(pp2 == 3), tick=True)
            vo(lambda e, q=q, bank=bank: e.tensor_tensor(
                state[:, q * 512:(q + 1) * 512], state[:, q * 512:(q + 1) * 512],
                bank[:, :], ALU.add),
               [B("state"), bbank], [B("state")])
    em.dma("sync", cc_s_in.ap(), state[:, :], reads=[B("state")],
           writes=[B("ccs_in")])
    em.coll([cc_s_in.ap().opt()], [cc_s_out.ap().opt()],
            reads=[B("ccs_in")], writes=[B("ccs_out")])
    if debug:
        em.dma("sync", dbg["d_statefin"].ap(), state[:, :], reads=[B("state")])
    em.dma("sync", state[:, :], cc_s_out.ap()[0:128, :],
           reads=[B("ccs_out")], writes=[B("state")])
    vo(lambda e: e.tensor_scalar_mul(state[:, :], state[:, :], flag_s[:, 0:1]),
       [B("state"), B("flag")], [B("state")])
    if debug:
        em.dma("sync", dbg["d_s0"].ap(), state[:, :], reads=[B("state")])

    bynT = B("ynT")
    bych = B("ych")
    PSTF = PST.ap().bitcast(F32)
    LH8 = hbuf[0:2, 2048:3072]
    RH8 = hbuf[0:2, 3072:4096]
    bLH8, bRH8 = B("LH8"), B("RH8")
    mtb = [wbuf[:, 1024:2048], wbuf[:, 2048:3072]]
    cxb = [wbuf[:, 3072:4096], wbuf[:, 4096:5120]]
    xd4 = wbuf.ap().bitcast(F32)[:, 2560:2816]
    bmtb = [B("mtb0"), B("mtb1")]
    bcxb = [B("cxb0"), B("cxb1")]
    bxd4 = B("xd4")
    vo(lambda e: e.memset(hbuf[0:2, 2048:3072], 1.0), [], [bLH8],
       war=[B("hbuf")])
    vo(lambda e: e.memset(hbuf[0:2, 3072:4096], -1.0), [], [bRH8],
       war=[B("hbuf")])
    for c in range(NCH):
        em.dma("gpsimd", xch.rearrange("p (f t) -> p f t", f=NXBC),
               xbc_sp.ap().rearrange("f p t -> p f t")[:, :, c * 128:(c + 1) * 128],
               reads=[B("xbc_hbm")], writes=[bxch])
        if c == 0:
            vo(lambda e: e.tensor_copy(
                xch.rearrange("p (f t) -> p f t", f=NXBC)[:, :, 0:3],
                fix3.ap().rearrange("p (f j) -> p f j", f=NXBC)),
               [B("fix3")], [bxch])
        em.dma("gpsimd", xt_v, xt_sp.ap()[c, :, :], reads=[B("xt_hbm")],
               writes=[bxt])
        em.dma("gpsimd", bt_v, bt_sp.ap()[c, :, :], reads=[B("bt_hbm")],
               writes=[bbt])
        chunk_decay(c, need_ecum=True)
        vo(lambda e: e.tensor_copy(cwbf, cw_tok[:, 0:64]), [B("cwt")], [bcwbf],
           war=[bws[0]])
        em.dma("gpsimd", gate_ch.rearrange("p (i t) -> p i t", i=32),
               gate_sp.ap().rearrange("i p t -> p i t")[:, :, c * 128:(c + 1) * 128],
               reads=[B("gate_hbm")], writes=[bgch])
        vo(lambda e: e.tensor_copy(state_bf, state[:, :]), [B("state")], [bsbf],
           war=[B("hbuf")])
        for r2 in range(2):
            for gg in range(4):
                g = r2 * 4 + gg
                em.mm(lambda e, g=g, gg=gg: e.matmul(
                    PSX[:, gg * 128:(gg + 1) * 128],
                    xch[:, (32 + g) * 128:(33 + g) * 128],
                    xch[:, (40 + g) * 128:(41 + g) * 128], start=True, stop=True),
                    [bxch], [bPSX], war=[bPSX] if gg == 0 else [],
                    last=(gg == 3), tick=False)
            vo(lambda e, r2=r2: e.tensor_tensor(
                sm_v.rearrange("p (a t) -> p a t", a=8)[:, r2 * 4:(r2 + 1) * 4, :],
                PSX.ap().rearrange("p (a t) -> p a t", a=4),
                tri01_s.ap().rearrange("p (a t) -> p a t", a=1)
                .to_broadcast([128, 4, 128]), ALU.mult),
               [bPSX, B("tri01")], [bsm])
        c0 = c * 128

        def fS1(g, c0=c0):
            pp = PA if g % 2 == 0 else PB
            bpp = [bPSF[0], bPSF[1]] if g % 2 == 0 else [bPSF[2], bPSF[3]]
            em.dma("sync", LH8[0:1, :], zc[g * 8: g * 8 + 8, :],
                   reads=[B("zc")], writes=[bLH8])
            em.dma("sync", RH8[1:2, :], gcum[g * 8: g * 8 + 8, c0:c0 + 128],
                   reads=[B("gcum")], writes=[bRH8])
            for hs in range(8):
                em.mm(lambda e, hs=hs, pp=pp: e.matmul(
                    pp[:, hs * 128:(hs + 1) * 128],
                    LH8[:, hs * 128:(hs + 1) * 128],
                    RH8[:, hs * 128:(hs + 1) * 128],
                    start=True, stop=True),
                    [bLH8, bRH8],
                    bpp if hs == 7 else [],
                    war=bpp if hs == 0 else [],
                    last=(hs == 7), tick=(hs == 3))

        def fS2(g):
            pp = PA if g % 2 == 0 else PB
            bpp = [bPSF[0], bPSF[1]] if g % 2 == 0 else [bPSF[2], bPSF[3]]
            par = g % 2
            vo(lambda e, pp=pp: e.scalar_tensor_tensor(
                tmask8.rearrange("p (a t) -> p a t", a=8),
                pp.ap().rearrange("p (a t) -> p a t", a=8), 1.0,
                maskadd_s.ap().rearrange("p (a t) -> p a t", a=1)
                .to_broadcast([128, 8, 128]), ALU.mult, ALU.add),
               bpp + [B("maskadd")], [btm], war=[bd1])
            so(lambda e: e.activation(out=exp8, in_=tmask8, func=AF.Exp),
               [btm], [bte], war=[bd2])
            em.dma("sync", ecfl[0:1, :], ecum_bf[g * 8:g * 8 + 8, :],
                   reads=[B("ecb_r")], writes=[B("ecfl")])
            vo(lambda e, g=g, par=par: e.tensor_tensor(
                mtb[par].rearrange("p (a t) -> p a t", a=8),
                exp8.rearrange("p (a t) -> p a t", a=8),
                sm_v[:, g * 128:(g + 1) * 128]
                .rearrange("p (a t) -> p a t", a=1).to_broadcast([128, 8, 128]),
                ALU.mult),
               [bte, bsm], [bmtb[par]], war=[bws[0], bws[1]])
            em.mm(lambda e: e.matmul(PSX[:, 0:512], onesr_s[0:1, :],
                                     ecfl[0:1, 0:512], start=True, stop=True),
                  [B("ecfl"), B("onesr")], [bPSX], war=[bPSX])
            em.mm(lambda e: e.matmul(PSTF[:, 0:512], onesr_s[0:1, :],
                                     ecfl[0:1, 512:1024], start=True, stop=True),
                  [B("ecfl"), B("onesr")], bPSTfull, war=bPSTfull)
            so(lambda e: e.activation(out=exp8[:, 0:512], in_=PSX[:, 0:512],
                                      func=AF.Copy),
               [bPSX], [bte])
            so(lambda e: e.activation(out=exp8[:, 512:1024], in_=PSTF[:, 0:512],
                                      func=AF.Copy),
               bPSTfull, [bte])
            vo(lambda e, g=g, par=par: e.tensor_tensor(
                cxb[par].rearrange("p (a t) -> p a t", a=8),
                exp8.rearrange("p (a t) -> p a t", a=8),
                xch[:, (40 + g) * 128:(41 + g) * 128]
                .rearrange("p (a t) -> p a t", a=1).to_broadcast([128, 8, 128]),
                ALU.mult),
               [bte, bxch], [bcxb[par]], war=[bws[1], bws[2], bws[3]])

        def fS3(g):
            par = g % 2
            for hh8 in range(8):
                h = g * 8 + hh8
                pr, hh = h // 2, h % 2
                q4 = (pr // 2) % 2
                sl2 = pr % 2
                pyp = PY0 if q4 == 0 else PY1
                bpy = bPSF[4] if q4 == 0 else bPSF[5]
                em.mm(lambda e, pr=pr, hh=hh, hh8=hh8, pyp=pyp, sl2=sl2, par=par: e.matmul(
                    pyp[hh * 64:hh * 64 + 64, sl2 * 128:(sl2 + 1) * 128],
                    xt_v[:, pr * 128 + hh * 64: pr * 128 + hh * 64 + 64],
                    mtb[par][:, hh8 * 128:(hh8 + 1) * 128],
                    start=True, stop=False, tile_position=(0, hh * 64)),
                    [bxt, bmtb[par]], [],
                    war=([bpy] if (hh == 0 and sl2 == 0) else []), last=False)
                em.mm(lambda e, h=h, hh=hh, hh8=hh8, pyp=pyp, sl2=sl2, par=par: e.matmul(
                    pyp[hh * 64:hh * 64 + 64, sl2 * 128:(sl2 + 1) * 128],
                    state_bf[:, h * 64:(h + 1) * 64],
                    cxb[par][:, hh8 * 128:(hh8 + 1) * 128],
                    start=False, stop=True, tile_position=(0, hh * 64)),
                    [bsbf, bcxb[par]],
                    [bpy] if (hh == 1 and sl2 == 1) else [],
                    last=(hh == 1 and sl2 == 1), tick=(hh == 1))
                if hh == 1 and sl2 == 1:
                    pr0 = pr - 1
                    em.op("gpsimd", lambda e, pr0=pr0: e.tensor_tensor(
                        xd4.rearrange("p (a t) -> p a t", a=2),
                        xch[:, pr0 * 128:(pr0 + 2) * 128]
                        .rearrange("p (a t) -> p a t", a=2),
                        dvec_s.ap().rearrange("p (a t) -> p a t", t=1)
                        [:, pr0:pr0 + 2, :].to_broadcast([128, 2, 128]), ALU.mult),
                        [bxch, B("dvec")], [bxd4], war=[bws[2]])
                    vo(lambda e, pr0=pr0, pyp=pyp: e.tensor_tensor(
                        ych[:, pr0 * 128:(pr0 + 2) * 128], xd4,
                        pyp[:, 0:256], ALU.add),
                       [bxd4, bpy], [bych], war=[B("dtvals")])

        for i in range(10):
            if i < 8:
                fS1(i)
            if 1 <= i < 9:
                fS2(i - 1)
            if i >= 2:
                fS3(i - 2)
        vo(lambda e: e.tensor_tensor(
            state.ap().rearrange("p (a t) -> p a t", a=64),
            state.ap().rearrange("p (a t) -> p a t", a=64),
            atot_cols.ap().rearrange("p (a t) -> p a t", t=1)
            .to_broadcast([128, 64, 64]), ALU.mult),
           [B("state"), B("atc")], [B("state")])
        for q in range(8):
            em.op("gpsimd", lambda e, q=q: e.tensor_tensor(
                uxw.rearrange("p (a t) -> p a t", a=8),
                xt_v[:, q * 512:(q + 1) * 512].rearrange("p (a t) -> p a t", a=8),
                cwbf.rearrange("p (a t) -> p a t", t=1)[:, q * 8:(q + 1) * 8, :]
                .to_broadcast([128, 8, 64]), ALU.mult),
                [bxt, bcwbf], [buxw], war=[bws[0]])
            bank = PY0 if q % 2 == 0 else PY1
            bbank = bPSF[4] if q % 2 == 0 else bPSF[5]
            for pp2 in range(4):
                pr = q * 4 + pp2
                g = pr // 4
                em.mm(lambda e, pr=pr, g=g, pp2=pp2, bank=bank: e.matmul(
                    bank[:, pp2 * 128:(pp2 + 1) * 128],
                    bt_v[:, g * 128:(g + 1) * 128],
                    uxw[:, pp2 * 128:(pp2 + 1) * 128], start=True, stop=True),
                    [bbt, buxw], [bbank] if pp2 == 3 else [],
                    war=[bbank] if pp2 == 0 else [], last=(pp2 == 3), tick=True)
            vo(lambda e, q=q, bank=bank: e.tensor_tensor(
                state[:, q * 512:(q + 1) * 512], state[:, q * 512:(q + 1) * 512],
                bank[:, :], ALU.add),
               [B("state"), bbank], [B("state")])
        if debug:
            em.dma("sync", dbg["d_y"].ap()[c, :, :], ych[:, :], reads=[bych])
        for i in range(32):
            ysl = i % 2
            em.op("gpsimd", lambda e, i=i, ysl=ysl: e.tensor_tensor(
                y2b[:, ysl * 128:(ysl + 1) * 128], ych[:, i * 128:(i + 1) * 128],
                ych[:, i * 128:(i + 1) * 128], ALU.mult),
                [bych], [B(f"y2{ysl}")])
            em.mm(lambda e, i=i, ysl=ysl: e.matmul(
                PSX[0:8, 0:128], gsel_s[:, i * 8:(i + 1) * 8],
                y2b[:, ysl * 128:(ysl + 1) * 128],
                start=(i == 0), stop=(i == 31)),
                [B(f"y2{ysl}"), B("gsel")],
                [bPSX] if i == 31 else [],
                war=[bPSX] if i == 0 else [], last=(i == 31), tick=True)
        vo(lambda e: e.tensor_copy(colsg[0:8, 0:128], PSX[0:8, 0:128]),
           [bPSX], [B("varg")])
        so(lambda e: e.activation(out=colsg[0:8, 128:256], in_=colsg[0:8, 0:128],
                                  func=AF.Ln, bias=epsc[0:8, 0:1], scale=1.0 / 512.0),
           [B("varg"), B("epsc")], [B("varg")])
        so(lambda e: e.activation(out=colsg[0:8, 0:128], in_=colsg[0:8, 128:256],
                                  func=AF.Exp, scale=-0.5),
           [B("varg")], [B("varg")])
        vo(lambda e: e.tensor_copy(rstd_bfr[0:8, :], colsg[0:8, 0:128]),
           [B("varg")], [B("rstdb")])
        em.dma("sync", rfl[0:1, :], rstd_bfr[0:8, :], reads=[B("rstdb")],
               writes=[B("rfl")])
        for g in range(8):
            em.mm(lambda e, g=g: e.matmul(
                PSX[:, 0:128], onesr_s[0:1, :], rfl[0:1, g * 128:(g + 1) * 128],
                start=True, stop=True),
                [B("rfl"), B("onesr")], [bPSX], war=[bPSX])
            vo(lambda e, g=g: e.tensor_tensor(
                xd4.rearrange("p (a t) -> p a t", a=2)
                if False else uxw.rearrange("p (a t) -> p a t", a=4)[:, 0:4, :],
                ych[:, g * 512:(g + 1) * 512].rearrange("p (a t) -> p a t", a=4),
                PSX[:, 0:128].rearrange("p (a t) -> p a t", a=1)
                .to_broadcast([128, 4, 128]), ALU.mult),
               [bych, bPSX], [buxw], war=[B("cwt2")])
            vo(lambda e, g=g, c=c: e.tensor_tensor(
                AR1.ap().rearrange("p (i t) -> p i t", i=NDT)
                [:, g * 4:(g + 1) * 4, c * 128:(c + 1) * 128],
                uxw.rearrange("p (a t) -> p a t", a=4),
                gate_ch.rearrange("p (a t) -> p a t", a=32)[:, g * 4:(g + 1) * 4, :],
                ALU.mult),
               [buxw, bgch], [bynT])
        if debug:
            em.dma("gpsimd",
                   dbg["d_yn"].ap()[c, :, :].rearrange("p (i t) -> p i t", i=NDT),
                   AR1.ap().rearrange("p (i t) -> p i t", i=NDT)
                   [:, :, c * 128:(c + 1) * 128],
                   reads=[bynT])

    bx2 = [B(f"x2_{tt}") for tt in range(NTT)]
    bwo = [B(f"wo{i}") for i in range(4)]

    def gpsum(tt):
        if tt < 6:
            return PSF[tt], [bPSF[tt]]
        if tt == 6:
            return PST.ap().bitcast(F32)[:, 0:512], bPSTfull
        return PSX[:, 0:512], [bPSX]

    for dblk in range(8):
        for i in range(NDT):
            s = (dblk * NDT + i) % 4
            em.dma("gpsimd", wbuf[:, s * 512:(s + 1) * 512],
                   w_outT.ap()[i * 128:(i + 1) * 128, dblk * 512:(dblk + 1) * 512],
                   writes=[bwo[s]],
                   war=((bws + [B("uxw"), B("cwbf"), B("mtb0")]) if (dblk == 0 and i < 4) else []))
            for tt in range(NTT):
                pap, bpb = gpsum(tt)
                em.mm(lambda e, i=i, tt=tt, s=s, pap=pap: e.matmul(
                    pap, AR1[:, i * NTOK + tt * 128: i * NTOK + tt * 128 + 128],
                    wbuf[:, s * 512:(s + 1) * 512],
                    start=(i == 0), stop=(i == NDT - 1)),
                    [bynT, bwo[s]],
                    bpb if i == NDT - 1 else [],
                    war=bpb if i == 0 else [], last=(i == NDT - 1),
                    tick=(tt == NTT - 1))
        em.dma("sync", hbuf.ap().rearrange("p (a d) -> p a d", a=8),
               hid.ap().rearrange("(a p) d -> p a d", p=128)
               [:, :, dblk * 512:(dblk + 1) * 512],
               writes=[B("hbuf")], war=[bsbf, bLH8, bRH8])
        for tt in range(NTT):
            pap, bpb = gpsum(tt)
            if debug and dblk == 0 and tt == 0:
                vo(lambda e, pap=pap: e.tensor_copy(mrtmp_dump[:, :], pap), bpb,
                   [B("opd")])
                em.dma("sync", dbg["d_op"].ap(), mrtmp_dump[:, :], reads=[B("opd")])
            vo(lambda e, tt=tt, pap=pap: e.tensor_tensor(
                ych[:, tt * 512:(tt + 1) * 512], pap,
                hbuf[:, tt * 512:(tt + 1) * 512], ALU.add),
               bpb + [B("hbuf")], [bych])
            vo(lambda e, tt=tt, dblk=dblk: e.bn_stats(
                bng[:, (tt * 8 + dblk) * 6:(tt * 8 + dblk) * 6 + 6],
                ych[:, tt * 512:(tt + 1) * 512]),
               [bych], [B("bng")])
            em.dma("sync", x2_sp.ap()[tt, :, dblk * 512:(dblk + 1) * 512],
                   ych[:, tt * 512:(tt + 1) * 512], reads=[bych], writes=[bx2[tt]])

    for half in range(2):
        em.dma("gpsimd",
               AR1.ap().rearrange("p (k t) -> p k t", k=8)
               [:, half * 4:(half + 1) * 4, :],
               b_flat.ap().rearrange("(k p) d -> p k d", p=128)
               [:, half * 4:(half + 1) * 4, :],
               writes=[B("bflat")], war=([bynT, bln1T] if half == 0 else []))
    bbf = B("bflat")
    wr_all = SCR[:, 4096:6144]
    bwr = B("wr_all")
    em.dma("gpsimd", wr_all.rearrange("p (k f) -> p k f", k=NDT),
           wr_Td.ap().rearrange("(k p) f -> p k f", p=128),
           writes=[bwr], war=[bxs_r[0], bxs_r[1], bgs, bxch])
    gt_all = SCR[:, 6144:7168]
    gexp_all = SCR[:, 8192:16384]
    bgt = B("gt_all")
    ln2st = BV3[:, 0:4096]
    bln2st = B("ln2st")
    for tt in range(NTT):
        vo(lambda e, tt=tt: e.bn_aggr(aggr[:, :], bng[:, tt * 48:(tt + 1) * 48]),
           [B("bng")], [B("aggr")])
        so(lambda e: e.activation(out=cols[:, 0:1], in_=aggr[:, 1:2], func=AF.Ln,
                                  bias=epsc[:, 0:1]), [B("aggr"), B("epsc")], [B("c0")])
        so(lambda e: e.activation(out=cols[:, 1:2], in_=cols[:, 0:1], func=AF.Exp,
                                  scale=-0.5), [B("c0")], [B("c1")])
        vo(lambda e: e.scalar_tensor_tensor(cols[:, 2:3], aggr[:, 0:1], -1.0,
                                            cols[:, 1:2], ALU.mult, ALU.mult),
           [B("aggr"), B("c1")], [B("c2")])
        em.dma("sync", hbuf[:, :], x2_sp.ap()[tt, :, :],
               reads=[bx2[tt]], writes=[B("hbuf")])
        vo(lambda e: e.tensor_scalar(lnbf, hbuf[:, :], cols[:, 1:2], cols[:, 2:3],
                                     ALU.mult, ALU.add),
           [B("hbuf"), B("c1"), B("c2")], [blnbf], war=[bxch, bts_r[0], bts_r[1]])
        for k in range(NDT):
            sl = k % 8
            em.mm(lambda e, k=k, sl=sl: e.transpose(
                PST[:, sl * 128:(sl + 1) * 128], lnbf[:, k * 128:(k + 1) * 128],
                ident_bf_s[:, :]),
                [blnbf, B("identb")], [bPSTh(sl)], war=[bPSTh(sl)])
            vo(lambda e, k=k, sl=sl: e.tensor_copy(
                ln2st[:, k * 128:(k + 1) * 128], PST[:, sl * 128:(sl + 1) * 128]),
               [bPSTh(sl)], [bln2st],
               war=([btm, bte, bmt, bcx, bd1, bd2, bpc[0]] if tt == 0 and k == 0
                    else []))
        em.dma("sync", ln2_sp.ap().rearrange("k p t -> p k t")[:, :, tt * 128:(tt + 1) * 128],
               ln2st.rearrange("p (k t) -> p k t", k=NDT),
               reads=[bln2st], writes=[B("ln2_hbm")])
        for k in range(NDT):
            em.mm(lambda e, k=k: e.matmul(
                PSX[:, 0:64], ln2st[:, k * 128:(k + 1) * 128],
                wr_all[:, k * 64:(k + 1) * 64], start=(k == 0), stop=(k == NDT - 1)),
                [bln2st, bwr],
                [bPSX] if k == NDT - 1 else [],
                war=[bPSX] if k == 0 else [], last=(k == NDT - 1))
        vo(lambda e: e.tensor_copy(scores_b[:, :], PSX[:, 0:64]),
           [bPSX], [B("scb")])
        if debug:
            em.dma("sync", dbg["d_scores"].ap()[tt, :, :], scores_b[:, :],
                   reads=[B("scb")])
        vo(lambda e: e.max(out=m8a[:, :], in_=scores_b[:, :]), [B("scb")], [B("m8a")])
        vo(lambda e: e.match_replace(out=mrtmp[:, :], in_to_replace=m8a[:, :],
                                     in_values=scores_b[:, :], imm_value=-1e30),
           [B("scb"), B("m8a")], [B("mr")])
        vo(lambda e: e.max(out=m8b[:, :], in_=mrtmp[:, :]), [B("mr")], [B("m8b")])
        vo(lambda e: e.tensor_scalar_mul(cols[:, 3:4], m8a[:, 0:1], -1.0),
           [B("m8a")], [B("c3")])
        so(lambda e: e.activation(out=gbuf[:, :], in_=scores_b[:, :], func=AF.Exp,
                                  bias=cols[:, 3:4]), [B("scb"), B("c3")], [B("gb")])
        vo(lambda e: e.tensor_scalar(mrtmp[:, :], scores_b[:, :], m8b[:, 7:8], None,
                                     ALU.is_ge), [B("scb"), B("m8b")], [B("mr")])
        vo(lambda e: e.tensor_mul(gbuf[:, :], gbuf[:, :], mrtmp[:, :]),
           [B("gb"), B("mr")], [B("gb")])
        vo(lambda e: e.reduce_sum(cols[:, 4:5], gbuf[:, :], axis=AX),
           [B("gb")], [B("c4")])
        vo(lambda e: e.reciprocal(cols[:, 5:6], cols[:, 4:5]), [B("c4")], [B("c5")])
        vo(lambda e: e.tensor_scalar_mul(gbuf[:, :], gbuf[:, :], cols[:, 5:6]),
           [B("gb"), B("c5")], [B("gb")])
        if debug:
            em.dma("sync", dbg["d_gates"].ap()[tt, :, :], gbuf[:, :],
                   reads=[B("gb")])
        em.mm(lambda e: e.transpose(PSX[0:64, 128:256], gbuf[:, 0:64],
                                    ident_f32_s[:, :]),
              [B("gb"), B("identf")], [bPSX], war=[bPSX])
        vo(lambda e, tt=tt: e.tensor_copy(gt_all[0:64, tt * 128:(tt + 1) * 128],
                                          PSX[0:64, 128:256]),
           [bPSX], [bgt], war=([bxt] if tt == 0 else []))
        for kr in range(8):
            em.mm(lambda e, kr=kr, tt=tt: e.matmul(
                PSX[:, 0:128], E_s[0:64, kr * 128:(kr + 1) * 128],
                gt_all[0:64, tt * 128:(tt + 1) * 128], start=True, stop=True),
                [B("E"), bgt], [bPSX], war=[bPSX])
            vo(lambda e, kr=kr, tt=tt: e.tensor_copy(
                gexp_all[:, kr * NTOK + tt * 128: kr * NTOK + tt * 128 + 128],
                PSX[:, 0:128]),
               [bPSX], [B("gexp")],
               war=([bsbf, bgch] if (tt == 0 and kr == 0) else []))

    hg_v = SCR[:, 0:8192]
    bhg = B("hg")
    bah = [B("ah0"), B("ah1")]
    aring = [wbuf[:, i * 1024:(i + 1) * 1024] for i in range(4)]
    baring = [B(f"ar{i}") for i in range(4)]
    ln2r = [AR3.ap().bitcast(BF16)[:, i * 512:(i + 1) * 512] for i in range(4)]
    bln2r = [B(f"ln2r{i}") for i in range(4)]
    HB = [(PA[:, 0:512], [bPSF[0]]), (PA[:, 512:1024], [bPSF[1]]),
          (PB[:, 0:512], [bPSF[2]]), (PB[:, 512:1024], [bPSF[3]]),
          (PY0[:, 0:512], [bPSF[4]]), (PY1[:, 0:512], [bPSF[5]]),
          (PSX[:, 0:512], [bPSX]), (PSTF[:, 0:512], bPSTfull)]
    for half in range(2):
        for k in range(NDT):
            rs = (half * NDT + k) % 4
            em.dma("sync", ln2r[rs],
                   ln2_sp.ap()[k, :, half * 512:(half + 1) * 512],
                   reads=[B("ln2_hbm")], writes=[bln2r[rs]],
                   war=([bd1, bd2, btm, bte, bmt, bcx, bln2st] if
                        (half == 0 and k < 4) else []))
            em.dma("gpsimd", aring[rs],
                   a_Td.ap()[k * 128:(k + 1) * 128, :],
                   writes=[baring[rs]],
                   war=((bwo + [B("uxw"), B("cwbf"), B("mtb0"), B("mtb1"),
                                B("cxb0"), B("cxb1")]) if (half == 0 and k < 4)
                        else []))
            for kr in range(8):
                pap, bpb = HB[kr]
                em.mm(lambda e, k=k, kr=kr, rs=rs, pap=pap: e.matmul(
                    pap, aring[rs][:, kr * 128:(kr + 1) * 128],
                    ln2r[rs], start=(k == 0), stop=(k == NDT - 1)),
                    [baring[rs], bln2r[rs]],
                    bpb if k == NDT - 1 else [],
                    war=bpb if k == 0 else [],
                    last=(k == NDT - 1), tick=(kr == 7))
        for kr in range(8):
            pap, bpb = HB[kr]
            vo(lambda e, kr=kr, half=half, pap=pap: e.tensor_tensor(
                hg_v[:, kr * NTOK + half * 512: kr * NTOK + (half + 1) * 512],
                pap, gexp_all[:, kr * NTOK + half * 512: kr * NTOK + (half + 1) * 512],
                ALU.mult),
               bpb + [B("gexp")], [bhg],
               war=([B("uxw"), B("cwbf")] if (half == 0 and kr == 0) else []))
    if debug:
        em.dma("gpsimd", dbg["d_hg"].ap(), hg_v, reads=[bhg])
    for tt in range(NTT):
        for kr in range(8):
            for ds in range(8):
                pap, bpb = gpsum(ds)
                em.mm(lambda e, kr=kr, tt=tt, ds=ds, pap=pap: e.matmul(
                    pap, hg_v[:, kr * NTOK + tt * 128: kr * NTOK + tt * 128 + 128],
                    AR1[:, kr * D_ + ds * 512: kr * D_ + (ds + 1) * 512],
                    start=(kr == 0), stop=(kr == 7)),
                    [bhg, bbf],
                    bpb if kr == 7 else [],
                    war=bpb if kr == 0 else [], last=(kr == 7))
        em.dma("sync", hbuf[:, :], x2_sp.ap()[tt, :, :],
               reads=[bx2[tt]], writes=[B("hbuf")])
        for ds in range(8):
            pap, bpb = gpsum(ds)
            vo(lambda e, ds=ds, pap=pap: e.scalar_tensor_tensor(
                ych[:, ds * 512:(ds + 1) * 512], pap, 2.0,
                hbuf[:, ds * 512:(ds + 1) * 512], ALU.mult, ALU.add),
               bpb + [B("hbuf")], [bych])
        em.dma("sync", out_d.ap()[tt * 128:(tt + 1) * 128, :], ych[:, :],
               reads=[bych], writes=[B("outd")])

    em.wait_all("sync", [B("outd")])
    em.ops["sync"].append(
        ([(k, em.cnt[k]) for k in em.cnt if k[:2] in ("qd", "pd") and em.cnt[k]],
         None, None, 0))
    em.finalize()
    return nc, dbg


_CACHE = {}


def _get_nc(debug=False):
    key = bool(debug)
    if key not in _CACHE:
        _CACHE[key] = build(debug)
    return _CACHE[key]


def kernel(**inputs):
    hidf = np.ascontiguousarray(np.asarray(inputs["L0_attn_hidden_states"],
                                           dtype=np.float32))
    w_in = np.asarray(inputs["L0_attn_in_proj_w"], dtype=np.float32)
    convw = np.asarray(inputs["L0_attn_conv_weight"], dtype=np.float32)
    convb = np.asarray(inputs["L0_attn_conv_bias"], dtype=np.float32)
    dtb = np.asarray(inputs["L0_attn_dt_bias"], dtype=np.float32)
    alog = np.asarray(inputs["L0_attn_A_log"], dtype=np.float32)
    Dv = np.asarray(inputs["L0_attn_D"], dtype=np.float32)
    normw = np.asarray(inputs["L0_attn_norm_weight"], dtype=np.float32)
    w_out = np.asarray(inputs["L0_attn_out_proj_w"], dtype=np.float32)
    A = np.asarray(inputs["L0_ffn_A"], dtype=np.float32)
    Bf = np.asarray(inputs["L0_ffn_B"], dtype=np.float32)
    Wr = np.asarray(inputs["L0_ffn_Wr"], dtype=np.float32)

    shared = {
        "w_inT": np.ascontiguousarray(w_in.T),
        "w_outT": np.ascontiguousarray(w_out.T),
        "a_T": np.ascontiguousarray(A.reshape(1024, D_).T),
        "b_flat": np.ascontiguousarray(Bf.transpose(0, 2, 1).reshape(1024, D_)),
        "wr_T": np.ascontiguousarray(Wr.T),
        "convw": np.ascontiguousarray(
            convw.reshape(NXBC, 128, 4).transpose(1, 0, 2).reshape(128, NXBC * 4)),
        "convb": np.ascontiguousarray(convb.reshape(NXBC, 128).T),
        "dtb": dtb.reshape(64, 1).copy(),
        "alog": alog.reshape(64, 1).copy(),
        "dvec": np.ascontiguousarray(np.repeat(Dv, P_).reshape(32, 128).T),
        "normw": np.ascontiguousarray(normw.reshape(32, 128).T),
    }
    nc, _ = _get_nc(False)
    in_maps = []
    for c in range(8):
        b, th = c // 2, c % 2
        m = dict(shared)
        m["hid"] = np.ascontiguousarray(hidf[b, th * NTOK:(th + 1) * NTOK, :])
        m["flagc"] = np.full((128, 1), float(th), dtype=np.float32)
        in_maps.append(m)
    r = run_bass_kernel_spmd(nc, in_maps, core_ids=list(range(8)))
    out = np.empty((B_, L_, D_), dtype=np.float32)
    for c in range(8):
        b, th = c // 2, c % 2
        out[b, th * NTOK:(th + 1) * NTOK, :] = np.asarray(r.results[c]["out"])
    return out


# version 14
# speedup vs baseline: 1.0373x; 1.0331x over previous
import numpy as np
import ml_dtypes
import concourse.bass as bass
import concourse.mybir as mybir
from concourse.bass_utils import run_bass_kernel_spmd

F32 = mybir.dt.float32
BF16 = mybir.dt.bfloat16
AF = mybir.ActivationFunctionType
ALU = mybir.AluOpType
AX = mybir.AxisListType.X
bfdt = ml_dtypes.bfloat16

B_, L_, D_ = 4, 2048, 4096
NTOK = 1024
NTT = 8
NDT = 32
PROJ = 10304
NFT = 81
NXBC = 48
NCH = 8
H_, P_, N_, G_ = 64, 64, 128, 8
EPS = 1e-5
GROUPS = [[0, 1], [2, 3], [4, 5], [6, 7]]


class Buf:
    __slots__ = ("name", "last_w", "readers")

    def __init__(self, name):
        self.name = name
        self.last_w = None
        self.readers = []


class Emit:
    ENGINES = ("tensor", "vector", "scalar", "gpsimd", "sync")

    def __init__(self, nc):
        self.nc = nc
        self.ops = {e: [] for e in self.ENGINES}
        self.cnt = {}
        self.syncd = {e: {} for e in self.ENGINES}
        self.sems = {}
        keys = ["t", "v", "s", "p"]
        keys += [f"qd{i}" for i in range(8)] + [f"pd{i}" for i in range(8)]
        for k in keys:
            self.sems[k] = nc.alloc_semaphore("sem_" + k)
            self.cnt[k] = 0
        self.dma_i = {"sync": 0, "gpsimd": 0}
        self.eng_sem = {"tensor": "t", "vector": "v", "scalar": "s", "gpsimd": "p"}
        self.pending = {e: [] for e in self.ENGINES}

    def _waits(self, eng, bufs_r, bufs_w):
        need = {}
        for b in bufs_r:
            if b.last_w is not None:
                k, v = b.last_w
                if need.get(k, 0) < v:
                    need[k] = v
        for b in bufs_w:
            if b.last_w is not None:
                k, v = b.last_w
                if need.get(k, 0) < v:
                    need[k] = v
            for k, v in b.readers:
                if need.get(k, 0) < v:
                    need[k] = v
        out = []
        sy = self.syncd[eng]
        for k, v in need.items():
            if sy.get(k, 0) < v:
                out.append((k, v))
                sy[k] = v
        return out

    def _rec(self, eng, fn, sk, n_inc, reads, writes, war):
        waits = self._waits(eng, reads, list(writes) + list(war))
        self.cnt[sk] += n_inc
        newv = self.cnt[sk]
        self.ops[eng].append((waits, fn, sk, n_inc))
        done = (sk, newv)
        if n_inc:
            for b in self.pending[eng]:
                b.readers.append(done)
            self.pending[eng] = []
            for b in writes:
                b.last_w = done
                b.readers = []
            for b in reads:
                b.readers.append(done)
        else:
            self.pending[eng].extend(reads)
        return done

    def op(self, eng, fn, reads=(), writes=(), war=(), n_inc=1):
        return self._rec(eng, fn, self.eng_sem[eng], n_inc, reads, writes, war)

    def mm(self, fn, reads=(), writes=(), war=(), last=True, tick=False):
        return self._rec("tensor", fn, "t", 1 if (last or tick) else 0,
                         reads, writes, war)

    def dma(self, queue, out_ap, in_ap, reads=(), writes=(), war=()):
        i = self.dma_i[queue]
        self.dma_i[queue] = i + 1
        sk = ("qd" if queue == "sync" else "pd") + str(i % 8)

        def fn(e, _o=out_ap, _i=in_ap):
            return e.dma_start(out=_o, in_=_i)

        return self._rec(queue, fn, sk, 16, reads, writes, war)

    def coll(self, ins, outs, reads=(), writes=()):
        def fn(e):
            return e.collective_compute("AllGather", ALU.bypass,
                                        replica_groups=GROUPS, ins=ins, outs=outs)

        return self._rec("gpsimd", fn, "p", 1, reads, writes, ())

    def wait_all(self, eng, bufs):
        waits = self._waits(eng, bufs, ())
        if waits:
            self.ops[eng].append((waits, None, None, 0))

    def finalize(self):
        sems = self.sems
        with self.nc.Block() as block:
            for eng in self.ENGINES:
                lst = self.ops[eng]

                def run(e, _lst=lst):
                    for waits, fn, sk, n_inc in _lst:
                        for k, v in waits:
                            e.wait_ge(sems[k], v)
                        if fn is not None:
                            ins = fn(e)
                            if n_inc:
                                ins.then_inc(sems[sk], n_inc)
                getattr(block, eng)(run)


def build(debug=False):
    nc = bass.Bass("TRN2", target_bir_lowering=False, debug=False)

    def inp(name, shape):
        return nc.dram_tensor(name, list(shape), F32, kind="ExternalInput")

    hid = inp("hid", (NTOK, D_))
    flagc = inp("flagc", (128, 1))
    w_inT = inp("w_inT", (D_, PROJ))
    w_outT = inp("w_outT", (D_, D_))
    a_Td = inp("a_T", (D_, 1024))
    b_flat = inp("b_flat", (1024, D_))
    wr_Td = inp("wr_T", (D_, 64))
    convw_d = inp("convw", (128, NXBC * 4))
    convb_d = inp("convb", (128, NXBC))
    dtb_d = inp("dtb", (64, 1))
    alog_d = inp("alog", (64, 1))
    dvec_d = inp("dvec", (128, 32))
    normw_d = inp("normw", (128, 32))

    out_d = nc.dram_tensor("out", [NTOK, D_], F32, kind="ExternalOutput")

    def scratch(name, shape, dt):
        kind = "ExternalOutput" if debug else "Internal"
        return nc.dram_tensor(name, list(shape), dt, kind=kind)

    xbc_sp = scratch("xbc_sp", (NXBC, 128, NTOK), BF16)
    gate_sp = scratch("gate_sp", (32, 128, NTOK), BF16)
    x2_sp = scratch("x2_sp", (NTT, 128, D_), F32)
    ln2_sp = scratch("ln2_sp", (NDT, 128, NTOK), BF16)
    xt_sp = nc.dram_tensor("xt_sp", [NCH, 128, 4096], BF16)
    bt_sp = nc.dram_tensor("bt_sp", [NCH, 128, 1024], BF16)
    cc_h_in = nc.dram_tensor("cc_h_in", [128, NXBC * 3], F32)
    cc_h_out = nc.dram_tensor("cc_h_out", [256, NXBC * 3], F32)
    cc_s_in = nc.dram_tensor("cc_s_in", [128, H_ * P_], F32)
    cc_s_out = nc.dram_tensor("cc_s_out", [256, H_ * P_], F32)
    dbg = {}
    if debug:
        for nm, shp in [("d_dtp", (64, NTOK)), ("d_gcum", (64, NTOK)),
                        ("d_statefin", (128, H_ * P_)), ("d_s0", (128, H_ * P_)),
                        ("d_y", (NCH, 128, D_)), ("d_yn", (NCH, 128, D_)),
                        ("d_scores", (NTT, 128, 64)), ("d_gates", (NTT, 128, 64)),
                        ("d_hg", (128, 8 * NTOK)),
                        ("d_g0", (128, NTOK)), ("d_op", (128, 512)),
                        ("d_cw", (128, 64)), ("d_at", (128, 64)),
                        ("d_dtt", (128, 64)), ("d_bt", (128, 1024)),
                        ("d_xt0", (128, 128)), ("d_st1", (128, H_ * P_)),
                        ("d_sm", (128, 1024)), ("d_mt0", (128, 128)),
                        ("d_ych_pre", (128, D_))]:
            dbg[nm] = nc.dram_tensor(nm, list(shp), F32, kind="ExternalOutput")

    ident_bf_d = nc.inline_tensor(np.eye(128, dtype=bfdt), "c_ident_bf")
    ident_f32_d = nc.inline_tensor(np.eye(128, dtype=np.float32), "c_ident_f32")
    si = np.arange(128)[:, None]
    ti = np.arange(128)[None, :]
    maskadd_d = nc.inline_tensor(np.where(si <= ti, 0.0, -1e30).astype(np.float32),
                                 "c_maskadd")
    tri01_d = nc.inline_tensor((si <= ti).astype(bfdt), "c_tri01")
    onesr_d = nc.inline_tensor(np.ones((1, 128), dtype=bfdt), "c_onesr")
    gsel_np = np.zeros((128, 32 * 8), dtype=bfdt)
    for i in range(32):
        gsel_np[:, i * 8 + i // 4] = 1.0
    gsel_d = nc.inline_tensor(gsel_np, "c_gsel")
    E_np = np.zeros((64, 1024), dtype=bfdt)
    for kr in range(1024):
        E_np[kr // 16, kr] = 1.0
    E_d = nc.inline_tensor(E_np, "c_E")

    SB = nc.alloc_sbuf_tensor
    AR1 = SB("AR1", [128, 32768], BF16)
    SCR = SB("SCR", [128, 16896], BF16)
    AR3 = SB("AR3", [128, 3072], F32)
    state = SB("state", [128, H_ * P_], F32)
    ych = SB("ych", [128, D_], F32)
    hbuf = SB("hbuf", [128, D_], F32)
    wbuf = SB("wbuf", [128, 8192], BF16)
    dtp = SB("dtp", [64, NTOK], F32)
    gcum = SB("gcum", [64, NTOK], F32)
    ecfl = SB("ecfl", [1, 1024], BF16)
    bnout = SB("bnout", [128, 6 * 8], F32)
    bng = SB("bng", [128, 8 * 8 * 6], F32)
    aggr = SB("aggr", [128, 2], F32)
    cols = SB("cols", [128, 16], F32)
    colsg = SB("colsg", [8, 256], F32)
    rstd_bfr = SB("rstd_bfr", [8, 128], BF16)
    rfl = SB("rfl", [1, 1024], BF16)
    m8a = SB("m8a", [128, 8], F32)
    m8b = SB("m8b", [128, 8], F32)
    mrtmp = SB("mrtmp", [128, 64], F32)
    mrtmp_dump = SB("mrtmp_dump", [128, 512], F32)
    gbuf = SB("gbuf", [128, 64], F32)
    scores_b = SB("scores_b", [128, 64], F32)
    ecwrk = SB("ecwrk", [64, 128], F32)
    zc = SB("zc", [64, 128], F32)
    lntmp = SB("lntmp", [64, 128], F32)
    cwrow = SB("cwrow", [64, 128], F32)
    ecum_bf = SB("ecum_bf", [64, 128], BF16)
    dtp_tok = SB("dtp_tok", [128, 64], F32)
    cw_tok = SB("cw_tok", [128, 64], F32)
    atot_cols = SB("atot_cols", [128, 64], F32)
    arow = SB("arow", [1, 256], F32)
    egl = SB("egl", [64, 4], F32)
    epsc = SB("epsc", [128, 1], F32)
    onec = SB("onec", [128, 1], F32)
    y2b = SB("y2b", [128, 256], BF16)
    halo_send = SB("halo_send", [128, NXBC * 3], F32)
    halo_in = SB("halo_in", [128, NXBC * 3], F32)
    presilu = SB("presilu", [128, NXBC * 3], F32)
    fix3 = SB("fix3", [128, NXBC * 3], BF16)
    convw_s = SB("convw_s", [128, NXBC * 4], F32)
    convb_s = SB("convb_s", [128, NXBC], F32)
    dtb_s = SB("dtb_s", [64, 1], F32)
    alog_s = SB("alog_s", [64, 1], F32)
    nA = SB("nA", [64, 1], F32)
    dvec_s = SB("dvec_s", [128, 32], F32)
    normw_s = SB("normw_s", [128, 32], F32)
    flag_s = SB("flag_s", [128, 1], F32)
    ident_bf_s = SB("ident_bf_s", [128, 128], BF16)
    ident_f32_s = SB("ident_f32_s", [128, 128], F32)
    maskadd_s = SB("maskadd_s", [128, 128], F32)
    tri01_s = SB("tri01_s", [128, 128], BF16)
    onesr_s = SB("onesr_s", [1, 128], BF16)
    gsel_s = SB("gsel_s", [128, 32 * 8], BF16)
    E_s = SB("E_s", [64, 1024], BF16)

    BV3 = AR3.ap().bitcast(BF16)
    SCRF = SCR.ap().bitcast(F32)
    dtvals = ych[0:64, 0:1024]

    PA = nc.alloc_psum_tensor("pa", [128, 1024], F32)
    PB = nc.alloc_psum_tensor("pb", [128, 1024], F32)
    PY0 = nc.alloc_psum_tensor("py0", [128, 512], F32)
    PY1 = nc.alloc_psum_tensor("py1", [128, 512], F32)
    PSX = nc.alloc_psum_tensor("psx", [128, 512], F32)
    PST = nc.alloc_psum_tensor("pst", [128, 1024], BF16)
    PSF = [PA[:, 0:512], PA[:, 512:1024], PB[:, 0:512], PB[:, 512:1024],
           PY0[:, 0:512], PY1[:, 0:512]]

    em = Emit(nc)
    bufs = {}

    def B(name):
        if name not in bufs:
            bufs[name] = Buf(name)
        return bufs[name]

    bPSF = [B(f"psf{i}") for i in range(6)]
    bPST0 = B("pst")
    bPST1 = B("pst1")
    bPSTh = lambda sl: bPST0 if sl < 4 else bPST1
    bPSTfull = [bPST0, bPST1]
    bPSX = B("psx")

    def vo(fn, r=(), w=(), war=()):
        return em.op("vector", fn, r, w, war)

    def so(fn, r=(), w=(), war=()):
        return em.op("scalar", fn, r, w, war)

    em.dma("sync", convw_s[:, :], convw_d.ap(), writes=[B("convw")])
    em.dma("sync", convb_s[:, :], convb_d.ap(), writes=[B("convb")])
    em.dma("sync", dtb_s[:, :], dtb_d.ap(), writes=[B("dtb")])
    em.dma("sync", alog_s[:, :], alog_d.ap(), writes=[B("alog")])
    em.dma("sync", dvec_s[:, :], dvec_d.ap(), writes=[B("dvec")])
    em.dma("sync", normw_s[:, :], normw_d.ap(), writes=[B("normw")])
    em.dma("sync", flag_s[:, :], flagc.ap(), writes=[B("flag")])
    em.dma("sync", ident_bf_s[:, :], ident_bf_d.ap(), writes=[B("identb")])
    em.dma("sync", ident_f32_s[:, :], ident_f32_d.ap(), writes=[B("identf")])
    em.dma("sync", maskadd_s[:, :], maskadd_d.ap(), writes=[B("maskadd")])
    em.dma("sync", tri01_s[:, :], tri01_d.ap(), writes=[B("tri01")])
    em.dma("sync", onesr_s[:, :], onesr_d.ap(), writes=[B("onesr")])
    em.dma("sync", gsel_s[:, :], gsel_d.ap(), writes=[B("gsel")])
    em.dma("sync", E_s[:, :], E_d.ap(), writes=[B("E")])
    so(lambda e: e.activation(out=nA[:, :], in_=alog_s[:, :], func=AF.Exp),
       [B("alog")], [B("nA")])
    vo(lambda e: e.tensor_scalar_mul(nA[:, :], nA[:, :], -1.0), [B("nA")], [B("nA")])
    vo(lambda e: e.memset(state[:, :], 0.0), [], [B("state")])
    vo(lambda e: e.memset(epsc[:, :], EPS), [], [B("epsc")])
    vo(lambda e: e.memset(onec[:, :], 1.0), [], [B("onec")])

    bln1T = B("ln1T")
    blnbf = B("lnbf")
    lnbf = SCR[:, 0:4096]
    for tt in range(NTT):
        em.dma("sync", hbuf[:, :], hid.ap()[tt * 128:(tt + 1) * 128, :],
               writes=[B("hbuf")])
        for i in range(8):
            vo(lambda e, i=i: e.bn_stats(bnout[:, i * 6:(i + 1) * 6],
                                         hbuf[:, i * 512:(i + 1) * 512]),
               [B("hbuf")], [B("bnout")])
        vo(lambda e: e.bn_aggr(aggr[:, :], bnout[:, :]), [B("bnout")], [B("aggr")])
        so(lambda e: e.activation(out=cols[:, 0:1], in_=aggr[:, 1:2], func=AF.Ln,
                                  bias=epsc[:, 0:1]), [B("aggr"), B("epsc")], [B("c0")])
        so(lambda e: e.activation(out=cols[:, 1:2], in_=cols[:, 0:1], func=AF.Exp,
                                  scale=-0.5), [B("c0")], [B("c1")])
        vo(lambda e: e.scalar_tensor_tensor(cols[:, 2:3], aggr[:, 0:1], -1.0,
                                            cols[:, 1:2], ALU.mult, ALU.mult),
           [B("aggr"), B("c1")], [B("c2")])
        vo(lambda e: e.tensor_scalar(lnbf, hbuf[:, :], cols[:, 1:2], cols[:, 2:3],
                                     ALU.mult, ALU.add),
           [B("hbuf"), B("c1"), B("c2")], [blnbf])
        for k in range(NDT):
            sl = k % 8
            em.mm(lambda e, k=k, sl=sl: e.transpose(
                PST[:, sl * 128:(sl + 1) * 128], lnbf[:, k * 128:(k + 1) * 128],
                ident_bf_s[:, :]),
                [blnbf, B("identb")], [bPSTh(sl)], war=[bPSTh(sl)], last=True)
            vo(lambda e, k=k, tt=tt, sl=sl: e.tensor_copy(
                AR1[:, k * NTOK + tt * 128: k * NTOK + tt * 128 + 128],
                PST[:, sl * 128:(sl + 1) * 128]),
               [bPSTh(sl)], [bln1T])

    pc_v = [AR3[:, 0:1024], AR3[:, 0:1024]]
    bpc = [B("pc0"), B("pc0")]
    cv_v = [AR3[:, 1024:2048], AR3[:, 2048:3072]]
    bcv = [B("cv0"), B("cv1")]
    xstage_r = [SCR[:, 4096:5120], SCR[:, 5120:6144]]
    tstage_r = [SCR[:, 0:1024], SCR[:, 1024:2048]]
    gstage = SCR[:, 5120:6144]
    bxs_r = [B("xstage0"), B("xstage1")]
    bts_r = [B("tstage0"), B("tstage1")]
    bgs = B("gstage")

    def wslot(s):
        return wbuf[:, s * 2048:(s + 1) * 2048]

    bws = [B(f"ws{i}") for i in range(4)]

    def tr_job(fx):
        xst = xstage_r[fx % 2]
        bxst = bxs_r[fx % 2]
        tst = tstage_r[fx % 2]
        btst = bts_r[fx % 2]
        for ch in range(8):
            sl = ch % 8
            em.mm(lambda e, ch=ch, sl=sl, xst=xst: e.transpose(
                PST[:, sl * 128:(sl + 1) * 128],
                xst[:, ch * 128:(ch + 1) * 128], ident_bf_s[:, :]),
                [bxst, B("identb")], [bPSTh(sl)], war=[bPSTh(sl)])
            vo(lambda e, ch=ch, sl=sl, tst=tst: e.tensor_copy(
                tst[:, ch * 128:(ch + 1) * 128],
                PST[:, sl * 128:(sl + 1) * 128]),
               [bPSTh(sl)], [btst], war=[blnbf])
        if fx < 32:
            em.dma("sync",
                   xt_sp.ap().rearrange("c p f -> p c f")
                   [:, :, fx * 128:(fx + 1) * 128],
                   tst.rearrange("p (c f) -> p c f", c=8),
                   reads=[btst], writes=[B("xt_hbm")])
        else:
            em.dma("sync",
                   bt_sp.ap().rearrange("c p f -> p c f")
                   [:, :, (fx - 32) * 128:(fx - 31) * 128],
                   tst.rearrange("p (c f) -> p c f", c=8),
                   reads=[btst], writes=[B("bt_hbm")])

    pend_tr = []

    for f in range(NFT):
        fw = 64 if f == NFT - 1 else 128

        for half in range(2):
            s = (2 * f + half) % 4
            em.dma("gpsimd",
                   wslot(s).rearrange("p (k f) -> p k f", k=16)[:, :, 0:fw],
                   w_inT.ap().rearrange("(k p) f -> p k f", p=128)
                   [:, half * 16:(half + 1) * 16, f * 128:f * 128 + fw],
                   writes=[bws[s]])
        pse = (0, 1) if f % 2 == 0 else (2, 3)
        for k in range(NDT):
            s = (2 * f + k // 16) % 4
            for j in range(2):
                pb, bpb = PSF[pse[j]], bPSF[pse[j]]
                em.mm(lambda e, k=k, j=j, pb=pb, s=s, fw=fw: e.matmul(
                    pb[0:fw, :],
                    wslot(s)[:, (k % 16) * 128:(k % 16) * 128 + fw],
                    AR1[:, k * NTOK + j * 512: k * NTOK + (j + 1) * 512],
                    start=(k == 0), stop=(k == NDT - 1)),
                    [bws[s], bln1T],
                    [bpb] if k == NDT - 1 else [],
                    war=[bpb] if k == 0 else [],
                    last=(k == NDT - 1))
        if f == NFT - 1:
            for j in range(2):
                vo(lambda e, j=j, pse=pse: e.tensor_copy(
                    dtvals[:, j * 512:(j + 1) * 512], PSF[pse[j]][0:64, :]),
                   [bPSF[pse[j]]], [B("dtvals")])
            continue
        pcb, bpcb = pc_v[f % 2], bpc[f % 2]
        if f < 32:
            for j in range(2):
                so(lambda e, j=j, pcb=pcb, pse=pse: e.activation(
                    out=pcb[:, j * 512:(j + 1) * 512], in_=PSF[pse[j]],
                    func=AF.Silu), [bPSF[pse[j]]], [bpcb])
            gst = xstage_r[f % 2]
            bgst = bxs_r[f % 2]
            vo(lambda e, f=f, pcb=pcb, gst=gst: e.tensor_scalar_mul(
                gst, pcb, normw_s[:, f:f + 1]),
               [bpcb, B("normw")], [bgst])
            if debug and f == 0:
                em.dma("sync", dbg["d_g0"].ap(), pcb, reads=[bpcb])
            em.dma("sync", gate_sp.ap()[f, :, :], gst, reads=[bgst], writes=[B("gate_hbm")])
        else:
            fx = f - 32
            for j in range(2):
                vo(lambda e, j=j, pcb=pcb, pse=pse: e.tensor_copy(
                    pcb[:, j * 512:(j + 1) * 512], PSF[pse[j]]),
                   [bPSF[pse[j]]], [bpcb])
            vo(lambda e, fx=fx, pcb=pcb: e.tensor_copy(
                halo_send[:, fx * 3:fx * 3 + 3], pcb[:, NTOK - 3:NTOK]),
               [bpcb], [B("halo_send")])
            cv0, cv1 = cv_v[0], cv_v[1]
            b0, b1 = bcv[0], bcv[1]
            w, bc = convw_s, convb_s
            vo(lambda e, fx=fx, pcb=pcb, cv0=cv0: e.tensor_scalar(
                cv0, pcb, w[:, fx * 4 + 3:fx * 4 + 4], bc[:, fx:fx + 1],
                ALU.mult, ALU.add),
               [bpcb, B("convw"), B("convb")], [b0])
            vo(lambda e, fx=fx, pcb=pcb, cv0=cv0, cv1=cv1: e.scalar_tensor_tensor(
                cv1[:, 1:NTOK], pcb[:, 0:NTOK - 1], w[:, fx * 4 + 2:fx * 4 + 3],
                cv0[:, 1:NTOK], ALU.mult, ALU.add),
               [bpcb, b0, B("convw")], [b1])
            vo(lambda e, cv0=cv0, cv1=cv1: e.tensor_copy(cv1[:, 0:1], cv0[:, 0:1]),
               [b0], [b1])
            vo(lambda e, fx=fx, pcb=pcb, cv0=cv0, cv1=cv1: e.scalar_tensor_tensor(
                cv0[:, 2:NTOK], pcb[:, 0:NTOK - 2], w[:, fx * 4 + 1:fx * 4 + 2],
                cv1[:, 2:NTOK], ALU.mult, ALU.add),
               [bpcb, b1, B("convw")], [b0])
            vo(lambda e, cv0=cv0, cv1=cv1: e.tensor_copy(cv0[:, 0:2], cv1[:, 0:2]),
               [b1], [b0])
            vo(lambda e, fx=fx, pcb=pcb, cv0=cv0, cv1=cv1: e.scalar_tensor_tensor(
                cv1[:, 3:NTOK], pcb[:, 0:NTOK - 3], w[:, fx * 4 + 0:fx * 4 + 1],
                cv0[:, 3:NTOK], ALU.mult, ALU.add),
               [bpcb, b0, B("convw")], [b1])
            vo(lambda e, cv0=cv0, cv1=cv1: e.tensor_copy(cv1[:, 0:3], cv0[:, 0:3]),
               [b0], [b1])
            vo(lambda e, fx=fx, cv1=cv1: e.tensor_copy(
                presilu[:, fx * 3:fx * 3 + 3], cv1[:, 0:3]),
               [b1], [B("presilu")])
            xst = xstage_r[fx % 2]
            bxst = bxs_r[fx % 2]
            tst = tstage_r[fx % 2]
            btst = bts_r[fx % 2]
            so(lambda e, cv1=cv1, xst=xst: e.activation(out=xst, in_=cv1,
                                                        func=AF.Silu),
               [b1], [bxst])
            em.dma("sync", xbc_sp.ap()[fx, :, :], xst, reads=[bxst],
                   writes=[B("xbc_hbm")])


    em.dma("sync", cc_h_in.ap(), halo_send[:, :], reads=[B("halo_send")],
           writes=[B("cch_in")])
    em.coll([cc_h_in.ap().opt()], [cc_h_out.ap().opt()],
            reads=[B("cch_in")], writes=[B("cch_out")])
    em.dma("sync", halo_in[:, :], cc_h_out.ap()[0:128, :], reads=[B("cch_out")],
           writes=[B("halo_in")])
    vo(lambda e: e.tensor_scalar_mul(halo_in[:, :], halo_in[:, :], flag_s[:, 0:1]),
       [B("halo_in"), B("flag")], [B("halo_in")])
    hv = halo_in.ap().rearrange("p (f j) -> p f j", f=NXBC)
    pv = presilu.ap().rearrange("p (f j) -> p f j", f=NXBC)
    wv = convw_s.ap().rearrange("p (f k) -> p f k", f=NXBC)
    tmp3 = mrtmp.ap().rearrange("p (f j) -> p f j", j=1)[:, 0:48, :]
    for (t, k, j) in [(0, 0, 0), (0, 1, 1), (0, 2, 2), (1, 0, 1), (1, 1, 2),
                      (2, 0, 2)]:
        vo(lambda e, k=k, j=j: e.tensor_tensor(
            tmp3, hv[:, :, j:j + 1], wv[:, :, k:k + 1], ALU.mult),
           [B("halo_in"), B("convw")], [B("mr")])
        vo(lambda e, t=t: e.tensor_tensor(
            pv[:, :, t:t + 1], pv[:, :, t:t + 1], tmp3, ALU.add),
           [B("mr"), B("presilu")], [B("presilu")])
    so(lambda e: e.activation(out=fix3[:, :], in_=presilu[:, :], func=AF.Silu),
       [B("presilu")], [B("fix3")])

    dt1 = AR3[:, 0:1024]
    dt2 = AR3[:, 1024:2048]
    bd1, bd2 = B("dt1"), B("dt2")
    vo(lambda e: e.tensor_scalar_add(dt1[0:64, :], dtvals[:, :], dtb_s[:, 0:1]),
       [B("dtvals"), B("dtb")], [bd1], war=[bpc[0], bcv[0]])
    so(lambda e: e.activation(out=dt2[0:64, :], in_=dt1[0:64, :], func=AF.Abs),
       [bd1], [bd2], war=[bcv[0], bcv[1]])
    so(lambda e: e.activation(out=dt2[0:64, :], in_=dt2[0:64, :], func=AF.Exp,
                              scale=-1.0), [bd2], [bd2])
    so(lambda e: e.activation(out=dt2[0:64, :], in_=dt2[0:64, :], func=AF.Ln,
                              bias=onec[0:64, 0:1]), [bd2, B("onec")], [bd2])
    vo(lambda e: e.tensor_scalar_max(dtp[:, :], dt1[0:64, :], 0.0), [bd1], [B("dtp")])
    vo(lambda e: e.tensor_add(dtp[:, :], dtp[:, :], dt2[0:64, :]),
       [B("dtp"), bd2], [B("dtp")])
    vo(lambda e: e.tensor_scalar_mul(dt1[0:64, :], dtp[:, :], nA[:, 0:1]),
       [B("dtp"), B("nA")], [bd1])
    vo(lambda e: e.tensor_tensor_scan(gcum[:, :], dt1[0:64, :], dt1[0:64, :], 0.0,
                                      ALU.add, ALU.bypass),
       [bd1], [B("gcum")])
    if debug:
        em.dma("sync", dbg["d_dtp"].ap(), dtp[:, :], reads=[B("dtp")])
        em.dma("sync", dbg["d_gcum"].ap(), gcum[:, :], reads=[B("gcum")])

    xch = SCR[:, 0:6144]
    xt_v = SCR[:, 6144:10240]
    bt_v = SCR[:, 10240:11264]
    sm_v = SCR[:, 11264:12288]
    state_bf = hbuf.ap().bitcast(BF16)[:, 0:4096]
    gate_ch = SCR[:, 12288:16384]
    ux_v = SCR[:, 16384:16896]
    bxch, bxt, bbt, bsm = B("xch"), B("xt"), B("bt"), B("sm")
    bsbf, bgch = B("sbf"), B("gch")
    tmask8 = AR3[:, 0:1024]
    exp8 = BV3[:, 2048:3072]
    mt8 = BV3[:, 3072:4096]
    cxt8 = BV3[:, 4096:5120]
    btm, bte, bmt, bcx = B("tmask"), B("texp"), B("mt"), B("cxt")

    def chunk_decay(c, need_ecum):
        c0, c1 = c * 128, (c + 1) * 128
        vo(lambda e, c0=c0, c1=c1: e.tensor_scalar(
            ecwrk[:, :], gcum[:, c0:c1], -1.0, gcum[:, c1 - 1:c1],
            ALU.mult, ALU.add), [B("gcum")], [B("ecw")])
        so(lambda e: e.activation(out=ecwrk[:, :], in_=ecwrk[:, :], func=AF.Exp),
           [B("ecw")], [B("ecw")])
        vo(lambda e, c0=c0, c1=c1: e.tensor_mul(
            cwrow[:, :], dtp[:, c0:c1], ecwrk[:, :]),
           [B("dtp"), B("ecw")], [B("cwr")])
        em.mm(lambda e: e.transpose(PSX[:, 0:64], cwrow[0:64, :], ident_f32_s[0:64, 0:64]),
              [B("cwr"), B("identf")], [bPSX], war=[bPSX])
        vo(lambda e: e.tensor_copy(cw_tok[:, 0:64], PSX[:, 0:64]),
           [bPSX], [B("cwt")])
        if c == 0:
            vo(lambda e, c1=c1: e.tensor_copy(egl[:, 0:1], gcum[:, c1 - 1:c1]),
               [B("gcum")], [B("egl")])
        else:
            vo(lambda e, c0=c0, c1=c1: e.tensor_scalar(
                egl[:, 0:1], gcum[:, c0 - 1:c0], -1.0, gcum[:, c1 - 1:c1],
                ALU.mult, ALU.add), [B("gcum")], [B("egl")])
        em.dma("sync", arow[0:1, 0:64], egl[:, 0:1], reads=[B("egl")],
               writes=[B("arow")])
        so(lambda e: e.activation(out=arow[0:1, 64:128], in_=arow[0:1, 0:64],
                                  func=AF.Exp), [B("arow")], [B("arow")])
        vo(lambda e: e.tensor_copy(arow.ap().bitcast(BF16)[0:1, 384:448],
                                   arow[0:1, 64:128]), [B("arow")], [B("arowb")])
        em.mm(lambda e: e.matmul(PSX[:, 256:320], onesr_s[0:1, :],
                                 arow.ap().bitcast(BF16)[0:1, 384:448],
                                 start=True, stop=True),
              [B("arowb"), B("onesr")], [bPSX], war=[bPSX])
        vo(lambda e: e.tensor_copy(atot_cols[:, :], PSX[:, 256:320]),
           [bPSX], [B("atc")])
        if need_ecum:
            if c == 0:
                so(lambda e, c0=c0, c1=c1: e.activation(
                    out=ecum_bf[:, :], in_=gcum[:, c0:c1], func=AF.Exp),
                    [B("gcum")], [B("ecb_r")])
            else:
                vo(lambda e, c0=c0: e.tensor_scalar_mul(
                    egl[:, 1:2], gcum[:, c0 - 1:c0], -1.0), [B("gcum")], [B("egl2")])
                so(lambda e, c0=c0, c1=c1: e.activation(
                    out=ecum_bf[:, :], in_=gcum[:, c0:c1], func=AF.Exp,
                    bias=egl[:, 1:2]), [B("gcum"), B("egl2")], [B("ecb_r")])
            so(lambda e, c0=c0, c1=c1: e.activation(
                out=lntmp[:, :], in_=dtp[:, c0:c1], func=AF.Ln),
               [B("dtp")], [B("lntmp")])
            vo(lambda e, c0=c0, c1=c1: e.tensor_sub(
                zc[:, :], gcum[:, c0:c1], lntmp[:, :]),
               [B("gcum"), B("lntmp")], [B("zc")])

    def load_chunk(c, n_ft):
        c0 = c * 128
        em.dma("sync", xch.rearrange("p (f t) -> p f t", f=NXBC)[:, 0:n_ft, :],
               xbc_sp.ap().rearrange("f p t -> p f t")[:, 0:n_ft, c0:c0 + 128],
               reads=[B("xbc_hbm")], writes=[bxch])
        if c == 0:
            vo(lambda e, n_ft=n_ft: e.tensor_copy(
                xch.rearrange("p (f t) -> p f t", f=NXBC)[:, 0:n_ft, 0:3],
                fix3.ap().rearrange("p (f j) -> p f j", f=NXBC)[:, 0:n_ft, :]),
               [B("fix3")], [bxch])
        for i in range(40):
            sl = i % 8
            em.mm(lambda e, i=i, sl=sl: e.transpose(
                PST[:, sl * 128:(sl + 1) * 128], xch[:, i * 128:(i + 1) * 128],
                ident_bf_s[:, :]),
                [bxch, B("identb")], [bPSTh(sl)], war=[bPSTh(sl)])
            if i < 32:
                vo(lambda e, i=i, sl=sl: e.tensor_copy(
                    xt_v[:, i * 128:(i + 1) * 128], PST[:, sl * 128:(sl + 1) * 128]),
                   [bPSTh(sl)], [bxt], war=[B("dtvals")])
            else:
                vo(lambda e, i=i, sl=sl: e.tensor_copy(
                    bt_v[:, (i - 32) * 128:(i - 31) * 128],
                    PST[:, sl * 128:(sl + 1) * 128]),
                   [bPSTh(sl)], [bbt])

    bux = [B(f"ux{i}") for i in range(8)]

    def state_update():
        for pr in range(32):
            g = pr // 4
            for hh in range(2):
                h = pr * 2 + hh
                us = h % 8
                vo(lambda e, h=h, pr=pr, hh=hh, us=us: e.tensor_scalar_mul(
                    ux_v[:, us * 64:(us + 1) * 64],
                    xt_v[:, pr * 128 + hh * 64: pr * 128 + hh * 64 + 64],
                    cw_tok[:, h:h + 1]),
                   [bxt, B("cwt")], [bux[us]])
            bank = 4 + pr % 2
            pb, bpb = PSF[bank], bPSF[bank]
            u0 = (pr * 2) % 8
            em.mm(lambda e, pr=pr, g=g, pb=pb, u0=u0: e.matmul(
                pb[:, 0:128], bt_v[:, g * 128:(g + 1) * 128],
                ux_v[:, u0 * 64: u0 * 64 + 128], start=True, stop=True),
                [bbt, bux[u0], bux[u0 + 1]], [bpb], war=[bpb])
            for hh in range(2):
                h = pr * 2 + hh
                vo(lambda e, h=h, hh=hh, pb=pb: e.scalar_tensor_tensor(
                    state[:, h * 64:(h + 1) * 64], state[:, h * 64:(h + 1) * 64],
                    atot_cols[:, h:h + 1], pb[:, hh * 64:hh * 64 + 64],
                    ALU.mult, ALU.add),
                   [B("state"), B("atc"), bpb], [B("state")])

    uxw = wbuf[:, 0:512]
    cwbf = wbuf[:, 512:576]
    buxw, bcwbf = B("uxw"), B("cwbf")
    for c in range(NCH):
        c0, c1 = c * 128, (c + 1) * 128
        load_chunk(c, 40)
        em.dma("gpsimd", xt_sp.ap()[c, :, :], xt_v, reads=[bxt],
               writes=[B("xt_hbm")])
        em.dma("gpsimd", bt_sp.ap()[c, :, :], bt_v, reads=[bbt],
               writes=[B("bt_hbm")])
        vo(lambda e, c0=c0, c1=c1: e.tensor_scalar(
            ecwrk[:, :], gcum[:, c0:c1], -1.0, gcum[:, NTOK - 1:NTOK],
            ALU.mult, ALU.add), [B("gcum")], [B("ecw")])
        so(lambda e: e.activation(out=ecwrk[:, :], in_=ecwrk[:, :], func=AF.Exp),
           [B("ecw")], [B("ecw")])
        vo(lambda e, c0=c0, c1=c1: e.tensor_mul(
            cwrow[:, :], dtp[:, c0:c1], ecwrk[:, :]),
           [B("dtp"), B("ecw")], [B("cwr")])
        em.mm(lambda e: e.transpose(PSX[:, 0:64], cwrow[0:64, :],
                                    ident_f32_s[0:64, 0:64]),
              [B("cwr"), B("identf")], [bPSX], war=[bPSX])
        vo(lambda e: e.tensor_copy(cwbf, PSX[:, 0:64]), [bPSX], [bcwbf],
           war=[bws[0]])
        for q in range(8):
            em.op("gpsimd", lambda e, q=q: e.tensor_tensor(
                uxw.rearrange("p (a t) -> p a t", a=8),
                xt_v[:, q * 512:(q + 1) * 512].rearrange("p (a t) -> p a t", a=8),
                cwbf.rearrange("p (a t) -> p a t", t=1)[:, q * 8:(q + 1) * 8, :]
                .to_broadcast([128, 8, 64]), ALU.mult),
                [bxt, bcwbf], [buxw], war=[bws[0]])
            bank = [PY0, PY1, PSF[0], PSF[2]][q % 4]
            bbank = [bPSF[4], bPSF[5], bPSF[0], bPSF[2]][q % 4]
            for pp2 in range(4):
                pr = q * 4 + pp2
                g = pr // 4
                em.mm(lambda e, pr=pr, g=g, pp2=pp2, bank=bank: e.matmul(
                    bank[:, pp2 * 128:(pp2 + 1) * 128],
                    bt_v[:, g * 128:(g + 1) * 128],
                    uxw[:, pp2 * 128:(pp2 + 1) * 128], start=True, stop=True),
                    [bbt, buxw], [bbank] if pp2 == 3 else [],
                    war=[bbank] if pp2 == 0 else [], last=(pp2 == 3), tick=True)
            vo(lambda e, q=q, bank=bank: e.tensor_tensor(
                state[:, q * 512:(q + 1) * 512], state[:, q * 512:(q + 1) * 512],
                bank[:, :], ALU.add),
               [B("state"), bbank], [B("state")])
    em.dma("sync", cc_s_in.ap(), state[:, :], reads=[B("state")],
           writes=[B("ccs_in")])
    em.coll([cc_s_in.ap().opt()], [cc_s_out.ap().opt()],
            reads=[B("ccs_in")], writes=[B("ccs_out")])
    if debug:
        em.dma("sync", dbg["d_statefin"].ap(), state[:, :], reads=[B("state")])
    em.dma("sync", state[:, :], cc_s_out.ap()[0:128, :],
           reads=[B("ccs_out")], writes=[B("state")])
    vo(lambda e: e.tensor_scalar_mul(state[:, :], state[:, :], flag_s[:, 0:1]),
       [B("state"), B("flag")], [B("state")])
    if debug:
        em.dma("sync", dbg["d_s0"].ap(), state[:, :], reads=[B("state")])

    bynT = B("ynT")
    bych = B("ych")
    PSTF = PST.ap().bitcast(F32)
    LH8 = hbuf[0:2, 2048:3072]
    RH8 = hbuf[0:2, 3072:4096]
    bLH8, bRH8 = B("LH8"), B("RH8")
    mtb = [wbuf[:, 1024:2048], wbuf[:, 2048:3072]]
    cxb = [wbuf[:, 3072:4096], wbuf[:, 4096:5120]]
    xd4 = wbuf.ap().bitcast(F32)[:, 2560:2816]
    bmtb = [B("mtb0"), B("mtb1")]
    bcxb = [B("cxb0"), B("cxb1")]
    bxd4 = B("xd4")
    vo(lambda e: e.memset(hbuf[0:2, 2048:3072], 1.0), [], [bLH8],
       war=[B("hbuf")])
    vo(lambda e: e.memset(hbuf[0:2, 3072:4096], -1.0), [], [bRH8],
       war=[B("hbuf")])
    for c in range(NCH):
        em.dma("gpsimd", xch.rearrange("p (f t) -> p f t", f=NXBC),
               xbc_sp.ap().rearrange("f p t -> p f t")[:, :, c * 128:(c + 1) * 128],
               reads=[B("xbc_hbm")], writes=[bxch])
        if c == 0:
            vo(lambda e: e.tensor_copy(
                xch.rearrange("p (f t) -> p f t", f=NXBC)[:, :, 0:3],
                fix3.ap().rearrange("p (f j) -> p f j", f=NXBC)),
               [B("fix3")], [bxch])
        em.dma("gpsimd", xt_v, xt_sp.ap()[c, :, :], reads=[B("xt_hbm")],
               writes=[bxt])
        em.dma("gpsimd", bt_v, bt_sp.ap()[c, :, :], reads=[B("bt_hbm")],
               writes=[bbt])
        chunk_decay(c, need_ecum=True)
        vo(lambda e: e.tensor_copy(cwbf, cw_tok[:, 0:64]), [B("cwt")], [bcwbf],
           war=[bws[0]])
        em.dma("gpsimd", gate_ch.rearrange("p (i t) -> p i t", i=32),
               gate_sp.ap().rearrange("i p t -> p i t")[:, :, c * 128:(c + 1) * 128],
               reads=[B("gate_hbm")], writes=[bgch])
        vo(lambda e: e.tensor_copy(state_bf, state[:, :]), [B("state")], [bsbf],
           war=[B("hbuf")])
        for r2 in range(2):
            for gg in range(4):
                g = r2 * 4 + gg
                em.mm(lambda e, g=g, gg=gg: e.matmul(
                    PSX[:, gg * 128:(gg + 1) * 128],
                    xch[:, (32 + g) * 128:(33 + g) * 128],
                    xch[:, (40 + g) * 128:(41 + g) * 128], start=True, stop=True),
                    [bxch], [bPSX], war=[bPSX] if gg == 0 else [],
                    last=(gg == 3), tick=False)
            vo(lambda e, r2=r2: e.tensor_tensor(
                sm_v.rearrange("p (a t) -> p a t", a=8)[:, r2 * 4:(r2 + 1) * 4, :],
                PSX.ap().rearrange("p (a t) -> p a t", a=4),
                tri01_s.ap().rearrange("p (a t) -> p a t", a=1)
                .to_broadcast([128, 4, 128]), ALU.mult),
               [bPSX, B("tri01")], [bsm])
        c0 = c * 128

        def fS1(g, c0=c0):
            pp = PA if g % 2 == 0 else PB
            bpp = [bPSF[0], bPSF[1]] if g % 2 == 0 else [bPSF[2], bPSF[3]]
            em.dma("sync", LH8[0:1, :], zc[g * 8: g * 8 + 8, :],
                   reads=[B("zc")], writes=[bLH8])
            em.dma("sync", RH8[1:2, :], gcum[g * 8: g * 8 + 8, c0:c0 + 128],
                   reads=[B("gcum")], writes=[bRH8])
            for hs in range(8):
                em.mm(lambda e, hs=hs, pp=pp: e.matmul(
                    pp[:, hs * 128:(hs + 1) * 128],
                    LH8[:, hs * 128:(hs + 1) * 128],
                    RH8[:, hs * 128:(hs + 1) * 128],
                    start=True, stop=True),
                    [bLH8, bRH8],
                    bpp if hs == 7 else [],
                    war=bpp if hs == 0 else [],
                    last=(hs == 7), tick=(hs == 3))

        def fS2(g):
            pp = PA if g % 2 == 0 else PB
            bpp = [bPSF[0], bPSF[1]] if g % 2 == 0 else [bPSF[2], bPSF[3]]
            par = g % 2
            vo(lambda e, pp=pp: e.scalar_tensor_tensor(
                tmask8.rearrange("p (a t) -> p a t", a=8),
                pp.ap().rearrange("p (a t) -> p a t", a=8), 1.0,
                maskadd_s.ap().rearrange("p (a t) -> p a t", a=1)
                .to_broadcast([128, 8, 128]), ALU.mult, ALU.add),
               bpp + [B("maskadd")], [btm], war=[bd1])
            so(lambda e: e.activation(out=exp8, in_=tmask8, func=AF.Exp),
               [btm], [bte], war=[bd2])
            em.dma("sync", ecfl[0:1, :], ecum_bf[g * 8:g * 8 + 8, :],
                   reads=[B("ecb_r")], writes=[B("ecfl")])
            vo(lambda e, g=g, par=par: e.tensor_tensor(
                mtb[par].rearrange("p (a t) -> p a t", a=8),
                exp8.rearrange("p (a t) -> p a t", a=8),
                sm_v[:, g * 128:(g + 1) * 128]
                .rearrange("p (a t) -> p a t", a=1).to_broadcast([128, 8, 128]),
                ALU.mult),
               [bte, bsm], [bmtb[par]], war=[bws[0], bws[1]])
            em.mm(lambda e: e.matmul(PSX[:, 0:512], onesr_s[0:1, :],
                                     ecfl[0:1, 0:512], start=True, stop=True),
                  [B("ecfl"), B("onesr")], [bPSX], war=[bPSX])
            em.mm(lambda e: e.matmul(PSTF[:, 0:512], onesr_s[0:1, :],
                                     ecfl[0:1, 512:1024], start=True, stop=True),
                  [B("ecfl"), B("onesr")], bPSTfull, war=bPSTfull)
            so(lambda e: e.activation(out=exp8[:, 0:512], in_=PSX[:, 0:512],
                                      func=AF.Copy),
               [bPSX], [bte])
            so(lambda e: e.activation(out=exp8[:, 512:1024], in_=PSTF[:, 0:512],
                                      func=AF.Copy),
               bPSTfull, [bte])
            vo(lambda e, g=g, par=par: e.tensor_tensor(
                cxb[par].rearrange("p (a t) -> p a t", a=8),
                exp8.rearrange("p (a t) -> p a t", a=8),
                xch[:, (40 + g) * 128:(41 + g) * 128]
                .rearrange("p (a t) -> p a t", a=1).to_broadcast([128, 8, 128]),
                ALU.mult),
               [bte, bxch], [bcxb[par]], war=[bws[1], bws[2], bws[3]])

        def fS3(g):
            par = g % 2
            for hh8 in range(8):
                h = g * 8 + hh8
                pr, hh = h // 2, h % 2
                q4 = (pr // 2) % 2
                sl2 = pr % 2
                pyp = PY0 if q4 == 0 else PY1
                bpy = bPSF[4] if q4 == 0 else bPSF[5]
                em.mm(lambda e, pr=pr, hh=hh, hh8=hh8, pyp=pyp, sl2=sl2, par=par: e.matmul(
                    pyp[hh * 64:hh * 64 + 64, sl2 * 128:(sl2 + 1) * 128],
                    xt_v[:, pr * 128 + hh * 64: pr * 128 + hh * 64 + 64],
                    mtb[par][:, hh8 * 128:(hh8 + 1) * 128],
                    start=True, stop=False, tile_position=(0, hh * 64)),
                    [bxt, bmtb[par]], [],
                    war=([bpy] if (hh == 0 and sl2 == 0) else []), last=False)
                em.mm(lambda e, h=h, hh=hh, hh8=hh8, pyp=pyp, sl2=sl2, par=par: e.matmul(
                    pyp[hh * 64:hh * 64 + 64, sl2 * 128:(sl2 + 1) * 128],
                    state_bf[:, h * 64:(h + 1) * 64],
                    cxb[par][:, hh8 * 128:(hh8 + 1) * 128],
                    start=False, stop=True, tile_position=(0, hh * 64)),
                    [bsbf, bcxb[par]],
                    [bpy] if (hh == 1 and sl2 == 1) else [],
                    last=(hh == 1 and sl2 == 1), tick=(hh == 1))
                if hh == 1 and sl2 == 1:
                    pr0 = pr - 1
                    em.op("gpsimd", lambda e, pr0=pr0: e.tensor_tensor(
                        xd4.rearrange("p (a t) -> p a t", a=2),
                        xch[:, pr0 * 128:(pr0 + 2) * 128]
                        .rearrange("p (a t) -> p a t", a=2),
                        dvec_s.ap().rearrange("p (a t) -> p a t", t=1)
                        [:, pr0:pr0 + 2, :].to_broadcast([128, 2, 128]), ALU.mult),
                        [bxch, B("dvec")], [bxd4], war=[bws[2]])
                    vo(lambda e, pr0=pr0, pyp=pyp: e.tensor_tensor(
                        ych[:, pr0 * 128:(pr0 + 2) * 128], xd4,
                        pyp[:, 0:256], ALU.add),
                       [bxd4, bpy], [bych], war=[B("dtvals")])

        for i in range(10):
            if i < 8:
                fS1(i)
            if 1 <= i < 9:
                fS2(i - 1)
            if i >= 2:
                fS3(i - 2)
        vo(lambda e: e.tensor_tensor(
            state.ap().rearrange("p (a t) -> p a t", a=64),
            state.ap().rearrange("p (a t) -> p a t", a=64),
            atot_cols.ap().rearrange("p (a t) -> p a t", t=1)
            .to_broadcast([128, 64, 64]), ALU.mult),
           [B("state"), B("atc")], [B("state")])
        for q in range(8):
            em.op("gpsimd", lambda e, q=q: e.tensor_tensor(
                uxw.rearrange("p (a t) -> p a t", a=8),
                xt_v[:, q * 512:(q + 1) * 512].rearrange("p (a t) -> p a t", a=8),
                cwbf.rearrange("p (a t) -> p a t", t=1)[:, q * 8:(q + 1) * 8, :]
                .to_broadcast([128, 8, 64]), ALU.mult),
                [bxt, bcwbf], [buxw], war=[bws[0]])
            bank = PY0 if q % 2 == 0 else PY1
            bbank = bPSF[4] if q % 2 == 0 else bPSF[5]
            for pp2 in range(4):
                pr = q * 4 + pp2
                g = pr // 4
                em.mm(lambda e, pr=pr, g=g, pp2=pp2, bank=bank: e.matmul(
                    bank[:, pp2 * 128:(pp2 + 1) * 128],
                    bt_v[:, g * 128:(g + 1) * 128],
                    uxw[:, pp2 * 128:(pp2 + 1) * 128], start=True, stop=True),
                    [bbt, buxw], [bbank] if pp2 == 3 else [],
                    war=[bbank] if pp2 == 0 else [], last=(pp2 == 3), tick=True)
            vo(lambda e, q=q, bank=bank: e.tensor_tensor(
                state[:, q * 512:(q + 1) * 512], state[:, q * 512:(q + 1) * 512],
                bank[:, :], ALU.add),
               [B("state"), bbank], [B("state")])
        if debug:
            em.dma("sync", dbg["d_y"].ap()[c, :, :], ych[:, :], reads=[bych])
        for i in range(32):
            ysl = i % 2
            em.op("gpsimd", lambda e, i=i, ysl=ysl: e.tensor_tensor(
                y2b[:, ysl * 128:(ysl + 1) * 128], ych[:, i * 128:(i + 1) * 128],
                ych[:, i * 128:(i + 1) * 128], ALU.mult),
                [bych], [B(f"y2{ysl}")])
            em.mm(lambda e, i=i, ysl=ysl: e.matmul(
                PSX[0:8, 0:128], gsel_s[:, i * 8:(i + 1) * 8],
                y2b[:, ysl * 128:(ysl + 1) * 128],
                start=(i == 0), stop=(i == 31)),
                [B(f"y2{ysl}"), B("gsel")],
                [bPSX] if i == 31 else [],
                war=[bPSX] if i == 0 else [], last=(i == 31), tick=True)
        vo(lambda e: e.tensor_copy(colsg[0:8, 0:128], PSX[0:8, 0:128]),
           [bPSX], [B("varg")])
        so(lambda e: e.activation(out=colsg[0:8, 128:256], in_=colsg[0:8, 0:128],
                                  func=AF.Ln, bias=epsc[0:8, 0:1], scale=1.0 / 512.0),
           [B("varg"), B("epsc")], [B("varg")])
        so(lambda e: e.activation(out=colsg[0:8, 0:128], in_=colsg[0:8, 128:256],
                                  func=AF.Exp, scale=-0.5),
           [B("varg")], [B("varg")])
        vo(lambda e: e.tensor_copy(rstd_bfr[0:8, :], colsg[0:8, 0:128]),
           [B("varg")], [B("rstdb")])
        em.dma("sync", rfl[0:1, :], rstd_bfr[0:8, :], reads=[B("rstdb")],
               writes=[B("rfl")])
        for g in range(8):
            em.mm(lambda e, g=g: e.matmul(
                PSX[:, 0:128], onesr_s[0:1, :], rfl[0:1, g * 128:(g + 1) * 128],
                start=True, stop=True),
                [B("rfl"), B("onesr")], [bPSX], war=[bPSX])
            vo(lambda e, g=g: e.tensor_tensor(
                xd4.rearrange("p (a t) -> p a t", a=2)
                if False else uxw.rearrange("p (a t) -> p a t", a=4)[:, 0:4, :],
                ych[:, g * 512:(g + 1) * 512].rearrange("p (a t) -> p a t", a=4),
                PSX[:, 0:128].rearrange("p (a t) -> p a t", a=1)
                .to_broadcast([128, 4, 128]), ALU.mult),
               [bych, bPSX], [buxw], war=[B("cwt2")])
            vo(lambda e, g=g, c=c: e.tensor_tensor(
                AR1.ap().rearrange("p (i t) -> p i t", i=NDT)
                [:, g * 4:(g + 1) * 4, c * 128:(c + 1) * 128],
                uxw.rearrange("p (a t) -> p a t", a=4),
                gate_ch.rearrange("p (a t) -> p a t", a=32)[:, g * 4:(g + 1) * 4, :],
                ALU.mult),
               [buxw, bgch], [bynT])
        if debug:
            em.dma("gpsimd",
                   dbg["d_yn"].ap()[c, :, :].rearrange("p (i t) -> p i t", i=NDT),
                   AR1.ap().rearrange("p (i t) -> p i t", i=NDT)
                   [:, :, c * 128:(c + 1) * 128],
                   reads=[bynT])

    bx2 = [B(f"x2_{tt}") for tt in range(NTT)]
    bwo = [B(f"wo{i}") for i in range(4)]

    def gpsum(tt):
        if tt < 6:
            return PSF[tt], [bPSF[tt]]
        if tt == 6:
            return PST.ap().bitcast(F32)[:, 0:512], bPSTfull
        return PSX[:, 0:512], [bPSX]

    for dblk in range(8):
        for i in range(NDT):
            s = (dblk * NDT + i) % 4
            em.dma("gpsimd", wbuf[:, s * 512:(s + 1) * 512],
                   w_outT.ap()[i * 128:(i + 1) * 128, dblk * 512:(dblk + 1) * 512],
                   writes=[bwo[s]],
                   war=((bws + [B("uxw"), B("cwbf"), B("mtb0")]) if (dblk == 0 and i < 4) else []))
            for tt in range(NTT):
                pap, bpb = gpsum(tt)
                em.mm(lambda e, i=i, tt=tt, s=s, pap=pap: e.matmul(
                    pap, AR1[:, i * NTOK + tt * 128: i * NTOK + tt * 128 + 128],
                    wbuf[:, s * 512:(s + 1) * 512],
                    start=(i == 0), stop=(i == NDT - 1)),
                    [bynT, bwo[s]],
                    bpb if i == NDT - 1 else [],
                    war=bpb if i == 0 else [], last=(i == NDT - 1),
                    tick=(tt == NTT - 1))
        em.dma("sync", hbuf.ap().rearrange("p (a d) -> p a d", a=8),
               hid.ap().rearrange("(a p) d -> p a d", p=128)
               [:, :, dblk * 512:(dblk + 1) * 512],
               writes=[B("hbuf")], war=[bsbf, bLH8, bRH8])
        for tt in range(NTT):
            pap, bpb = gpsum(tt)
            if debug and dblk == 0 and tt == 0:
                vo(lambda e, pap=pap: e.tensor_copy(mrtmp_dump[:, :], pap), bpb,
                   [B("opd")])
                em.dma("sync", dbg["d_op"].ap(), mrtmp_dump[:, :], reads=[B("opd")])
            vo(lambda e, tt=tt, pap=pap: e.tensor_tensor(
                ych[:, tt * 512:(tt + 1) * 512], pap,
                hbuf[:, tt * 512:(tt + 1) * 512], ALU.add),
               bpb + [B("hbuf")], [bych])
            vo(lambda e, tt=tt, dblk=dblk: e.bn_stats(
                bng[:, (tt * 8 + dblk) * 6:(tt * 8 + dblk) * 6 + 6],
                ych[:, tt * 512:(tt + 1) * 512]),
               [bych], [B("bng")])
            em.dma("sync", x2_sp.ap()[tt, :, dblk * 512:(dblk + 1) * 512],
                   ych[:, tt * 512:(tt + 1) * 512], reads=[bych], writes=[bx2[tt]])

    for half in range(2):
        em.dma("gpsimd",
               AR1.ap().rearrange("p (k t) -> p k t", k=8)
               [:, half * 4:(half + 1) * 4, :],
               b_flat.ap().rearrange("(k p) d -> p k d", p=128)
               [:, half * 4:(half + 1) * 4, :],
               writes=[B("bflat")], war=([bynT, bln1T] if half == 0 else []))
    bbf = B("bflat")
    wr_all = SCR[:, 4096:6144]
    bwr = B("wr_all")
    em.dma("gpsimd", wr_all.rearrange("p (k f) -> p k f", k=NDT),
           wr_Td.ap().rearrange("(k p) f -> p k f", p=128),
           writes=[bwr], war=[bxs_r[0], bxs_r[1], bgs, bxch])
    gt_all = SCR[:, 6144:7168]
    gexp_all = SCR[:, 8192:16384]
    bgt = B("gt_all")
    ln2st = BV3[:, 0:4096]
    bln2st = B("ln2st")
    for tt in range(NTT):
        vo(lambda e, tt=tt: e.bn_aggr(aggr[:, :], bng[:, tt * 48:(tt + 1) * 48]),
           [B("bng")], [B("aggr")])
        so(lambda e: e.activation(out=cols[:, 0:1], in_=aggr[:, 1:2], func=AF.Ln,
                                  bias=epsc[:, 0:1]), [B("aggr"), B("epsc")], [B("c0")])
        so(lambda e: e.activation(out=cols[:, 1:2], in_=cols[:, 0:1], func=AF.Exp,
                                  scale=-0.5), [B("c0")], [B("c1")])
        vo(lambda e: e.scalar_tensor_tensor(cols[:, 2:3], aggr[:, 0:1], -1.0,
                                            cols[:, 1:2], ALU.mult, ALU.mult),
           [B("aggr"), B("c1")], [B("c2")])
        em.dma("sync", hbuf[:, :], x2_sp.ap()[tt, :, :],
               reads=[bx2[tt]], writes=[B("hbuf")])
        vo(lambda e: e.tensor_scalar(lnbf, hbuf[:, :], cols[:, 1:2], cols[:, 2:3],
                                     ALU.mult, ALU.add),
           [B("hbuf"), B("c1"), B("c2")], [blnbf], war=[bxch, bts_r[0], bts_r[1]])
        for k in range(NDT):
            sl = k % 8
            em.mm(lambda e, k=k, sl=sl: e.transpose(
                PST[:, sl * 128:(sl + 1) * 128], lnbf[:, k * 128:(k + 1) * 128],
                ident_bf_s[:, :]),
                [blnbf, B("identb")], [bPSTh(sl)], war=[bPSTh(sl)])
            vo(lambda e, k=k, sl=sl: e.tensor_copy(
                ln2st[:, k * 128:(k + 1) * 128], PST[:, sl * 128:(sl + 1) * 128]),
               [bPSTh(sl)], [bln2st],
               war=([btm, bte, bmt, bcx, bd1, bd2, bpc[0]] if tt == 0 and k == 0
                    else []))
        em.dma("sync", ln2_sp.ap().rearrange("k p t -> p k t")[:, :, tt * 128:(tt + 1) * 128],
               ln2st.rearrange("p (k t) -> p k t", k=NDT),
               reads=[bln2st], writes=[B("ln2_hbm")])
        for k in range(NDT):
            em.mm(lambda e, k=k: e.matmul(
                PSX[:, 0:64], ln2st[:, k * 128:(k + 1) * 128],
                wr_all[:, k * 64:(k + 1) * 64], start=(k == 0), stop=(k == NDT - 1)),
                [bln2st, bwr],
                [bPSX] if k == NDT - 1 else [],
                war=[bPSX] if k == 0 else [], last=(k == NDT - 1))
        vo(lambda e: e.tensor_copy(scores_b[:, :], PSX[:, 0:64]),
           [bPSX], [B("scb")])
        if debug:
            em.dma("sync", dbg["d_scores"].ap()[tt, :, :], scores_b[:, :],
                   reads=[B("scb")])
        vo(lambda e: e.max(out=m8a[:, :], in_=scores_b[:, :]), [B("scb")], [B("m8a")])
        vo(lambda e: e.match_replace(out=mrtmp[:, :], in_to_replace=m8a[:, :],
                                     in_values=scores_b[:, :], imm_value=-1e30),
           [B("scb"), B("m8a")], [B("mr")])
        vo(lambda e: e.max(out=m8b[:, :], in_=mrtmp[:, :]), [B("mr")], [B("m8b")])
        vo(lambda e: e.tensor_scalar_mul(cols[:, 3:4], m8a[:, 0:1], -1.0),
           [B("m8a")], [B("c3")])
        so(lambda e: e.activation(out=gbuf[:, :], in_=scores_b[:, :], func=AF.Exp,
                                  bias=cols[:, 3:4]), [B("scb"), B("c3")], [B("gb")])
        vo(lambda e: e.tensor_scalar(mrtmp[:, :], scores_b[:, :], m8b[:, 7:8], None,
                                     ALU.is_ge), [B("scb"), B("m8b")], [B("mr")])
        vo(lambda e: e.tensor_mul(gbuf[:, :], gbuf[:, :], mrtmp[:, :]),
           [B("gb"), B("mr")], [B("gb")])
        vo(lambda e: e.reduce_sum(cols[:, 4:5], gbuf[:, :], axis=AX),
           [B("gb")], [B("c4")])
        vo(lambda e: e.reciprocal(cols[:, 5:6], cols[:, 4:5]), [B("c4")], [B("c5")])
        vo(lambda e: e.tensor_scalar_mul(gbuf[:, :], gbuf[:, :], cols[:, 5:6]),
           [B("gb"), B("c5")], [B("gb")])
        if debug:
            em.dma("sync", dbg["d_gates"].ap()[tt, :, :], gbuf[:, :],
                   reads=[B("gb")])
        em.mm(lambda e: e.transpose(PSX[0:64, 128:256], gbuf[:, 0:64],
                                    ident_f32_s[:, :]),
              [B("gb"), B("identf")], [bPSX], war=[bPSX])
        vo(lambda e, tt=tt: e.tensor_copy(gt_all[0:64, tt * 128:(tt + 1) * 128],
                                          PSX[0:64, 128:256]),
           [bPSX], [bgt], war=([bxt] if tt == 0 else []))
        for kr in range(8):
            em.mm(lambda e, kr=kr, tt=tt: e.matmul(
                PSX[:, 0:128], E_s[0:64, kr * 128:(kr + 1) * 128],
                gt_all[0:64, tt * 128:(tt + 1) * 128], start=True, stop=True),
                [B("E"), bgt], [bPSX], war=[bPSX])
            vo(lambda e, kr=kr, tt=tt: e.tensor_copy(
                gexp_all[:, kr * NTOK + tt * 128: kr * NTOK + tt * 128 + 128],
                PSX[:, 0:128]),
               [bPSX], [B("gexp")],
               war=([bsbf, bgch] if (tt == 0 and kr == 0) else []))

    hg_v = SCR[:, 0:8192]
    bhg = B("hg")
    bah = [B("ah0"), B("ah1")]
    aring = [wbuf[:, i * 1024:(i + 1) * 1024] for i in range(4)]
    baring = [B(f"ar{i}") for i in range(4)]
    ln2r = [AR3.ap().bitcast(BF16)[:, i * 512:(i + 1) * 512] for i in range(4)]
    bln2r = [B(f"ln2r{i}") for i in range(4)]
    HB = [(PA[:, 0:512], [bPSF[0]]), (PA[:, 512:1024], [bPSF[1]]),
          (PB[:, 0:512], [bPSF[2]]), (PB[:, 512:1024], [bPSF[3]]),
          (PY0[:, 0:512], [bPSF[4]]), (PY1[:, 0:512], [bPSF[5]]),
          (PSX[:, 0:512], [bPSX]), (PSTF[:, 0:512], bPSTfull)]
    for half in range(2):
        for k in range(NDT):
            rs = (half * NDT + k) % 4
            em.dma("sync", ln2r[rs],
                   ln2_sp.ap()[k, :, half * 512:(half + 1) * 512],
                   reads=[B("ln2_hbm")], writes=[bln2r[rs]],
                   war=([bd1, bd2, btm, bte, bmt, bcx, bln2st] if
                        (half == 0 and k < 4) else []))
            em.dma("gpsimd", aring[rs],
                   a_Td.ap()[k * 128:(k + 1) * 128, :],
                   writes=[baring[rs]],
                   war=((bwo + [B("uxw"), B("cwbf"), B("mtb0"), B("mtb1"),
                                B("cxb0"), B("cxb1")]) if (half == 0 and k < 4)
                        else []))
            for kr in range(8):
                pap, bpb = HB[kr]
                em.mm(lambda e, k=k, kr=kr, rs=rs, pap=pap: e.matmul(
                    pap, aring[rs][:, kr * 128:(kr + 1) * 128],
                    ln2r[rs], start=(k == 0), stop=(k == NDT - 1)),
                    [baring[rs], bln2r[rs]],
                    bpb if k == NDT - 1 else [],
                    war=bpb if k == 0 else [],
                    last=(k == NDT - 1), tick=(kr == 7))
        for kr in range(8):
            pap, bpb = HB[kr]
            vo(lambda e, kr=kr, half=half, pap=pap: e.tensor_tensor(
                hg_v[:, kr * NTOK + half * 512: kr * NTOK + (half + 1) * 512],
                pap, gexp_all[:, kr * NTOK + half * 512: kr * NTOK + (half + 1) * 512],
                ALU.mult),
               bpb + [B("gexp")], [bhg],
               war=([B("uxw"), B("cwbf")] if (half == 0 and kr == 0) else []))
    if debug:
        em.dma("gpsimd", dbg["d_hg"].ap(), hg_v, reads=[bhg])
    for tt in range(NTT):
        for kr in range(8):
            for ds in range(8):
                pap, bpb = gpsum(ds)
                em.mm(lambda e, kr=kr, tt=tt, ds=ds, pap=pap: e.matmul(
                    pap, hg_v[:, kr * NTOK + tt * 128: kr * NTOK + tt * 128 + 128],
                    AR1[:, kr * D_ + ds * 512: kr * D_ + (ds + 1) * 512],
                    start=(kr == 0), stop=(kr == 7)),
                    [bhg, bbf],
                    bpb if kr == 7 else [],
                    war=bpb if kr == 0 else [], last=(kr == 7))
        em.dma("sync", hbuf[:, :], x2_sp.ap()[tt, :, :],
               reads=[bx2[tt]], writes=[B("hbuf")])
        for ds in range(8):
            pap, bpb = gpsum(ds)
            vo(lambda e, ds=ds, pap=pap: e.scalar_tensor_tensor(
                ych[:, ds * 512:(ds + 1) * 512], pap, 2.0,
                hbuf[:, ds * 512:(ds + 1) * 512], ALU.mult, ALU.add),
               bpb + [B("hbuf")], [bych])
        em.dma("sync", out_d.ap()[tt * 128:(tt + 1) * 128, :], ych[:, :],
               reads=[bych], writes=[B("outd")])

    em.wait_all("sync", [B("outd")])
    em.ops["sync"].append(
        ([(k, em.cnt[k]) for k in em.cnt if k[:2] in ("qd", "pd") and em.cnt[k]],
         None, None, 0))
    em.finalize()
    return nc, dbg


_CACHE = {}


def _get_nc(debug=False):
    key = bool(debug)
    if key not in _CACHE:
        _CACHE[key] = build(debug)
    return _CACHE[key]


def kernel(**inputs):
    hidf = np.ascontiguousarray(np.asarray(inputs["L0_attn_hidden_states"],
                                           dtype=np.float32))
    w_in = np.asarray(inputs["L0_attn_in_proj_w"], dtype=np.float32)
    convw = np.asarray(inputs["L0_attn_conv_weight"], dtype=np.float32)
    convb = np.asarray(inputs["L0_attn_conv_bias"], dtype=np.float32)
    dtb = np.asarray(inputs["L0_attn_dt_bias"], dtype=np.float32)
    alog = np.asarray(inputs["L0_attn_A_log"], dtype=np.float32)
    Dv = np.asarray(inputs["L0_attn_D"], dtype=np.float32)
    normw = np.asarray(inputs["L0_attn_norm_weight"], dtype=np.float32)
    w_out = np.asarray(inputs["L0_attn_out_proj_w"], dtype=np.float32)
    A = np.asarray(inputs["L0_ffn_A"], dtype=np.float32)
    Bf = np.asarray(inputs["L0_ffn_B"], dtype=np.float32)
    Wr = np.asarray(inputs["L0_ffn_Wr"], dtype=np.float32)

    shared = {
        "w_inT": np.ascontiguousarray(w_in.T),
        "w_outT": np.ascontiguousarray(w_out.T),
        "a_T": np.ascontiguousarray(A.reshape(1024, D_).T),
        "b_flat": np.ascontiguousarray(Bf.transpose(0, 2, 1).reshape(1024, D_)),
        "wr_T": np.ascontiguousarray(Wr.T),
        "convw": np.ascontiguousarray(
            convw.reshape(NXBC, 128, 4).transpose(1, 0, 2).reshape(128, NXBC * 4)),
        "convb": np.ascontiguousarray(convb.reshape(NXBC, 128).T),
        "dtb": dtb.reshape(64, 1).copy(),
        "alog": alog.reshape(64, 1).copy(),
        "dvec": np.ascontiguousarray(np.repeat(Dv, P_).reshape(32, 128).T),
        "normw": np.ascontiguousarray(normw.reshape(32, 128).T),
    }
    nc, _ = _get_nc(False)
    in_maps = []
    for c in range(8):
        b, th = c // 2, c % 2
        m = dict(shared)
        m["hid"] = np.ascontiguousarray(hidf[b, th * NTOK:(th + 1) * NTOK, :])
        m["flagc"] = np.full((128, 1), float(th), dtype=np.float32)
        in_maps.append(m)
    r = run_bass_kernel_spmd(nc, in_maps, core_ids=list(range(8)))
    out = np.empty((B_, L_, D_), dtype=np.float32)
    for c in range(8):
        b, th = c // 2, c % 2
        out[b, th * NTOK:(th + 1) * NTOK, :] = np.asarray(r.results[c]["out"])
    return out
